# version 17
# baseline (speedup 1.0000x reference)
import numpy as np
import ml_dtypes
from contextlib import ExitStack

import concourse.bass as bass
import concourse.mybir as mybir
from concourse.bass_utils import run_bass_kernel_spmd

F32 = mybir.dt.float32
BF16 = mybir.dt.bfloat16
AF = mybir.ActivationFunctionType
ALU = mybir.AluOpType
AX = mybir.AxisListType
BF = ml_dtypes.bfloat16

_DBG = False

S = 8192
D = 1024
TKV = 4096
NGRP = 16
NOG = 8
E = 8
NTT = 8
NQT = 16
NGR = 16


class Ctr:
    def __init__(self, stack, nc, name):
        self.h = stack.enter_context(nc.semaphore(name))
        self.n = 0
    def bump(self, inst, k=1):
        inst.then_inc(self.h, k)
        self.n += k
        return self.n


def build_program():
    nc = bass.Bass("TRN2", target_bir_lowering=False)
    DT = nc.dram_tensor
    sk = "ExternalOutput" if _DBG else "Internal"

    d_x = DT("x", [S, D], F32, kind="ExternalInput").ap()
    d_xo = DT("xo", [4096, D], F32, kind="ExternalInput").ap()
    d_keysT = DT("keysT", [D, TKV], F32, kind="ExternalInput").ap()
    d_valuesT = DT("valuesT", [D, TKV], F32, kind="ExternalInput").ap()
    d_wqT = DT("wqT", [128, 8, 4, 128], F32, kind="ExternalInput").ap()
    d_wkT = DT("wkT", [128, 8, 4, 128], F32, kind="ExternalInput").ap()
    d_wvT = DT("wvT", [128, 8, 4, 128], F32, kind="ExternalInput").ap()
    d_bq = DT("bq", [128, 4], F32, kind="ExternalInput").ap()
    d_bk = DT("bk", [128, 4], F32, kind="ExternalInput").ap()
    d_bvr = DT("bvr", [1, 512], F32, kind="ExternalInput").ap()
    d_wrT = DT("wrT", [128, 8, 8], F32, kind="ExternalInput").ap()
    d_br = DT("br", [1, 8], F32, kind="ExternalInput").ap()
    d_w1t = DT("w1t", [E, 8, 128, 4096], F32, kind="ExternalInput").ap()
    d_w2t = DT("w2t", [E, 8, 128, 32, 128], F32, kind="ExternalInput").ap()
    d_b1 = DT("b1", [128, E, 32], F32, kind="ExternalInput").ap()
    d_b2 = DT("b2", [8, D], F32, kind="ExternalInput").ap()
    d_gam = DT("gam", [128, D], F32, kind="ExternalInput").ap()
    d_bet = DT("bet", [128, D], F32, kind="ExternalInput").ap()
    d_idf = DT("idf", [128, 128], F32, kind="ExternalInput").ap()
    d_ones1f = DT("ones1f", [1, 128], F32, kind="ExternalInput").ap()
    d_ones1b = DT("ones1b", [1, 128], BF16, kind="ExternalInput").ap()
    d_onehot = DT("onehot", [8, 8 * 128], BF16, kind="ExternalInput").ap()
    d_rep8 = DT("rep8", [8, 128], BF16, kind="ExternalInput").ap()

    d_y = DT("y", [4096, D], F32, kind="ExternalOutput").ap()
    d_t2 = DT("t2", [4096, D], F32, kind="Internal").ap()
    d_qts = DT("qts", [4, 128, S], BF16, kind=sk).ap()
    d_xnts = DT("xnts", [128, 8, 4096], BF16, kind=sk).ap()
    d_ss = DT("ss", [4096, D], BF16, kind=sk).ap()
    d_w1b = DT("w1b", [E, 8, 128, 4096], BF16, kind="Internal").ap()
    d_w2b = DT("w2b", [E, 8, 128, 32, 128], BF16, kind="Internal").ap()
    d_bnc = DT("bnc", [128, S], BF16, kind="Internal").ap()
    d_bncr2 = DT("bncr2", [2, S], BF16, kind="Internal").ap()

    with ExitStack() as st0:
        def sb(name, shape, dt):
            return st0.enter_context(nc.sbuf_tensor(name, shape, dt))
        def sem(name):
            return Ctr(st0, nc, name)

        cIdF = sb("cIdF", [128, 128], F32)
        cOn1f = sb("cOn1f", [1, 128], F32)
        cOn1b = sb("cOn1b", [1, 128], BF16)
        cOneHot = sb("cOneHot", [8, 8 * 128], BF16)
        cRep8 = sb("cRep8", [8, 128], BF16)
        cWrT = sb("cWrT", [128, 8 * 8], F32)
        cBr = sb("cBr", [1, 8], F32)
        cBq = sb("cBq", [128, 4], F32)
        cBk = sb("cBk", [128, 4], F32)
        cBvR = sb("cBvR", [1, 512], BF16)
        cB1 = sb("cB1", [128, E * 32], F32)
        cB2 = sb("cB2", [8, D], BF16)
        stRS = sb("stRS", [128, 32], F32)
        stBL = sb("stBL", [128, 32], F32)
        stLG = sb("stLG", [128, 32 * 8], F32)
        cDmy = sb("cDmy", [1, 4], F32)
        st_ab = ExitStack()
        KT = st_ab.enter_context(nc.sbuf_tensor("KT", [128, 4 * TKV], BF16))
        vAll = st_ab.enter_context(nc.sbuf_tensor("vAll", [128, 8 * 32 * 65], BF16))
        cWt = st_ab.enter_context(nc.sbuf_tensor("cWt", [8, 8 * 512], BF16))
        st_w = ExitStack()
        cWq = st_w.enter_context(nc.sbuf_tensor("cWq", [128, 8 * 4 * 128], BF16))
        cWk = st_w.enter_context(nc.sbuf_tensor("cWk", [128, 8 * 4 * 128], BF16))
        cWv = st_w.enter_context(nc.sbuf_tensor("cWv", [128, 8 * 4 * 128], BF16))

        c_const = sem("c_const")
        c_constg = sem("c_constg")
        c_kv = sem("c_kv")
        c_x = [sem("c_x0"), sem("c_x1")]
        c_sp = sem("c_sp")
        c_sq = [sem("c_sq0"), sem("c_sq1")]
        c_qT = [sem("c_qT0"), sem("c_qT1"), sem("c_qT2"), sem("c_qT3")]
        c_xr = sem("c_xr"); c_d8 = sem("c_d8"); c_scr = sem("c_scr")
        c_w1 = sem("c_w1")
        c_w2 = sem("c_w2")
        c_w1l = [sem("c_w1l0"), sem("c_w1l1"), sem("c_w1l2")]
        c_w2l = [sem("c_w2l0"), sem("c_w2l1"), sem("c_w2l2")]
        c_cxn = [sem("c_cxn0"), sem("c_cxn1")]
        c_cs = [sem("c_cs0"), sem("c_cs1")]
        c_cxx = [sem("c_cxx0"), sem("c_cxx1")]
        c_out = [sem("c_out0"), sem("c_out1")]
        k_mm = sem("k_mm"); k_ev = sem("k_ev")
        a_mu = sem("a_mu"); a_sq = sem("a_sq"); a_var = sem("a_var")
        a_rs = sem("a_rs"); a_bl = sem("a_bl"); a_xn = sem("a_xn")
        a_tr = sem("a_tr"); a_ev = sem("a_ev"); a_q = sem("a_q")
        a_qe = sem("a_qe"); a_lg = sem("a_lg"); a_lgs = sem("a_lgs")
        a_bar = sem("a_bar")
        c_bn = sem("c_bn")
        b_sc = sem("b_sc"); b_ex = sem("b_ex"); b_av = sem("b_av")
        b_ae = sem("b_ae"); b_ve = sem("b_ve"); b_dn = sem("b_dn")
        b_rc = sem("b_rc"); b_xn = sem("b_xn"); b_sg = sem("b_sg")
        f_h = sem("f_h"); f_hr = sem("f_hr"); f_hw = sem("f_hw")
        f_o = sem("f_o"); f_oe = sem("f_oe")
        t_wb = sem("t_wb"); f_wb = sem("f_wb")
        r_w = sem("r_w"); r_t = sem("r_t"); r_wt = sem("r_wt")
        s_rex = sem("s_rex"); s_xn2 = sem("s_xn2")
        n_nm = sem("n_nm"); n_var = sem("n_var")
        s_sq = sem("s_sq"); s_rs = sem("s_rs"); s_A = sem("s_A"); s_tanh = sem("s_tanh")
        vf = sem("vf"); sf = sem("sf")
        f_t2 = sem("f_t2"); f_t2a = sem("f_t2a"); f_bl = sem("f_bl")
        f_vf = sem("f_vf")
        f_y = sem("f_y"); f_fin = sem("f_fin")
        c_vo = sem("c_vo"); t_bc = sem("t_bc"); f_t2c = sem("f_t2c")
        c_t2s = sem("c_t2s"); c_t2l = [sem("c_t2l0"), sem("c_t2l1")]
        c_bnA = [sem("c_bnA0"), sem("c_bnA1")]
        c_bnD = [sem("c_bnD0"), sem("c_bnD1")]
        m_vd = sem("m_vd"); m_sd = sem("m_sd")

        blk = st0.enter_context(nc.Block())

        @blk.sync
        def _(sync):
            for dst, src in [(cIdF, d_idf), (cOn1f, d_ones1f), (cOn1b, d_ones1b),
                             (cOneHot, d_onehot), (cRep8, d_rep8), (cBr, d_br),
                             (cBq, d_bq), (cBk, d_bk)]:
                c_const.bump(sync.dma_start(out=dst[:], in_=src), 16)
            c_const.bump(sync.dma_start(
                out=cWrT[:].rearrange("p (a b) -> p a b", a=8), in_=d_wrT), 16)
            c_const.bump(sync.dma_start(
                out=cB1[:].rearrange("p (a b) -> p a b", a=E), in_=d_b1), 16)

        @blk.gpsimd
        def _(g):
            for dst, src in [(cWq, d_wqT), (cWk, d_wkT), (cWv, d_wvT)]:
                for half in range(2):
                    c_constg.bump(g.dma_start(
                        out=dst[:, half * 2048:(half + 1) * 2048]
                            .rearrange("p (a b c) -> p a b c", a=4, b=4),
                        in_=src[:, half * 4:(half + 1) * 4]), 16)
            c_constg.bump(g.dma_start(out=cBvR[:], in_=d_bvr), 16)
            c_constg.bump(g.dma_start(out=cB2[:], in_=d_b2), 16)
        N_CONST = 10

        @blk.vector
        def _(v):
            c_vo.bump(v.memset(
                vAll[:].rearrange("p (h t e) -> p h t e", h=8, t=32)[:, :, :, 64:65], 1.0), 1)

        with ExitStack() as stkv:
            kvK = [stkv.enter_context(nc.sbuf_tensor(f"kvK{i}", [128, 8 * 512], BF16)) for i in range(2)]
            kvV = [stkv.enter_context(nc.sbuf_tensor(f"kvV{i}", [128, 8 * 512], BF16)) for i in range(2)]
            pKV = [stkv.enter_context(nc.psum_tensor(f"pKV{i}", [128, 512], F32)) for i in range(4)]

            @blk.gpsimd
            def _(g):
                for tg in range(8):
                    if tg >= 2:
                        g.wait_ge(k_mm.h, 8 * (tg - 1))
                    g.wait_ge(c_kv.h, c_kv.n)
                    for half in range(2):
                        g.dma_start(
                            out=kvK[tg % 2][:, half * 2048:(half + 1) * 2048]
                                .rearrange("p (a b) -> p a b", a=4),
                            in_=d_keysT[half * 512:(half + 1) * 512, tg * 512:(tg + 1) * 512]
                                .rearrange("(a p) t -> p a t", p=128)
                        ).then_inc(c_kv.h, 16); c_kv.n += 16
                    g.wait_ge(c_kv.h, c_kv.n)
                    for half in range(2):
                        g.dma_start(
                            out=kvV[tg % 2][:, half * 2048:(half + 1) * 2048]
                                .rearrange("p (a b) -> p a b", a=4),
                            in_=d_valuesT[half * 512:(half + 1) * 512, tg * 512:(tg + 1) * 512]
                                .rearrange("(a p) t -> p a t", p=128)
                        ).then_inc(c_kv.h, 16); c_kv.n += 16

            @blk.tensor
            def _(t):
                t.wait_ge(c_const.h, 16 * N_CONST)
                t.wait_ge(c_constg.h, 16 * 8)
                for tg in range(8):
                    t.wait_ge(c_kv.h, 64 * (tg + 1))
                    kb, vb = kvK[tg % 2], kvV[tg % 2]
                    for dkc in range(4):
                        if dkc >= 2:
                            t.wait_ge(k_ev.h, 8 * tg + dkc - 1)
                        elif tg >= 1:
                            t.wait_ge(k_ev.h, 8 * tg + dkc - 5)
                        for dc in range(8):
                            inst = t.matmul(pKV[dkc % 2][:],
                                            cWk[:, (dc * 4 + dkc) * 128:(dc * 4 + dkc + 1) * 128],
                                            kb[:, dc * 512:(dc + 1) * 512],
                                            start=(dc == 0), stop=(dc == 7))
                        k_mm.bump(inst, 1)
                    for tcl in range(4):
                        if tcl >= 2:
                            t.wait_ge(k_ev.h, 8 * tg + tcl + 3)
                        elif tg >= 1:
                            t.wait_ge(k_ev.h, 8 * tg + tcl - 1)
                        for dc in range(8):
                            t.matmul(pKV[2 + tcl % 2][:],
                                     vb[:, dc * 512 + tcl * 128: dc * 512 + (tcl + 1) * 128],
                                     cWv[:, dc * 512:(dc + 1) * 512],
                                     start=(dc == 0), stop=False)
                        inst = t.matmul(pKV[2 + tcl % 2][:], cOn1b[:], cBvR[:],
                                        start=False, stop=True)
                        k_mm.bump(inst, 1)

            @blk.scalar
            def _(s):
                for tg in range(8):
                    for dkc in range(4):
                        s.wait_ge(k_mm.h, 8 * tg + dkc + 1)
                        inst = s.activation(KT[:, dkc * TKV + tg * 512: dkc * TKV + (tg + 1) * 512],
                                            pKV[dkc % 2][:], AF.Relu, bias=cBk[:, dkc:dkc + 1])
                        k_ev.bump(inst, 1)
                    for tcl in range(4):
                        s.wait_ge(k_mm.h, 8 * tg + 4 + tcl + 1)
                        tc = tg * 4 + tcl
                        inst = s.activation(
                            vAll[:].rearrange("p (h m) -> p h m", h=8)[:, :, tc * 65:tc * 65 + 64],
                            pKV[2 + tcl % 2][:].rearrange("p (h d) -> p h d", h=8),
                            AF.Relu)
                        k_ev.bump(inst, 1)

        assert k_mm.n == 64 and k_ev.n == 64


        def phaseA(tag, src_ap, ngrp, own):
            base_bar = a_bar.n
            base_mu, base_sq, base_var = a_mu.n, a_sq.n, a_var.n
            base_rs, base_bl, base_xn = a_rs.n, a_bl.n, a_xn.n
            base_tr, base_ev, base_q = a_tr.n, a_ev.n, a_q.n
            base_qe, base_lg = a_qe.n, a_lg.n
            base_lgs = a_lgs.n
            base_cx = [c_x[0].n, c_x[1].n]
            base_sp = c_sp.n
            base_cq = [c_sq[0].n, c_sq[1].n]
            BAR = 2
            tprog = (a_lg if own else a_q)
            base_tp = tprog.n
            EVB = 8 if own else 4
            EVG = 8 * EVB

            with ExitStack() as sta:
                aX = [sta.enter_context(nc.sbuf_tensor(f"aX{tag}{i}", [128, 4 * 1024], F32)) for i in range(2)]
                aSq = sta.enter_context(nc.sbuf_tensor(f"aSq{tag}", [128, 4 * 1024], F32))
                aXn = sta.enter_context(nc.sbuf_tensor(f"aXn{tag}", [128, 4 * 1024], F32))
                aSm = sta.enter_context(nc.sbuf_tensor(f"aSm{tag}", [128, 4], F32))
                aNM = sta.enter_context(nc.sbuf_tensor(f"aNM{tag}", [128, 4], F32))
                aVar = sta.enter_context(nc.sbuf_tensor(f"aVar{tag}", [128, 4], F32))
                aRS = sta.enter_context(nc.sbuf_tensor(f"aRS{tag}", [128, 4], F32))
                aBL = sta.enter_context(nc.sbuf_tensor(f"aBL{tag}", [128, 4], F32))
                aXnTs = [sta.enter_context(nc.sbuf_tensor(f"aXnT{tag}{i}", [128, 8 * 512], BF16))
                         for i in range(2 if own else 1)]
                aXnT32 = sta.enter_context(nc.sbuf_tensor(f"aXnT32{tag}", [128, 8 * 512], F32)) if own else None
                aQe = [sta.enter_context(nc.sbuf_tensor(f"aQe{tag}{i}", [128, 4 * 512], BF16))
                       for i in range(2)] if not own else None
                pTr = [sta.enter_context(nc.psum_tensor(f"pTr{tag}{i}", [128, 512], F32)) for i in range(2)]
                pQL = [sta.enter_context(nc.psum_tensor(f"pQL{tag}{i}", [128, 512], F32)) for i in range(4)]

                @blk.sync
                def _(sync):
                    for g in range(ngrp):
                        if g == 0:
                            sync.wait_ge(k_ev.h, 64)
                            sync.wait_ge(a_bar.h, base_bar)
                            sync.wait_ge(a_q.h, base_q)
                            sync.wait_ge(c_sq[0].h, base_cq[0])
                            sync.wait_ge(c_sq[1].h, base_cq[1])
                        if g >= 2:
                            sync.wait_ge(a_bar.h, base_bar + BAR * (g - 1))
                            sync.wait_ge(tprog.h, base_tp + 4 * (g - 1))
                        for i in range(4):
                            c_x[g % 2].bump(sync.dma_start(
                                out=aX[g % 2][:, i * 1024:(i + 1) * 1024],
                                in_=src_ap[g * 512 + i * 128: g * 512 + (i + 1) * 128, :]
                            ), 16)

                @blk.vector
                def _(v):
                    for g in range(ngrp):
                        if g >= 1:
                            v.wait_ge(a_bar.h, base_bar + BAR * g)
                            v.wait_ge(tprog.h, base_tp + 4 * g)
                        xb = aX[g % 2]
                        v.wait_ge(c_x[g % 2].h, base_cx[g % 2] + 64 * (g // 2 + 1))
                        for i in range(4):
                            vf.bump(v.reduce_sum(aSm[:, i:i + 1], xb[:, i * 1024:(i + 1) * 1024], axis=AX.X), 1)
                        v.wait_ge(vf.h, vf.n)
                        for i in range(4):
                            inst = v.tensor_scalar(aNM[:, i:i + 1], aSm[:, i:i + 1],
                                                   -1.0 / 1024, None, ALU.mult)
                            a_mu.bump(inst, 1)
                        for i in range(4):
                            v.wait_ge(a_sq.h, base_sq + 4 * g + i + 1)
                            vf.bump(v.reduce_sum(aVar[:, i:i + 1], aSq[:, i * 1024:(i + 1) * 1024], axis=AX.X), 1)
                        v.wait_ge(vf.h, vf.n)
                        for i in range(4):
                            inst = v.tensor_scalar(aVar[:, i:i + 1], aVar[:, i:i + 1],
                                                   1.0 / 1024, 1e-5, ALU.mult, ALU.add)
                            a_var.bump(inst, 1)
                        v.wait_ge(a_mu.h, base_mu + 4 * g + 4)
                        for i in range(4):
                            v.wait_ge(a_rs.h, base_rs + 4 * g + i + 1)
                            inst = v.tensor_tensor(aBL[:, i:i + 1], aNM[:, i:i + 1],
                                                   aRS[:, i:i + 1], ALU.mult)
                            a_bl.bump(inst, 1)
                        if own:
                            v.wait_ge(a_bl.h, base_bl + 4 * g + 4)
                            vf.bump(v.tensor_copy(stRS[:, 4 * g:4 * g + 4], aRS[:]), 1)
                            vf.bump(v.tensor_copy(stBL[:, 4 * g:4 * g + 4], aBL[:]), 1)
                        if own and g >= 2:
                            v.wait_ge(c_sq[g % 2].h, base_cq[g % 2] + 64 * (g // 2))
                        xnt = aXnTs[g % 2] if own else aXnTs[0]
                        for i in range(4):
                            for bank in range(2):
                                v.wait_ge(a_tr.h, base_tr + 8 * g + 2 * i + bank + 1)
                                p = pTr[bank]
                                for dcq in range(4):
                                    dc = bank * 4 + dcq
                                    a_ev.bump(v.tensor_copy(
                                        xnt[:, dc * 512 + i * 128: dc * 512 + (i + 1) * 128],
                                        p[:, dcq * 128:(dcq + 1) * 128]), 1)
                                    if own:
                                        a_ev.bump(v.tensor_copy(
                                            aXnT32[:, dc * 512 + i * 128: dc * 512 + (i + 1) * 128],
                                            p[:, dcq * 128:(dcq + 1) * 128]), 1)
                        if own:
                            for i in range(4):
                                v.wait_ge(a_lg.h, base_lg + 4 * g + i + 1)
                                inst = v.tensor_copy(stLG[:, (4 * g + i) * 8:(4 * g + i + 1) * 8],
                                                     pQL[i][:, 0:8])
                                a_lgs.bump(inst, 1)
                        v.wait_ge(a_ev.h, base_ev + EVG * (g + 1))
                        if own:
                            v.wait_ge(a_lgs.h, base_lgs + 4 * (g + 1))
                        v.wait_ge(vf.h, vf.n)
                        inst = v.engine_nop()
                        a_bar.bump(inst, 1)

                @blk.scalar
                def _(s):
                    for g in range(ngrp):
                        if g >= 1:
                            s.wait_ge(a_bar.h, base_bar + BAR * g)
                            s.wait_ge(tprog.h, base_tp + 4 * g)
                        xb = aX[g % 2]
                        for i in range(4):
                            s.wait_ge(a_mu.h, base_mu + 4 * g + i + 1)
                            inst = s.activation(aSq[:, i * 1024:(i + 1) * 1024],
                                                xb[:, i * 1024:(i + 1) * 1024],
                                                AF.Square, bias=aNM[:, i:i + 1])
                            a_sq.bump(inst, 1)
                        for i in range(4):
                            s.wait_ge(a_var.h, base_var + 4 * g + i + 1)
                            sf.bump(s.activation(aRS[:, i:i + 1], aVar[:, i:i + 1], AF.Ln), 1)
                        s.wait_ge(sf.h, sf.n)
                        for i in range(4):
                            inst = s.activation(aRS[:, i:i + 1], aRS[:, i:i + 1], AF.Exp, scale=-0.5)
                            a_rs.bump(inst, 1)
                        for i in range(4):
                            s.wait_ge(a_bl.h, base_bl + 4 * g + i + 1)
                            inst = s.activation(aXn[:, i * 1024:(i + 1) * 1024],
                                                xb[:, i * 1024:(i + 1) * 1024],
                                                AF.Identity, bias=aBL[:, i:i + 1], scale=aRS[:, i:i + 1])
                            a_xn.bump(inst, 1)
                        if not own:
                            if g >= 2:
                                s.wait_ge(c_sq[g % 2].h, base_cq[g % 2] + 64 * (g // 2))
                            for dqc in range(4):
                                s.wait_ge(a_q.h, base_q + 4 * g + dqc + 1)
                                inst = s.activation(aQe[g % 2][:, dqc * 512:(dqc + 1) * 512], pQL[dqc][:],
                                                    AF.Relu, bias=cBq[:, dqc:dqc + 1])
                                a_qe.bump(inst, 1)
                            s.wait_ge(a_qe.h, base_qe + 4 * (g + 1))
                            for ai in range(4):
                                s.dma_start(
                                    out=d_qts[ai, :, g * 512:(g + 1) * 512],
                                    in_=aQe[g % 2][:, ai * 512:(ai + 1) * 512]
                                ).then_inc(c_sq[g % 2].h, 16); c_sq[g % 2].n += 16
                        else:
                            s.wait_ge(a_ev.h, base_ev + EVG * (g + 1))
                            for j in range(4):
                                s.dma_start(
                                    out=d_xnts[:, 2 * j:2 * j + 2, g * 512:(g + 1) * 512],
                                    in_=aXnTs[g % 2][:, (2 * j) * 512:(2 * j + 2) * 512]
                                        .rearrange("p (a t) -> p a t", a=2)
                                ).then_inc(c_sq[g % 2].h, 16); c_sq[g % 2].n += 16
                        a_bar.bump(s.memzero(cDmy[:]), 1)

                @blk.tensor
                def _(t):
                    for g in range(ngrp):
                        if g == 0:
                            t.wait_ge(k_ev.h, 64)
                            t.wait_ge(a_bar.h, base_bar)
                            t.wait_ge(a_q.h, base_q)
                        if g >= 1:
                            t.wait_ge(a_bar.h, base_bar + BAR * g)
                        for i in range(4):
                            t.wait_ge(a_xn.h, base_xn + 4 * g + i + 1)
                            for bank in range(2):
                                prev = 2 * i + bank - 2
                                if prev >= 0:
                                    t.wait_ge(a_ev.h, base_ev + EVG * g + EVB * (prev + 1))
                                p = pTr[bank]
                                for dcq in range(4):
                                    dc = bank * 4 + dcq
                                    inst = t.transpose(p[:, dcq * 128:(dcq + 1) * 128],
                                                       aXn[:, i * 1024 + dc * 128: i * 1024 + (dc + 1) * 128],
                                                       cIdF[:])
                                a_tr.bump(inst, 1)
                        t.wait_ge(a_ev.h, base_ev + EVG * (g + 1))
                        if not own:
                            for dqc in range(4):
                                for dc in range(8):
                                    inst = t.matmul(pQL[dqc][:],
                                                    cWq[:, (dc * 4 + dqc) * 128:(dc * 4 + dqc + 1) * 128],
                                                    aXnTs[0][:, dc * 512:(dc + 1) * 512],
                                                    start=(dc == 0), stop=(dc == 7))
                                a_q.bump(inst, 1)
                        else:
                            for i in range(4):
                                for dc in range(8):
                                    t.matmul(pQL[i][:, 0:8],
                                             aXnT32[:, dc * 512 + i * 128: dc * 512 + (i + 1) * 128],
                                             cWrT[:, dc * 8:(dc + 1) * 8],
                                             start=(dc == 0), stop=False)
                                inst = t.matmul(pQL[i][:, 0:8], cOn1f[:], cBr[:],
                                                start=False, stop=True)
                                a_lg.bump(inst, 1)

        phaseA("1", d_x, NGRP, own=False)
        phaseA("2", d_xo, NOG, own=True)
        st_w.close()
        A_SP = c_sp.n
        A_SQ = (c_sq[0].n, c_sq[1].n)
        assert A_SP == 0 and A_SQ == (64 * (NGRP + NOG) // 2,) * 2
        assert a_lgs.n == 32 and a_bar.n == 2 * (NGRP + NOG)
        A_BAR = a_bar.n


        with ExitStack() as str_:
            rMx = str_.enter_context(nc.sbuf_tensor("rMx", [128, 32 * 8], F32))
            rNm = str_.enter_context(nc.sbuf_tensor("rNm", [128, 32], F32))
            rE2 = str_.enter_context(nc.sbuf_tensor("rE2", [128, 32], F32))
            rRc = str_.enter_context(nc.sbuf_tensor("rRc", [128, 32], F32))
            rEx = str_.enter_context(nc.sbuf_tensor("rEx", [128, 32 * 8], F32))
            rMk = str_.enter_context(nc.sbuf_tensor("rMk", [128, 32 * 8], F32))
            rW = str_.enter_context(nc.sbuf_tensor("rW", [128, 32 * 8], F32))
            pRt = str_.enter_context(nc.psum_tensor("pRt", [128, 512], F32))

            @blk.scalar
            def _(s):
                for i2 in range(32):
                    s.wait_ge(r_w.h, 32 + i2 + 1)
                    s_rex.bump(s.activation(rE2[:, i2:i2 + 1], rMx[:, i2 * 8 + 1:i2 * 8 + 2],
                                            AF.Exp, bias=rNm[:, i2:i2 + 1]), 1)
                    s_rex.bump(s.activation(rEx[:, i2 * 8:(i2 + 1) * 8],
                                            stLG[:, i2 * 8:(i2 + 1) * 8],
                                            AF.Exp, bias=rNm[:, i2:i2 + 1]), 1)

            @blk.vector
            def _(v):
                v.wait_ge(a_lgs.h, 32)
                v.wait_ge(c_sq[0].h, A_SQ[0])
                v.wait_ge(c_sq[1].h, A_SQ[1])
                for i2 in range(32):
                    r_w.bump(v.max(out=rMx[:, i2 * 8:(i2 + 1) * 8],
                                   in_=stLG[:, i2 * 8:(i2 + 1) * 8]), 1)
                v.wait_ge(r_w.h, 32)
                for i2 in range(32):
                    r_w.bump(v.tensor_scalar(rNm[:, i2:i2 + 1], rMx[:, i2 * 8:i2 * 8 + 1],
                                             -1.0, None, ALU.mult), 1)
                for i2 in range(32):
                    v.wait_ge(s_rex.h, 2 * i2 + 1)
                    vf.bump(v.tensor_scalar(rE2[:, i2:i2 + 1], rE2[:, i2:i2 + 1],
                                            1.0, 1.0, ALU.mult, ALU.add), 1)
                v.wait_ge(vf.h, vf.n)
                for i2 in range(32):
                    vf.bump(v.reciprocal(rRc[:, i2:i2 + 1], rE2[:, i2:i2 + 1]), 1)
                for i2 in range(32):
                    vf.bump(v.tensor_scalar(rMk[:, i2 * 8:(i2 + 1) * 8],
                                            stLG[:, i2 * 8:(i2 + 1) * 8],
                                            rMx[:, i2 * 8 + 1:i2 * 8 + 2], None, ALU.is_ge), 1)
                v.wait_ge(s_rex.h, 64)
                v.wait_ge(vf.h, vf.n)
                for i2 in range(32):
                    vf.bump(v.tensor_scalar(rEx[:, i2 * 8:(i2 + 1) * 8], rEx[:, i2 * 8:(i2 + 1) * 8],
                                            rRc[:, i2:i2 + 1], None, ALU.mult), 1)
                v.wait_ge(vf.h, vf.n)
                for i2 in range(32):
                    r_w.bump(v.tensor_tensor(rW[:, i2 * 8:(i2 + 1) * 8], rEx[:, i2 * 8:(i2 + 1) * 8],
                                             rMk[:, i2 * 8:(i2 + 1) * 8], ALU.mult), 1)
                for tt in range(NTT):
                    v.wait_ge(r_t.h, 4 * (tt + 1))
                    r_wt.bump(v.tensor_copy(cWt[:, tt * 512:(tt + 1) * 512], pRt[0:8, :]), 1)

            @blk.tensor
            def _(t):
                for tt in range(NTT):
                    if tt >= 1:
                        t.wait_ge(r_wt.h, tt)
                    for i in range(4):
                        i2 = tt * 4 + i
                        t.wait_ge(r_w.h, 64 + i2 + 1)
                        inst = t.transpose(pRt[0:8, i * 128:(i + 1) * 128],
                                           rW[:, i2 * 8:(i2 + 1) * 8], cIdF[:])
                        r_t.bump(inst, 1)
        assert r_w.n == 96 and r_t.n == 32 and r_wt.n == 8 and s_rex.n == 64

        chains = []
        for tt in range(NTT):
            chains.append(("cxn", tt))
            if tt >= 1:
                for i in range(4):
                    for half in range(2):
                        chains.append(("tr", (tt - 1) * 4 + i, half, 4))
            for e in range(E):
                ge = tt * 8 + e
                chains.append(("bc", ge, 0))
                for fc in range(32):
                    chains.append(("hc", ge, fc, 8))
                for dc in range(8):
                    chains.append(("oc", ge, dc, 32))
        for i in range(4):
            for half in range(2):
                chains.append(("tr", 28 + i, half, 4))
        chains_tt = []
        cur_tt = 0
        for ch in chains:
            if ch[0] == "cxn":
                cur_tt = ch[1]
            chains_tt.append((cur_tt, ch))
        PREFIX = 1536
        cmm_tt = [0] * NTT
        for tt, ch in chains_tt:
            if ch[0] in ("tr", "hc", "oc"):
                cmm_tt[tt] += ch[-1]
        w_tt = [1.9, 1.35, 1.1, 0.95, 0.85, 0.8, 0.75, 0.7]
        rem = 4096 - PREFIX
        denom = sum(w_tt[t] * cmm_tt[t] for t in range(NTT))
        rate_tt = [rem * w_tt[t] / denom for t in range(NTT)]
        EV = []
        pend = []
        bu = 0

        def push(ev):
            EV.append(ev)
            for p in pend:
                p[0] -= 1
            ready = [p for p in pend if p[0] <= 0]
            pend[:] = [p for p in pend if p[0] > 0]
            for p in ready:
                EV.append(p[1])

        def emit_unit(k, u):
            if u == 0 and k + 2 < 128:
                push(("bq", k + 2))
            push(("bu", k, u))
            if u == 31:
                push(("bev", k))
                pend.append([2, ("bnc", k)])
                if k % 16 == 15:
                    h = k // 16
                    pend.extend([[3, ("d8", h)], [5, ("pdb", h, 0)], [6, ("rcp", h, 0)],
                                 [7, ("scr", h)], [8, ("nrm", h)], [9, ("pdb", h, 1)],
                                 [10, ("rcp", h, 1)], [13, ("sadd", h)], [23, ("sst", h)]])

        push(("bq", 0))
        push(("bq", 1))
        while bu < PREFIX:
            emit_unit(bu // 32, bu % 32)
            bu += 1
        pend.sort(key=lambda p: p[0])
        for p in pend:
            EV.append(p[1])
        pend.clear()
        acc = 0.0
        for tt, ch in chains_tt:
            push(ch)
            if ch[0] in ("tr", "hc", "oc"):
                acc += ch[-1] * rate_tt[tt]
            while acc >= 1.0 and bu < 4096:
                acc -= 1.0
                emit_unit(bu // 32, bu % 32)
                bu += 1
        while bu < 4096:
            emit_unit(bu // 32, bu % 32)
            bu += 1
        pend.sort(key=lambda p: p[0])
        for p in pend:
            EV.append(p[1])

        with ExitStack() as stm:
            bQt = stm.enter_context(nc.sbuf_tensor("bQt", [128, 4 * 512], BF16))
            bEx = stm.enter_context(nc.sbuf_tensor("bEx", [128, 4 * 512], BF16))
            bXr = stm.enter_context(nc.sbuf_tensor("bXr", [128, 4 * 1024], BF16))
            bScr = stm.enter_context(nc.sbuf_tensor("bScr", [128, 4 * 1024], BF16))
            bDiv = stm.enter_context(nc.sbuf_tensor("bDiv", [128, 1024], BF16))
            bAssy = stm.enter_context(nc.sbuf_tensor("bAssy", [65, 2 * 512], BF16))
            bD8 = stm.enter_context(nc.sbuf_tensor("bD8", [8, 1024], BF16))
            cXnT = stm.enter_context(nc.sbuf_tensor("cXnT", [128, 8 * 512], BF16))
            cW1 = [stm.enter_context(nc.sbuf_tensor(f"cW1_{i}", [128, 4 * 8 * 128], BF16)) for i in range(2)]
            cW2 = [stm.enter_context(nc.sbuf_tensor(f"cW2_{i}", [128, 32 * 128], BF16)) for i in range(2)]
            cHW = stm.enter_context(nc.sbuf_tensor("cHW", [128, 32 * 512], BF16))
            cHR = stm.enter_context(nc.sbuf_tensor("cHR", [128, 2 * 512], BF16))
            cWbc = stm.enter_context(nc.sbuf_tensor("cWbc", [128, 2 * 512], BF16))
            cAcc = stm.enter_context(nc.sbuf_tensor("cAcc", [128, 8 * 512], F32))
            sT2 = stm.enter_context(nc.sbuf_tensor("sT2", [128, 1024], F32))
            pH = [stm.enter_context(nc.psum_tensor(f"pH{i}", [128, 512], F32)) for i in range(2)]
            pO = stm.enter_context(nc.psum_tensor("pO", [128, 512], F32))
            pSc = [stm.enter_context(nc.psum_tensor(f"pSc{i}", [128, 512], F32)) for i in range(3)]
            pAv = [stm.enter_context(nc.psum_tensor(f"pAv{i}", [128, 512], F32)) for i in range(2)]

            @blk.gpsimd
            def _(g):
                def cast_pair(e):
                    g.wait_ge(c_w1.h, c_w1.n)
                    g.dma_start(out=d_w1b[e], in_=d_w1t[e]).then_inc(c_w1.h, 16)
                    c_w1.n += 16
                    g.wait_ge(c_w2.h, c_w2.n)
                    g.dma_start(out=d_w2b[e], in_=d_w2t[e]).then_inc(c_w2.h, 16)
                    c_w2.n += 16

                def issue_xr(h):
                    if h >= 1:
                        g.wait_ge(b_sg.h, 4 * h)
                    g.wait_ge(c_xr.h, 16 * h)
                    c_xr.bump(g.dma_start(
                        out=bXr[:].rearrange("p (i c) -> p i c", i=4),
                        in_=d_xo[h * 512:(h + 1) * 512, :].rearrange("(i p) c -> p i c", p=128)), 16)

                g.wait_ge(a_bar.h, A_BAR)
                g.wait_ge(c_sq[0].h, A_SQ[0])
                g.wait_ge(c_sq[1].h, A_SQ[1])
                g.wait_ge(r_wt.h, 8)
                issue_xr(0)
                cast_pair(0)
                issue_xr(1)
                cast_pair(1)
                cast_pair(2)
                issue_xr(2)
                cast_pair(3)
                cast_pair(4)
                cast_pair(5)
                cast_pair(6)
                cast_pair(7)
                for h in range(3, 8):
                    issue_xr(h)
                assert PREFIX >= 1536

            @blk.sync
            def _(sync):
                sync.wait_ge(a_bar.h, A_BAR)
                sync.wait_ge(c_sq[0].h, A_SQ[0])
                sync.wait_ge(c_sq[1].h, A_SQ[1])
                sync.wait_ge(r_wt.h, 8)
                next_w1 = [0]
                next_w2 = [0]

                def issue_w1():
                    ug = next_w1[0]
                    e_mod = (ug // 8) % 8
                    sync.wait_ge(c_w1.h, 16 * (e_mod + 1))
                    if ug >= 2:
                        sync.wait_ge(f_h.h, 4 * ug - 4)
                    c_w1l[ug % 2].bump(sync.dma_start(
                        out=cW1[ug % 2][:], in_=d_w1b[(ug // 8) % 8, ug % 8]), 16)
                    next_w1[0] += 1

                def issue_w2():
                    vg = next_w2[0]
                    e_mod = (vg // 8) % 8
                    sync.wait_ge(c_w2.h, 16 * (e_mod + 1))
                    if vg >= 1:
                        sync.wait_ge(f_o.h, vg - 1)
                    c_w2l[vg % 2].bump(sync.dma_start(
                        out=cW2[vg % 2][:],
                        in_=d_w2b[(vg // 8) % 8, vg % 8].rearrange("p a q -> p (a q)")), 16)
                    next_w2[0] += 1

                for ev in EV:
                    kind = ev[0]
                    if kind == "cxn":
                        tt = ev[1]
                        if tt >= 1:
                            sync.wait_ge(f_h.h, 256 * tt)
                        c_cxn[0].bump(sync.dma_start(
                            out=cXnT[:].rearrange("p (a t) -> p a t", a=8),
                            in_=d_xnts[:, :, tt * 512:(tt + 1) * 512]), 16)
                    elif kind == "hc":
                        ge, fc = ev[1], ev[2]
                        if fc % 4 == 0:
                            ug_cur = ge * 8 + fc // 4
                            while next_w1[0] <= min(ug_cur + 2, 511):
                                issue_w1()
                    elif kind == "oc":
                        ge, dc = ev[1], ev[2]
                        vg_cur = ge * 8 + dc
                        while next_w2[0] <= min(vg_cur + 2, 511):
                            issue_w2()
                    elif kind == "tr":
                        gi, half = ev[1], ev[2]
                        if half == 1:
                            sync.wait_ge(f_t2c.h, 2 * gi + 2)
                            c_t2s.bump(sync.dma_start(
                                out=d_t2[gi * 128:(gi + 1) * 128, :], in_=sT2[:]), 16)
                    elif kind == "bq":
                        k = ev[1]
                        hp2, a2, qt2 = k // 32, (k // 16) % 2, k % 16
                        if k >= 4:
                            sync.wait_ge(b_sc.h, 32 * (k - 3))
                        c_qT[k % 4].bump(sync.dma_start(
                            out=bQt[64 * a2:64 * a2 + 64, (k % 4) * 512:(k % 4 + 1) * 512],
                            in_=d_qts[hp2, 64 * a2:64 * a2 + 64, qt2 * 512:(qt2 + 1) * 512]), 16)

            @blk.tensor
            def _(t):
                for ev in EV:
                    kind = ev[0]
                    if kind == "bc":
                        ge = ev[1]
                        tt, e = ge // 8, ge % 8
                        t.wait_ge(r_wt.h, 8)
                        if ge >= 1:
                            t.wait_ge(f_wb.h, ge)
                        if e == 0 and tt >= 1:
                            t.wait_ge(f_t2c.h, 8 * tt)
                        if ge >= 1:
                            t.wait_ge(f_oe.h, 8 * ge)
                        t_bc.bump(t.matmul(pO[:], cOneHot[:, e * 128:(e + 1) * 128],
                                           cWt[:, tt * 512:(tt + 1) * 512],
                                           start=True, stop=True), 1)
                    elif kind == "hc":
                        ge, fc = ev[1], ev[2]
                        tt, e = ge // 8, ge % 8
                        gf = ge * 32 + fc
                        ug = ge * 8 + fc // 4
                        if e == 0 and fc == 0:
                            t.wait_ge(c_cxn[0].h, 16 * (tt + 1))
                        if fc % 4 == 0:
                            t.wait_ge(c_w1l[ug % 2].h, 16 * (ug // 2 + 1))
                        if gf >= 2:
                            t.wait_ge(f_hr.h, gf - 1)
                        for dc in range(8):
                            inst = t.matmul(pH[fc % 2][:],
                                            cW1[ug % 2][:, ((fc % 4) * 8 + dc) * 128:
                                                          ((fc % 4) * 8 + dc + 1) * 128],
                                            cXnT[:, dc * 512:(dc + 1) * 512],
                                            start=(dc == 0), stop=(dc == 7))
                        f_h.bump(inst, 1)
                    elif kind == "oc":
                        ge, dc = ev[1], ev[2]
                        tt, e = ge // 8, ge % 8
                        vg = ge * 8 + dc
                        t.wait_ge(c_w2l[vg % 2].h, 16 * (vg // 2 + 1))
                        if dc == 0:
                            t.wait_ge(f_hw.h, 32 * (ge + 1))
                        if vg >= 1:
                            t.wait_ge(f_oe.h, vg)
                        first = True
                        if e == 0:
                            t.matmul(pO[:], cB2[:, dc * 128:(dc + 1) * 128],
                                     cWt[:, tt * 512:(tt + 1) * 512],
                                     start=True, stop=False)
                            first = False
                        for fc in range(32):
                            inst = t.matmul(pO[:],
                                            cW2[vg % 2][:, fc * 128:(fc + 1) * 128],
                                            cHW[:, fc * 512:(fc + 1) * 512],
                                            start=(first and fc == 0), stop=(fc == 31))
                        f_o.bump(inst, 1)
                    elif kind == "tr":
                        gi, half = ev[1], ev[2]
                        tt, i = gi // 4, gi % 4
                        tridx = gi * 2 + half
                        if i == 0 and half == 0:
                            t.wait_ge(f_oe.h, 64 * (tt + 1))
                            t.wait_ge(f_wb.h, 8 * (tt + 1))
                        t.wait_ge(f_t2c.h, tridx)
                        for dcq in range(4):
                            dc = half * 4 + dcq
                            inst = t.transpose(pO[:, dcq * 128:(dcq + 1) * 128],
                                               cAcc[:, dc * 512 + i * 128: dc * 512 + (i + 1) * 128],
                                               cIdF[:])
                        f_t2.bump(inst, 1)
                    elif kind == "bu":
                        k, u = ev[1], ev[2]
                        h = k // 16
                        hp2, a2, qt2 = h // 2, h % 2, k % 16
                        gu = k * 32 + u
                        if gu == 0:
                            t.wait_ge(r_wt.h, 8)
                            t.wait_ge(c_vo.h, 1)
                        if gu >= 3:
                            t.wait_ge(b_ex.h, gu - 2)
                        if u == 0:
                            t.wait_ge(c_qT[k % 4].h, 16 * (k // 4 + 1))
                            if qt2 >= 1:
                                t.wait_ge(b_ae.h, k)
                            elif h >= 1:
                                t.wait_ge(b_ae.h, 16 * (h - 1) if h >= 2 else 0)
                                if h >= 2:
                                    t.wait_ge(b_rc.h, 2 * (h - 1))
                        b_sc.bump(t.matmul(
                            pSc[gu % 3][:],
                            KT[64 * a2:64 * a2 + 64,
                               hp2 * TKV + u * 128: hp2 * TKV + (u + 1) * 128],
                            bQt[64 * a2:64 * a2 + 64, (k % 4) * 512:(k % 4 + 1) * 512],
                            start=True, stop=True), 1)

                        def emit_av(um, stop):
                            gum = k * 32 + um
                            inst = t.matmul(
                                pAv[a2][0:65, :],
                                vAll[:, (h * 32 + um) * 65:(h * 32 + um + 1) * 65],
                                bEx[:, (gum % 4) * 512:(gum % 4 + 1) * 512],
                                start=(um == 0), stop=stop)
                            b_av.bump(inst, 1)
                        if u >= 3:
                            emit_av(u - 3, False)
                        if u == 31:
                            for um in (29, 30, 31):
                                t.wait_ge(b_ex.h, k * 32 + um + 1)
                                emit_av(um, um == 31)
                    elif kind == "pdb":
                        h, half = ev[1], ev[2]
                        if half == 0:
                            t.wait_ge(c_d8.h, 16 * (h + 1))
                            t.wait_ge(b_ae.h, 16 * (h + 1))
                        else:
                            t.wait_ge(b_rc.h, 2 * h + 1)
                        b_dn.bump(t.matmul(pAv[h % 2][:], cRep8[:],
                                           bD8[:, half * 512:(half + 1) * 512],
                                           start=True, stop=True), 1)

            @blk.scalar
            def _(s):
                for ev in EV:
                    kind = ev[0]
                    if kind == "hc":
                        ge, fc = ev[1], ev[2]
                        e = ge % 8
                        gf = ge * 32 + fc
                        s.wait_ge(f_h.h, gf + 1)
                        if gf >= 2:
                            s.wait_ge(f_hw.h, gf - 1)
                        f_hr.bump(s.activation(cHR[:, (fc % 2) * 512:(fc % 2 + 1) * 512],
                                               pH[fc % 2][:], AF.Relu,
                                               bias=cB1[:, e * 32 + fc:e * 32 + fc + 1]), 1)
                    elif kind == "bu":
                        k, u = ev[1], ev[2]
                        gu = k * 32 + u
                        s.wait_ge(b_sc.h, gu + 1)
                        if gu >= 4:
                            s.wait_ge(b_av.h, gu - 3)
                        b_ex.bump(s.activation(bEx[:, (gu % 4) * 512:(gu % 4 + 1) * 512],
                                               pSc[gu % 3][:], AF.Exp, scale=0.125), 1)
                    elif kind == "d8":
                        h = ev[1]
                        s.wait_ge(c_bnD[0].h, 128 * (h + 1))
                        s.wait_ge(c_bnD[1].h, 128 * (h + 1))
                        if h >= 1:
                            s.wait_ge(b_dn.h, 2 * h)
                        c_d8.bump(s.dma_start(
                            out=bD8[:], in_=d_bncr2[h % 2].rearrange("(j c) -> j c", j=8)), 16)
                    elif kind == "scr":
                        h = ev[1]
                        s.wait_ge(c_bnA[0].h, 128 * (h + 1))
                        s.wait_ge(c_bnA[1].h, 128 * (h + 1))
                        if h >= 1:
                            s.wait_ge(b_sg.h, 4 * h)
                            s.wait_ge(c_sp.h, A_SP + 16 * h)
                        c_scr.bump(s.dma_start(
                            out=bScr[:].rearrange("p (i c) -> p i c", i=4),
                            in_=d_bnc[64 * (h % 2):64 * (h % 2) + 64, :]
                                .rearrange("d (j c) -> (d j) c", j=8)
                                .rearrange("(i p) c -> p i c", p=128)), 16)
                    elif kind == "nrm":
                        h = ev[1]
                        s.wait_ge(c_xr.h, 16 * (h + 1))
                        for i in range(4):
                            ti = h * 4 + i
                            b_xn.bump(s.activation(bXr[:, i * 1024:(i + 1) * 1024],
                                                   bXr[:, i * 1024:(i + 1) * 1024],
                                                   AF.Identity, bias=stBL[:, ti:ti + 1],
                                                   scale=stRS[:, ti:ti + 1]), 1)
                    elif kind == "bnc":
                        k = ev[1]
                        h, qt2 = k // 16, k % 16
                        s.wait_ge(b_ae.h, k + 1)
                        if h >= 2:
                            s.wait_ge(c_d8.h, 16 * (h - 1))
                            s.wait_ge(c_scr.h, 16 * (h - 1))
                        c_bnA[k % 2].bump(s.dma_start(
                            out=d_bnc[64 * (h % 2):64 * (h % 2) + 64, qt2 * 512:(qt2 + 1) * 512],
                            in_=bAssy[0:64, (k % 2) * 512:(k % 2 + 1) * 512]), 16)
                        c_bnD[k % 2].bump(s.dma_start(
                            out=d_bncr2[h % 2:h % 2 + 1, qt2 * 512:(qt2 + 1) * 512],
                            in_=bAssy[64:65, (k % 2) * 512:(k % 2 + 1) * 512]), 16)
                    elif kind == "sst":
                        h = ev[1]
                        s.wait_ge(b_sg.h, 4 * (h + 1))
                        s.wait_ge(c_sp.h, A_SP + 16 * h)
                        s.dma_start(out=d_ss[h * 512:(h + 1) * 512, :].rearrange("(i p) c -> p i c", p=128),
                                    in_=bScr[:].rearrange("p (i c) -> p i c", i=4)).then_inc(c_sp.h, 16)
                        c_sp.n += 16
                s.wait_ge(b_ex.h, 4096)
                s.wait_ge(f_hr.h, 2048)
                s.wait_ge(b_xn.h, 32)
                s.wait_ge(c_sp.h, A_SP + 16 * 8)
                s.wait_ge(c_bnA[0].h, 16 * 64)
                s.wait_ge(c_bnA[1].h, 16 * 64)
                s.wait_ge(c_bnD[0].h, 16 * 64)
                s.wait_ge(c_bnD[1].h, 16 * 64)
                m_sd.bump(s.memzero(cDmy[:]), 1)

            @blk.vector
            def _(v):
                for ev in EV:
                    kind = ev[0]
                    if kind == "bc":
                        ge = ev[1]
                        v.wait_ge(t_bc.h, ge + 1)
                        if ge >= 2:
                            v.wait_ge(f_hw.h, 32 * (ge - 1))
                        f_wb.bump(v.tensor_copy(cWbc[:, (ge % 2) * 512:(ge % 2 + 1) * 512],
                                                pO[:]), 1)
                    elif kind == "hc":
                        ge, fc = ev[1], ev[2]
                        gf = ge * 32 + fc
                        v.wait_ge(f_hr.h, gf + 1)
                        if fc == 0:
                            v.wait_ge(f_wb.h, ge + 1)
                            if ge >= 1:
                                v.wait_ge(f_o.h, ge * 8)
                        f_hw.bump(v.tensor_tensor(cHW[:, fc * 512:(fc + 1) * 512],
                                                  cHR[:, (fc % 2) * 512:(fc % 2 + 1) * 512],
                                                  cWbc[:, (ge % 2) * 512:(ge % 2 + 1) * 512],
                                                  ALU.mult), 1)
                    elif kind == "oc":
                        ge, dc = ev[1], ev[2]
                        tt, e = ge // 8, ge % 8
                        vg = ge * 8 + dc
                        v.wait_ge(f_o.h, vg + 1)
                        if e == 0:
                            if dc == 0 and tt >= 1:
                                v.wait_ge(f_t2.h, 8 * tt)
                            inst = v.tensor_copy(cAcc[:, dc * 512:(dc + 1) * 512], pO[:])
                        else:
                            inst = v.tensor_tensor(cAcc[:, dc * 512:(dc + 1) * 512],
                                                   pO[:], cAcc[:, dc * 512:(dc + 1) * 512],
                                                   ALU.add)
                        f_oe.bump(inst, 1)
                    elif kind == "tr":
                        gi, half = ev[1], ev[2]
                        tridx = gi * 2 + half
                        v.wait_ge(f_t2.h, tridx + 1)
                        if gi >= 1 and half == 0:
                            v.wait_ge(c_t2s.h, 16 * gi)
                        f_t2c.bump(v.tensor_copy(sT2[:, half * 512:(half + 1) * 512],
                                                 pO[:]), 1)
                    elif kind == "bev":
                        k = ev[1]
                        a2 = (k // 16) % 2
                        v.wait_ge(b_av.h, 32 * (k + 1))
                        if k >= 2:
                            v.wait_ge(c_bnA[k % 2].h, 16 * (k // 2))
                            v.wait_ge(c_bnD[k % 2].h, 16 * (k // 2))
                        b_ae.bump(v.tensor_copy(bAssy[0:65, (k % 2) * 512:(k % 2 + 1) * 512],
                                                pAv[a2][0:65, :]), 1)
                    elif kind == "rcp":
                        h, half = ev[1], ev[2]
                        v.wait_ge(b_dn.h, 2 * h + half + 1)
                        with nc.allow_low_precision(reason="1/den in bf16 is within tolerance"):
                            b_rc.bump(v.reciprocal(bDiv[:, half * 512:(half + 1) * 512],
                                                   pAv[h % 2][:]), 1)
                    elif kind == "sadd":
                        h = ev[1]
                        v.wait_ge(b_rc.h, 2 * (h + 1))
                        v.wait_ge(c_scr.h, 16 * (h + 1))
                        for i in range(4):
                            vf.bump(v.tensor_tensor(bScr[:, i * 1024:(i + 1) * 1024],
                                                    bScr[:, i * 1024:(i + 1) * 1024],
                                                    bDiv[:], ALU.mult), 1)
                        v.wait_ge(vf.h, vf.n)
                        for i in range(4):
                            v.wait_ge(b_xn.h, 4 * h + i + 1)
                            b_sg.bump(v.tensor_tensor(bScr[:, i * 1024:(i + 1) * 1024],
                                                      bScr[:, i * 1024:(i + 1) * 1024],
                                                      bXr[:, i * 1024:(i + 1) * 1024],
                                                      ALU.add), 1)
                v.wait_ge(f_t2c.h, 64)
                v.wait_ge(b_sg.h, 32)
                v.wait_ge(f_oe.h, 512)
                v.wait_ge(b_ae.h, 128)
                v.wait_ge(vf.h, vf.n)
                m_vd.bump(v.engine_nop(), 1)

        assert b_sc.n == 4096 and b_ex.n == 4096 and b_av.n == 4096
        assert b_ae.n == 128 and c_bnA[0].n == 1024 and c_bnD[1].n == 1024
        assert b_dn.n == 16 and b_rc.n == 16 and b_xn.n == 32 and b_sg.n == 32
        assert f_h.n == 2048 and f_hr.n == 2048 and f_hw.n == 2048
        assert f_o.n == 512 and f_oe.n == 512
        assert t_bc.n == 64 and f_wb.n == 64 and f_t2.n == 64 and f_t2c.n == 64
        assert c_t2s.n == 32 * 16 and c_sp.n == A_SP + 128
        st_w2 = c_w1.n

        st_ab.close()

        with ExitStack() as stf:
            cGamF = stf.enter_context(nc.sbuf_tensor("cGamF", [128, D], F32))
            cBetF = stf.enter_context(nc.sbuf_tensor("cBetF", [128, D], F32))
            cS = stf.enter_context(nc.sbuf_tensor("cS", [128, 2 * 1024], BF16))
            cX = stf.enter_context(nc.sbuf_tensor("cX", [128, 2 * 1024], F32))
            cXn2 = stf.enter_context(nc.sbuf_tensor("cXn2", [128, 2 * 1024], F32))
            cT2 = stf.enter_context(nc.sbuf_tensor("cT2", [128, 2 * 1024], F32))
            cTh = stf.enter_context(nc.sbuf_tensor("cTh", [128, 2 * 1024], F32))
            cOut = stf.enter_context(nc.sbuf_tensor("cOut", [128, 2 * 1024], F32))
            nSm = stf.enter_context(nc.sbuf_tensor("nSm", [128, 2], F32))
            nNm = stf.enter_context(nc.sbuf_tensor("nNm", [128, 2], F32))
            nV2 = stf.enter_context(nc.sbuf_tensor("nV2", [128, 2], F32))
            nVar = stf.enter_context(nc.sbuf_tensor("nVar", [128, 2], F32))
            nRS = stf.enter_context(nc.sbuf_tensor("nRS", [128, 2], F32))
            nBL = stf.enter_context(nc.sbuf_tensor("nBL", [128, 2], F32))
            c_gb = sem("c_gb")

            @blk.sync
            def _(sync):
                sync.wait_ge(m_vd.h, 1)
                sync.wait_ge(m_sd.h, 1)
                sync.wait_ge(f_t2.h, 64)
                c_gb.bump(sync.dma_start(out=cGamF[:], in_=d_gam), 16)
                c_gb.bump(sync.dma_start(out=cBetF[:], in_=d_bet), 16)
                for gs in range(32):
                    sl = gs % 2
                    sync.wait_ge(c_t2s.h, 16 * (gs + 1))
                    if gs >= 2:
                        sync.wait_ge(f_fin.h, gs - 1)
                    c_t2l[sl].bump(sync.dma_start(
                        out=cT2[:, sl * 1024:(sl + 1) * 1024],
                        in_=d_t2[gs * 128:(gs + 1) * 128, :]), 16)
                sync.wait_ge(c_out[0].h, 16 * 16)
                sync.wait_ge(c_out[1].h, 16 * 16)

            @blk.gpsimd
            def _(g):
                g.wait_ge(m_vd.h, 1)
                g.wait_ge(m_sd.h, 1)
                g.wait_ge(f_t2.h, 64)
                for gs in range(32):
                    sl = gs % 2
                    tt = gs // 4
                    if gs >= 2:
                        g.wait_ge(f_fin.h, gs - 1)
                    g.wait_ge(c_sp.h, A_SP + 16 * (tt + 1))
                    g.wait_ge(c_cs[sl].h, 16 * (gs // 2))
                    c_cs[sl].bump(g.dma_start(
                        out=cS[:, sl * 1024:(sl + 1) * 1024],
                        in_=d_ss[gs * 128:(gs + 1) * 128, :]), 16)
                    g.wait_ge(c_cxx[sl].h, 16 * (gs // 2))
                    c_cxx[sl].bump(g.dma_start(
                        out=cX[:, sl * 1024:(sl + 1) * 1024],
                        in_=d_xo[gs * 128:(gs + 1) * 128, :]), 16)

            @blk.scalar
            def _(s):
                for gs in range(32):
                    sl = gs % 2
                    s.wait_ge(c_cxx[sl].h, 16 * (gs // 2 + 1))
                    if gs >= 2:
                        s.wait_ge(f_t2a.h, gs - 1)
                    inst = s.activation(cXn2[:, sl * 1024:(sl + 1) * 1024],
                                        cX[:, sl * 1024:(sl + 1) * 1024],
                                        AF.Identity, bias=stBL[:, gs:gs + 1],
                                        scale=stRS[:, gs:gs + 1])
                    s_xn2.bump(inst, 1)
                    s.wait_ge(f_t2a.h, gs + 1)
                    s.wait_ge(n_nm.h, gs + 1)
                    if gs >= 2:
                        s.wait_ge(f_fin.h, gs - 1)
                    inst = s.activation(cTh[:, sl * 1024:(sl + 1) * 1024],
                                        cT2[:, sl * 1024:(sl + 1) * 1024],
                                        AF.Square, bias=nNm[:, sl:sl + 1])
                    s_sq.bump(inst, 1)
                    s.wait_ge(n_var.h, gs + 1)
                    sf.bump(s.activation(nRS[:, sl:sl + 1], nVar[:, sl:sl + 1], AF.Ln), 1)
                    s.wait_ge(sf.h, sf.n)
                    inst = s.activation(nRS[:, sl:sl + 1], nRS[:, sl:sl + 1], AF.Exp, scale=-0.5)
                    s_rs.bump(inst, 1)
                    s.wait_ge(f_bl.h, gs + 1)
                    inst = s.activation(cT2[:, sl * 1024:(sl + 1) * 1024],
                                        cT2[:, sl * 1024:(sl + 1) * 1024],
                                        AF.Identity, bias=nBL[:, sl:sl + 1],
                                        scale=nRS[:, sl:sl + 1])
                    s_A.bump(inst, 1)
                    s.wait_ge(f_y.h, gs + 1)
                    if gs >= 2:
                        s.wait_ge(f_fin.h, gs - 1)
                    inst = s.activation(cTh[:, sl * 1024:(sl + 1) * 1024],
                                        cT2[:, sl * 1024:(sl + 1) * 1024], AF.Tanh, scale=0.5)
                    s_tanh.bump(inst, 1)
                    s.wait_ge(f_fin.h, gs + 1)
                    c_out[sl].bump(s.dma_start(out=d_y[gs * 128:(gs + 1) * 128, :],
                                               in_=cOut[:, sl * 1024:(sl + 1) * 1024]), 16)

            @blk.vector
            def _(v):
                def vfence(inst):
                    n = f_vf.bump(inst, 1)
                    v.wait_ge(f_vf.h, n)
                v.wait_ge(c_gb.h, 32)
                for gs in range(32):
                    sl = gs % 2
                    v.wait_ge(c_t2l[sl].h, 16 * (gs // 2 + 1))
                    v.wait_ge(s_xn2.h, gs + 1)
                    v.wait_ge(c_cs[sl].h, 16 * (gs // 2 + 1))
                    v.wait_ge(c_cxx[sl].h, 16 * (gs // 2 + 1))
                    if gs >= 2:
                        v.wait_ge(s_tanh.h, gs - 1)
                    inst = v.tensor_tensor(cT2[:, sl * 1024:(sl + 1) * 1024],
                                           cT2[:, sl * 1024:(sl + 1) * 1024],
                                           cXn2[:, sl * 1024:(sl + 1) * 1024], ALU.add)
                    f_t2a.bump(inst, 1)
                    v.wait_ge(f_t2a.h, gs + 1)
                    inst = v.reduce_sum(nSm[:, sl:sl + 1], cT2[:, sl * 1024:(sl + 1) * 1024],
                                        axis=AX.X)
                    vfence(inst)
                    inst = v.tensor_scalar(nNm[:, sl:sl + 1], nSm[:, sl:sl + 1],
                                           -1.0 / 1024, None, ALU.mult)
                    n_nm.bump(inst, 1)
                    v.wait_ge(s_sq.h, gs + 1)
                    inst = v.reduce_sum(nV2[:, sl:sl + 1], cTh[:, sl * 1024:(sl + 1) * 1024],
                                        axis=AX.X)
                    vfence(inst)
                    inst = v.tensor_scalar(nVar[:, sl:sl + 1], nV2[:, sl:sl + 1],
                                           1.0 / 1024, 1e-5, ALU.mult, ALU.add)
                    n_var.bump(inst, 1)
                    v.wait_ge(s_rs.h, gs + 1)
                    v.wait_ge(n_nm.h, gs + 1)
                    inst = v.tensor_tensor(nBL[:, sl:sl + 1], nNm[:, sl:sl + 1],
                                           nRS[:, sl:sl + 1], ALU.mult)
                    f_bl.bump(inst, 1)
                    v.wait_ge(s_A.h, gs + 1)
                    inst = v.tensor_tensor(cT2[:, sl * 1024:(sl + 1) * 1024],
                                           cT2[:, sl * 1024:(sl + 1) * 1024], cGamF[:], ALU.mult)
                    vfence(inst)
                    inst = v.tensor_tensor(cT2[:, sl * 1024:(sl + 1) * 1024],
                                           cT2[:, sl * 1024:(sl + 1) * 1024], cBetF[:], ALU.add)
                    f_y.bump(inst, 1)
                    v.wait_ge(s_tanh.h, gs + 1)
                    inst = v.tensor_scalar(cTh[:, sl * 1024:(sl + 1) * 1024],
                                           cTh[:, sl * 1024:(sl + 1) * 1024], 0.5, 0.5,
                                           ALU.mult, ALU.add)
                    vfence(inst)
                    v.wait_ge(f_y.h, gs + 1)
                    inst = v.tensor_tensor(cTh[:, sl * 1024:(sl + 1) * 1024],
                                           cT2[:, sl * 1024:(sl + 1) * 1024],
                                           cTh[:, sl * 1024:(sl + 1) * 1024], ALU.mult)
                    vfence(inst)
                    inst = v.tensor_tensor(cTh[:, sl * 1024:(sl + 1) * 1024],
                                           cS[:, sl * 1024:(sl + 1) * 1024],
                                           cTh[:, sl * 1024:(sl + 1) * 1024], ALU.mult)
                    vfence(inst)
                    if gs >= 2:
                        v.wait_ge(c_out[sl].h, 16 * (gs // 2))
                    inst = v.tensor_tensor(cOut[:, sl * 1024:(sl + 1) * 1024],
                                           cX[:, sl * 1024:(sl + 1) * 1024],
                                           cTh[:, sl * 1024:(sl + 1) * 1024], ALU.add)
                    f_fin.bump(inst, 1)

        assert f_fin.n == 32
        assert c_out[0].n == 16 * 16 and c_out[1].n == 16 * 16

    return nc


def _prep_maps(inputs):
    x = np.ascontiguousarray(np.asarray(inputs["L0_attn_queries"]), dtype=np.float32)
    keys = np.ascontiguousarray(np.asarray(inputs["L0_attn_keys"])[0], dtype=np.float32)
    values = np.ascontiguousarray(np.asarray(inputs["L0_attn_values"])[0], dtype=np.float32)
    Wq = np.asarray(inputs["L0_attn_Wq_w"], dtype=np.float32)
    bq = np.asarray(inputs["L0_attn_Wq_b"], dtype=np.float32)
    Wk = np.asarray(inputs["L0_attn_Wk_w"], dtype=np.float32)
    bk = np.asarray(inputs["L0_attn_Wk_b"], dtype=np.float32)
    Wv = np.asarray(inputs["L0_attn_Wv_w"], dtype=np.float32)
    bv = np.asarray(inputs["L0_attn_Wv_b"], dtype=np.float32)
    Wr = np.asarray(inputs["L0_ffn_Wr"], dtype=np.float32)
    br = np.asarray(inputs["L0_ffn_br"], dtype=np.float32)
    W1 = np.asarray(inputs["L0_ffn_W1"], dtype=np.float32)
    b1 = np.asarray(inputs["L0_ffn_b1"], dtype=np.float32)
    W2 = np.asarray(inputs["L0_ffn_W2"], dtype=np.float32)
    b2 = np.asarray(inputs["L0_ffn_b2"], dtype=np.float32)
    gam = np.asarray(inputs["L0_ffn_gamma"], dtype=np.float32)
    bet = np.asarray(inputs["L0_ffn_beta"], dtype=np.float32)

    keysT = np.ascontiguousarray(keys.T)
    valuesT = np.ascontiguousarray(values.T)
    w1t = np.ascontiguousarray(
        W1.reshape(E, 8, 4, 128, 8, 128).transpose(0, 1, 5, 2, 4, 3).reshape(E, 8, 128, 4096))
    w2t = np.ascontiguousarray(W2.reshape(E, 8, 128, 32, 128).transpose(0, 1, 4, 3, 2))
    wrT = np.ascontiguousarray(Wr.T.reshape(8, 128, 8).transpose(1, 0, 2))
    b1t = np.ascontiguousarray(b1.reshape(E, 32, 128).transpose(2, 0, 1))
    brf = np.ascontiguousarray(br.reshape(1, 8))
    gamb = np.ascontiguousarray(np.broadcast_to(gam, (128, D)))
    betb = np.ascontiguousarray(np.broadcast_to(bet, (128, D)))
    idf = np.eye(128, dtype=np.float32)
    ones1f = np.ones((1, 128), dtype=np.float32)
    ones1b = np.ones((1, 128), dtype=BF)
    onehot = np.zeros((8, 8 * 128), dtype=BF)
    for e in range(8):
        onehot[e, e * 128:(e + 1) * 128] = 1.0
    rep8 = np.zeros((8, 128), dtype=BF)
    for m in range(128):
        rep8[m % 8, m] = 1.0

    maps = []
    for c in range(8):
        b, half = c // 2, c % 2
        cols = slice(half * 512, (half + 1) * 512)
        m = dict(
            x=x[b],
            xo=np.ascontiguousarray(x[b, half * 4096:(half + 1) * 4096]),
            keysT=keysT, valuesT=valuesT,
            wqT=np.ascontiguousarray(Wq.T[:, cols].reshape(8, 128, 4, 128).transpose(1, 0, 2, 3)),
            wkT=np.ascontiguousarray(Wk.T[:, cols].reshape(8, 128, 4, 128).transpose(1, 0, 2, 3)),
            wvT=np.ascontiguousarray(Wv.T[:, cols].reshape(8, 128, 4, 128).transpose(1, 0, 2, 3)),
            bq=np.ascontiguousarray(bq[cols].reshape(4, 128).T),
            bk=np.ascontiguousarray(bk[cols].reshape(4, 128).T),
            bvr=np.ascontiguousarray(bv[cols].reshape(1, 512)),
            wrT=wrT, br=brf, w1t=w1t, w2t=w2t, b1=b1t, b2=np.ascontiguousarray(b2),
            gam=gamb, bet=betb, idf=idf, ones1f=ones1f, ones1b=ones1b,
            onehot=onehot, rep8=rep8,
        )
        maps.append(m)
    return maps


_DT_EXPECT = dict(x="float32", xo="float32", keysT="float32", valuesT="float32",
                  wqT="float32", wkT="float32", wvT="float32", bq="float32",
                  bk="float32", bvr="float32", wrT="float32", br="float32",
                  w1t="float32", w2t="float32", b1="float32", b2="float32",
                  gam="float32", bet="float32", idf="float32", ones1f="float32",
                  ones1b="bfloat16", onehot="bfloat16", rep8="bfloat16")


def kernel(**inputs):
    maps = _prep_maps(inputs)
    for m in maps:
        for k, v in m.items():
            assert str(v.dtype) == _DT_EXPECT[k], (k, v.dtype)
    nc = build_program()
    res = run_bass_kernel_spmd(nc, maps, core_ids=list(range(8)))
    out = np.empty((4, S, D), dtype=np.float32)
    for c in range(8):
        b, half = c // 2, c % 2
        out[b, half * 4096:(half + 1) * 4096] = np.asarray(res.results[c]["y"])
    if _DBG:
        kernel._last = res
    return out


# version 18
# speedup vs baseline: 1.0578x; 1.0578x over previous
import numpy as np
import ml_dtypes
from contextlib import ExitStack

import concourse.bass as bass
import concourse.mybir as mybir
from concourse.bass_utils import run_bass_kernel_spmd

F32 = mybir.dt.float32
BF16 = mybir.dt.bfloat16
AF = mybir.ActivationFunctionType
ALU = mybir.AluOpType
AX = mybir.AxisListType
BF = ml_dtypes.bfloat16

_DBG = False

S = 8192
D = 1024
TKV = 4096
NGRP = 16
NOG = 8
E = 8
NTT = 8
NQT = 16
NGR = 16


class Ctr:
    def __init__(self, stack, nc, name):
        self.h = stack.enter_context(nc.semaphore(name))
        self.n = 0
    def bump(self, inst, k=1):
        inst.then_inc(self.h, k)
        self.n += k
        return self.n


def build_program():
    nc = bass.Bass("TRN2", target_bir_lowering=False)
    DT = nc.dram_tensor
    sk = "ExternalOutput" if _DBG else "Internal"

    d_x = DT("x", [S, D], F32, kind="ExternalInput").ap()
    d_xo = DT("xo", [4096, D], F32, kind="ExternalInput").ap()
    d_keysT = DT("keysT", [D, TKV], F32, kind="ExternalInput").ap()
    d_valuesT = DT("valuesT", [D, TKV], F32, kind="ExternalInput").ap()
    d_wqT = DT("wqT", [128, 8, 4, 128], F32, kind="ExternalInput").ap()
    d_wkT = DT("wkT", [128, 8, 4, 128], F32, kind="ExternalInput").ap()
    d_wvT = DT("wvT", [128, 8, 4, 128], F32, kind="ExternalInput").ap()
    d_bq = DT("bq", [128, 4], F32, kind="ExternalInput").ap()
    d_bk = DT("bk", [128, 4], F32, kind="ExternalInput").ap()
    d_bvr = DT("bvr", [1, 512], F32, kind="ExternalInput").ap()
    d_wrT = DT("wrT", [128, 8, 8], F32, kind="ExternalInput").ap()
    d_br = DT("br", [1, 8], F32, kind="ExternalInput").ap()
    d_w1t = DT("w1t", [E, 8, 128, 4096], F32, kind="ExternalInput").ap()
    d_w2t = DT("w2t", [E, 8, 128, 32, 128], F32, kind="ExternalInput").ap()
    d_b1 = DT("b1", [128, E, 32], F32, kind="ExternalInput").ap()
    d_b2 = DT("b2", [8, D], F32, kind="ExternalInput").ap()
    d_gam = DT("gam", [128, D], F32, kind="ExternalInput").ap()
    d_bet = DT("bet", [128, D], F32, kind="ExternalInput").ap()
    d_idf = DT("idf", [128, 128], F32, kind="ExternalInput").ap()
    d_ones1f = DT("ones1f", [1, 128], F32, kind="ExternalInput").ap()
    d_ones1b = DT("ones1b", [1, 128], BF16, kind="ExternalInput").ap()
    d_onehot = DT("onehot", [8, 8 * 128], BF16, kind="ExternalInput").ap()
    d_rep8 = DT("rep8", [8, 128], BF16, kind="ExternalInput").ap()

    d_y = DT("y", [4096, D], F32, kind="ExternalOutput").ap()
    d_t2 = DT("t2", [4096, D], F32, kind="Internal").ap()
    d_qts = DT("qts", [4, 128, S], BF16, kind=sk).ap()
    d_xnts = DT("xnts", [128, 8, 4096], BF16, kind=sk).ap()
    d_ss = DT("ss", [4096, D], BF16, kind=sk).ap()
    d_w1b = DT("w1b", [E, 8, 128, 4096], BF16, kind="Internal").ap()
    d_w2b = DT("w2b", [E, 8, 128, 32, 128], BF16, kind="Internal").ap()
    d_bnc = DT("bnc", [128, S], BF16, kind="Internal").ap()
    d_bncr2 = DT("bncr2", [2, S], BF16, kind="Internal").ap()

    with ExitStack() as st0:
        def sb(name, shape, dt):
            return st0.enter_context(nc.sbuf_tensor(name, shape, dt))
        def sem(name):
            return Ctr(st0, nc, name)

        cIdF = sb("cIdF", [128, 128], F32)
        cOn1f = sb("cOn1f", [1, 128], F32)
        cOn1b = sb("cOn1b", [1, 128], BF16)
        cOneHot = sb("cOneHot", [8, 8 * 128], BF16)
        cRep8 = sb("cRep8", [8, 128], BF16)
        cWrT = sb("cWrT", [128, 8 * 8], F32)
        cBr = sb("cBr", [1, 8], F32)
        cBq = sb("cBq", [128, 4], F32)
        cBk = sb("cBk", [128, 4], F32)
        cBvR = sb("cBvR", [1, 512], BF16)
        cB1 = sb("cB1", [128, E * 32], F32)
        cB2 = sb("cB2", [8, D], BF16)
        stRS = sb("stRS", [128, 32], F32)
        stBL = sb("stBL", [128, 32], F32)
        stLG = sb("stLG", [128, 32 * 8], F32)
        cDmy = sb("cDmy", [1, 4], F32)
        st_ab = ExitStack()
        KT = st_ab.enter_context(nc.sbuf_tensor("KT", [128, 4 * TKV], BF16))
        vAll = st_ab.enter_context(nc.sbuf_tensor("vAll", [128, 8 * 32 * 65], BF16))
        cWt = st_ab.enter_context(nc.sbuf_tensor("cWt", [8, 8 * 512], BF16))
        st_w = ExitStack()
        cWq = st_w.enter_context(nc.sbuf_tensor("cWq", [128, 8 * 4 * 128], BF16))
        cWk = st_w.enter_context(nc.sbuf_tensor("cWk", [128, 8 * 4 * 128], BF16))
        cWv = st_w.enter_context(nc.sbuf_tensor("cWv", [128, 8 * 4 * 128], BF16))

        c_const = sem("c_const")
        c_constg = sem("c_constg")
        c_kv = sem("c_kv")
        c_x = [sem("c_x0"), sem("c_x1")]
        c_sp = sem("c_sp")
        c_sq = [sem("c_sq0"), sem("c_sq1")]
        c_qT = [sem("c_qT0"), sem("c_qT1"), sem("c_qT2"), sem("c_qT3")]
        c_xr = sem("c_xr"); c_d8 = sem("c_d8"); c_scr = sem("c_scr")
        c_w1 = sem("c_w1")
        c_w2 = sem("c_w2")
        c_w1l = [sem("c_w1l0"), sem("c_w1l1"), sem("c_w1l2")]
        c_w2l = [sem("c_w2l0"), sem("c_w2l1"), sem("c_w2l2")]
        c_cxn = [sem("c_cxn0"), sem("c_cxn1")]
        c_cs = [sem("c_cs0"), sem("c_cs1")]
        c_cxx = [sem("c_cxx0"), sem("c_cxx1")]
        c_out = [sem("c_out0"), sem("c_out1")]
        k_mm = sem("k_mm"); k_ev = sem("k_ev")
        a_mu = sem("a_mu"); a_sq = sem("a_sq"); a_var = sem("a_var")
        a_rs = sem("a_rs"); a_bl = sem("a_bl"); a_xn = sem("a_xn")
        a_tr = sem("a_tr"); a_ev = sem("a_ev"); a_q = sem("a_q")
        a_qe = sem("a_qe"); a_lg = sem("a_lg"); a_lgs = sem("a_lgs")
        a_bar = sem("a_bar")
        c_bn = sem("c_bn")
        b_sc = sem("b_sc"); b_ex = sem("b_ex"); b_av = sem("b_av")
        b_ae = sem("b_ae"); b_ve = sem("b_ve"); b_dn = sem("b_dn")
        b_rc = sem("b_rc"); b_xn = sem("b_xn"); b_sg = sem("b_sg")
        f_h = sem("f_h"); f_hr = sem("f_hr"); f_hw = sem("f_hw")
        f_o = sem("f_o"); f_oe = sem("f_oe")
        t_wb = sem("t_wb"); f_wb = sem("f_wb")
        r_w = sem("r_w"); r_t = sem("r_t"); r_wt = sem("r_wt")
        s_rex = sem("s_rex"); s_xn2 = sem("s_xn2")
        n_nm = sem("n_nm"); n_var = sem("n_var")
        s_sq = sem("s_sq"); s_rs = sem("s_rs"); s_A = sem("s_A"); s_tanh = sem("s_tanh")
        vf = sem("vf"); sf = sem("sf")
        f_t2 = sem("f_t2"); f_t2a = sem("f_t2a"); f_bl = sem("f_bl")
        f_vf = sem("f_vf")
        f_y = sem("f_y"); f_fin = sem("f_fin")
        c_vo = sem("c_vo"); t_bc = sem("t_bc"); f_t2c = sem("f_t2c")
        c_t2s = sem("c_t2s"); c_t2l = [sem("c_t2l0"), sem("c_t2l1")]
        c_bnA = [sem("c_bnA0"), sem("c_bnA1")]
        c_bnD = [sem("c_bnD0"), sem("c_bnD1")]
        m_vd = sem("m_vd"); m_sd = sem("m_sd")

        blk = st0.enter_context(nc.Block())

        @blk.sync
        def _(sync):
            for dst, src in [(cIdF, d_idf), (cOn1f, d_ones1f), (cOn1b, d_ones1b),
                             (cOneHot, d_onehot), (cRep8, d_rep8), (cBr, d_br),
                             (cBq, d_bq), (cBk, d_bk)]:
                c_const.bump(sync.dma_start(out=dst[:], in_=src), 16)
            c_const.bump(sync.dma_start(
                out=cWrT[:].rearrange("p (a b) -> p a b", a=8), in_=d_wrT), 16)
            c_const.bump(sync.dma_start(
                out=cB1[:].rearrange("p (a b) -> p a b", a=E), in_=d_b1), 16)

        @blk.gpsimd
        def _(g):
            for dst, src in [(cWq, d_wqT), (cWk, d_wkT), (cWv, d_wvT)]:
                for half in range(2):
                    c_constg.bump(g.dma_start(
                        out=dst[:, half * 2048:(half + 1) * 2048]
                            .rearrange("p (a b c) -> p a b c", a=4, b=4),
                        in_=src[:, half * 4:(half + 1) * 4]), 16)
            c_constg.bump(g.dma_start(out=cBvR[:], in_=d_bvr), 16)
            c_constg.bump(g.dma_start(out=cB2[:], in_=d_b2), 16)
        N_CONST = 10

        @blk.vector
        def _(v):
            c_vo.bump(v.memset(
                vAll[:].rearrange("p (h t e) -> p h t e", h=8, t=32)[:, :, :, 64:65], 1.0), 1)

        with ExitStack() as stkv:
            kvK = [stkv.enter_context(nc.sbuf_tensor(f"kvK{i}", [128, 8 * 512], BF16)) for i in range(2)]
            kvV = [stkv.enter_context(nc.sbuf_tensor(f"kvV{i}", [128, 8 * 512], BF16)) for i in range(2)]
            pKV = [stkv.enter_context(nc.psum_tensor(f"pKV{i}", [128, 512], F32)) for i in range(4)]

            @blk.gpsimd
            def _(g):
                for tg in range(8):
                    if tg >= 2:
                        g.wait_ge(k_mm.h, 8 * (tg - 1))
                    g.wait_ge(c_kv.h, c_kv.n)
                    for half in range(2):
                        g.dma_start(
                            out=kvK[tg % 2][:, half * 2048:(half + 1) * 2048]
                                .rearrange("p (a b) -> p a b", a=4),
                            in_=d_keysT[half * 512:(half + 1) * 512, tg * 512:(tg + 1) * 512]
                                .rearrange("(a p) t -> p a t", p=128)
                        ).then_inc(c_kv.h, 16); c_kv.n += 16
                    g.wait_ge(c_kv.h, c_kv.n)
                    for half in range(2):
                        g.dma_start(
                            out=kvV[tg % 2][:, half * 2048:(half + 1) * 2048]
                                .rearrange("p (a b) -> p a b", a=4),
                            in_=d_valuesT[half * 512:(half + 1) * 512, tg * 512:(tg + 1) * 512]
                                .rearrange("(a p) t -> p a t", p=128)
                        ).then_inc(c_kv.h, 16); c_kv.n += 16

            @blk.tensor
            def _(t):
                t.wait_ge(c_const.h, 16 * N_CONST)
                t.wait_ge(c_constg.h, 16 * 8)
                for tg in range(8):
                    t.wait_ge(c_kv.h, 64 * (tg + 1))
                    kb, vb = kvK[tg % 2], kvV[tg % 2]
                    for dkc in range(4):
                        if dkc >= 2:
                            t.wait_ge(k_ev.h, 8 * tg + dkc - 1)
                        elif tg >= 1:
                            t.wait_ge(k_ev.h, 8 * tg + dkc - 5)
                        for dc in range(8):
                            inst = t.matmul(pKV[dkc % 2][:],
                                            cWk[:, (dc * 4 + dkc) * 128:(dc * 4 + dkc + 1) * 128],
                                            kb[:, dc * 512:(dc + 1) * 512],
                                            start=(dc == 0), stop=(dc == 7))
                        k_mm.bump(inst, 1)
                    for tcl in range(4):
                        if tcl >= 2:
                            t.wait_ge(k_ev.h, 8 * tg + tcl + 3)
                        elif tg >= 1:
                            t.wait_ge(k_ev.h, 8 * tg + tcl - 1)
                        for dc in range(8):
                            t.matmul(pKV[2 + tcl % 2][:],
                                     vb[:, dc * 512 + tcl * 128: dc * 512 + (tcl + 1) * 128],
                                     cWv[:, dc * 512:(dc + 1) * 512],
                                     start=(dc == 0), stop=False)
                        inst = t.matmul(pKV[2 + tcl % 2][:], cOn1b[:], cBvR[:],
                                        start=False, stop=True)
                        k_mm.bump(inst, 1)

            @blk.scalar
            def _(s):
                for tg in range(8):
                    for dkc in range(4):
                        s.wait_ge(k_mm.h, 8 * tg + dkc + 1)
                        inst = s.activation(KT[:, dkc * TKV + tg * 512: dkc * TKV + (tg + 1) * 512],
                                            pKV[dkc % 2][:], AF.Relu, bias=cBk[:, dkc:dkc + 1])
                        k_ev.bump(inst, 1)
                    for tcl in range(4):
                        s.wait_ge(k_mm.h, 8 * tg + 4 + tcl + 1)
                        tc = tg * 4 + tcl
                        inst = s.activation(
                            vAll[:].rearrange("p (h m) -> p h m", h=8)[:, :, tc * 65:tc * 65 + 64],
                            pKV[2 + tcl % 2][:].rearrange("p (h d) -> p h d", h=8),
                            AF.Relu)
                        k_ev.bump(inst, 1)

        assert k_mm.n == 64 and k_ev.n == 64


        def phaseA(tag, src_ap, ngrp, own):
            base_bar = a_bar.n
            base_mu, base_sq, base_var = a_mu.n, a_sq.n, a_var.n
            base_rs, base_bl, base_xn = a_rs.n, a_bl.n, a_xn.n
            base_tr, base_ev, base_q = a_tr.n, a_ev.n, a_q.n
            base_qe, base_lg = a_qe.n, a_lg.n
            base_lgs = a_lgs.n
            base_cx = [c_x[0].n, c_x[1].n]
            base_sp = c_sp.n
            base_cq = [c_sq[0].n, c_sq[1].n]
            BAR = 2
            tprog = (a_lg if own else a_q)
            base_tp = tprog.n
            EVB = 8 if own else 4
            EVG = 8 * EVB

            with ExitStack() as sta:
                aX = [sta.enter_context(nc.sbuf_tensor(f"aX{tag}{i}", [128, 4 * 1024], F32)) for i in range(2)]
                aSq = sta.enter_context(nc.sbuf_tensor(f"aSq{tag}", [128, 4 * 1024], F32))
                aXn = sta.enter_context(nc.sbuf_tensor(f"aXn{tag}", [128, 4 * 1024], F32))
                aSm = sta.enter_context(nc.sbuf_tensor(f"aSm{tag}", [128, 4], F32))
                aNM = sta.enter_context(nc.sbuf_tensor(f"aNM{tag}", [128, 4], F32))
                aVar = sta.enter_context(nc.sbuf_tensor(f"aVar{tag}", [128, 4], F32))
                aRS = sta.enter_context(nc.sbuf_tensor(f"aRS{tag}", [128, 4], F32))
                aBL = sta.enter_context(nc.sbuf_tensor(f"aBL{tag}", [128, 4], F32))
                aXnTs = [sta.enter_context(nc.sbuf_tensor(f"aXnT{tag}{i}", [128, 8 * 512], BF16))
                         for i in range(2 if own else 1)]
                aXnT32 = sta.enter_context(nc.sbuf_tensor(f"aXnT32{tag}", [128, 8 * 512], F32)) if own else None
                aQe = [sta.enter_context(nc.sbuf_tensor(f"aQe{tag}{i}", [128, 4 * 512], BF16))
                       for i in range(2)] if not own else None
                pTr = [sta.enter_context(nc.psum_tensor(f"pTr{tag}{i}", [128, 512], F32)) for i in range(2)]
                pQL = [sta.enter_context(nc.psum_tensor(f"pQL{tag}{i}", [128, 512], F32)) for i in range(4)]

                @blk.sync
                def _(sync):
                    for g in range(ngrp):
                        if g == 0:
                            sync.wait_ge(k_ev.h, 64)
                            sync.wait_ge(a_bar.h, base_bar)
                            sync.wait_ge(a_q.h, base_q)
                            sync.wait_ge(c_sq[0].h, base_cq[0])
                            sync.wait_ge(c_sq[1].h, base_cq[1])
                        if g >= 2:
                            sync.wait_ge(a_bar.h, base_bar + BAR * (g - 1))
                            sync.wait_ge(tprog.h, base_tp + 4 * (g - 1))
                        for i in range(4):
                            c_x[g % 2].bump(sync.dma_start(
                                out=aX[g % 2][:, i * 1024:(i + 1) * 1024],
                                in_=src_ap[g * 512 + i * 128: g * 512 + (i + 1) * 128, :]
                            ), 16)

                @blk.vector
                def _(v):
                    for g in range(ngrp):
                        if g >= 1:
                            v.wait_ge(a_bar.h, base_bar + BAR * g)
                            v.wait_ge(tprog.h, base_tp + 4 * g)
                        xb = aX[g % 2]
                        v.wait_ge(c_x[g % 2].h, base_cx[g % 2] + 64 * (g // 2 + 1))
                        for i in range(4):
                            vf.bump(v.reduce_sum(aSm[:, i:i + 1], xb[:, i * 1024:(i + 1) * 1024], axis=AX.X), 1)
                        v.wait_ge(vf.h, vf.n)
                        for i in range(4):
                            inst = v.tensor_scalar(aNM[:, i:i + 1], aSm[:, i:i + 1],
                                                   -1.0 / 1024, None, ALU.mult)
                            a_mu.bump(inst, 1)
                        for i in range(4):
                            v.wait_ge(a_sq.h, base_sq + 4 * g + i + 1)
                            vf.bump(v.reduce_sum(aVar[:, i:i + 1], aSq[:, i * 1024:(i + 1) * 1024], axis=AX.X), 1)
                        v.wait_ge(vf.h, vf.n)
                        for i in range(4):
                            inst = v.tensor_scalar(aVar[:, i:i + 1], aVar[:, i:i + 1],
                                                   1.0 / 1024, 1e-5, ALU.mult, ALU.add)
                            a_var.bump(inst, 1)
                        v.wait_ge(a_mu.h, base_mu + 4 * g + 4)
                        for i in range(4):
                            v.wait_ge(a_rs.h, base_rs + 4 * g + i + 1)
                            inst = v.tensor_tensor(aBL[:, i:i + 1], aNM[:, i:i + 1],
                                                   aRS[:, i:i + 1], ALU.mult)
                            a_bl.bump(inst, 1)
                        if own:
                            v.wait_ge(a_bl.h, base_bl + 4 * g + 4)
                            vf.bump(v.tensor_copy(stRS[:, 4 * g:4 * g + 4], aRS[:]), 1)
                            vf.bump(v.tensor_copy(stBL[:, 4 * g:4 * g + 4], aBL[:]), 1)
                        if own and g >= 2:
                            v.wait_ge(c_sq[g % 2].h, base_cq[g % 2] + 64 * (g // 2))
                        xnt = aXnTs[g % 2] if own else aXnTs[0]
                        for i in range(4):
                            for bank in range(2):
                                v.wait_ge(a_tr.h, base_tr + 8 * g + 2 * i + bank + 1)
                                p = pTr[bank]
                                for dcq in range(4):
                                    dc = bank * 4 + dcq
                                    a_ev.bump(v.tensor_copy(
                                        xnt[:, dc * 512 + i * 128: dc * 512 + (i + 1) * 128],
                                        p[:, dcq * 128:(dcq + 1) * 128]), 1)
                                    if own:
                                        a_ev.bump(v.tensor_copy(
                                            aXnT32[:, dc * 512 + i * 128: dc * 512 + (i + 1) * 128],
                                            p[:, dcq * 128:(dcq + 1) * 128]), 1)
                        if own:
                            for i in range(4):
                                v.wait_ge(a_lg.h, base_lg + 4 * g + i + 1)
                                inst = v.tensor_copy(stLG[:, (4 * g + i) * 8:(4 * g + i + 1) * 8],
                                                     pQL[i][:, 0:8])
                                a_lgs.bump(inst, 1)
                        v.wait_ge(a_ev.h, base_ev + EVG * (g + 1))
                        if own:
                            v.wait_ge(a_lgs.h, base_lgs + 4 * (g + 1))
                        v.wait_ge(vf.h, vf.n)
                        inst = v.engine_nop()
                        a_bar.bump(inst, 1)

                @blk.scalar
                def _(s):
                    for g in range(ngrp):
                        if g >= 1:
                            s.wait_ge(a_bar.h, base_bar + BAR * g)
                            s.wait_ge(tprog.h, base_tp + 4 * g)
                        xb = aX[g % 2]
                        for i in range(4):
                            s.wait_ge(a_mu.h, base_mu + 4 * g + i + 1)
                            inst = s.activation(aSq[:, i * 1024:(i + 1) * 1024],
                                                xb[:, i * 1024:(i + 1) * 1024],
                                                AF.Square, bias=aNM[:, i:i + 1])
                            a_sq.bump(inst, 1)
                        for i in range(4):
                            s.wait_ge(a_var.h, base_var + 4 * g + i + 1)
                            sf.bump(s.activation(aRS[:, i:i + 1], aVar[:, i:i + 1], AF.Ln), 1)
                        s.wait_ge(sf.h, sf.n)
                        for i in range(4):
                            inst = s.activation(aRS[:, i:i + 1], aRS[:, i:i + 1], AF.Exp, scale=-0.5)
                            a_rs.bump(inst, 1)
                        for i in range(4):
                            s.wait_ge(a_bl.h, base_bl + 4 * g + i + 1)
                            inst = s.activation(aXn[:, i * 1024:(i + 1) * 1024],
                                                xb[:, i * 1024:(i + 1) * 1024],
                                                AF.Identity, bias=aBL[:, i:i + 1], scale=aRS[:, i:i + 1])
                            a_xn.bump(inst, 1)
                        if not own:
                            if g >= 2:
                                s.wait_ge(c_sq[g % 2].h, base_cq[g % 2] + 64 * (g // 2))
                            for dqc in range(4):
                                s.wait_ge(a_q.h, base_q + 4 * g + dqc + 1)
                                inst = s.activation(aQe[g % 2][:, dqc * 512:(dqc + 1) * 512], pQL[dqc][:],
                                                    AF.Relu, bias=cBq[:, dqc:dqc + 1])
                                a_qe.bump(inst, 1)
                            s.wait_ge(a_qe.h, base_qe + 4 * (g + 1))
                            for ai in range(4):
                                s.dma_start(
                                    out=d_qts[ai, :, g * 512:(g + 1) * 512],
                                    in_=aQe[g % 2][:, ai * 512:(ai + 1) * 512]
                                ).then_inc(c_sq[g % 2].h, 16); c_sq[g % 2].n += 16
                        else:
                            s.wait_ge(a_ev.h, base_ev + EVG * (g + 1))
                            for j in range(4):
                                s.dma_start(
                                    out=d_xnts[:, 2 * j:2 * j + 2, g * 512:(g + 1) * 512],
                                    in_=aXnTs[g % 2][:, (2 * j) * 512:(2 * j + 2) * 512]
                                        .rearrange("p (a t) -> p a t", a=2)
                                ).then_inc(c_sq[g % 2].h, 16); c_sq[g % 2].n += 16
                        a_bar.bump(s.memzero(cDmy[:]), 1)

                @blk.tensor
                def _(t):
                    for g in range(ngrp):
                        if g == 0:
                            t.wait_ge(k_ev.h, 64)
                            t.wait_ge(a_bar.h, base_bar)
                            t.wait_ge(a_q.h, base_q)
                        if g >= 1:
                            t.wait_ge(a_bar.h, base_bar + BAR * g)
                        for i in range(4):
                            t.wait_ge(a_xn.h, base_xn + 4 * g + i + 1)
                            for bank in range(2):
                                prev = 2 * i + bank - 2
                                if prev >= 0:
                                    t.wait_ge(a_ev.h, base_ev + EVG * g + EVB * (prev + 1))
                                p = pTr[bank]
                                for dcq in range(4):
                                    dc = bank * 4 + dcq
                                    inst = t.transpose(p[:, dcq * 128:(dcq + 1) * 128],
                                                       aXn[:, i * 1024 + dc * 128: i * 1024 + (dc + 1) * 128],
                                                       cIdF[:])
                                a_tr.bump(inst, 1)
                        t.wait_ge(a_ev.h, base_ev + EVG * (g + 1))
                        if not own:
                            for dqc in range(4):
                                for dc in range(8):
                                    inst = t.matmul(pQL[dqc][:],
                                                    cWq[:, (dc * 4 + dqc) * 128:(dc * 4 + dqc + 1) * 128],
                                                    aXnTs[0][:, dc * 512:(dc + 1) * 512],
                                                    start=(dc == 0), stop=(dc == 7))
                                a_q.bump(inst, 1)
                        else:
                            for i in range(4):
                                for dc in range(8):
                                    t.matmul(pQL[i][:, 0:8],
                                             aXnT32[:, dc * 512 + i * 128: dc * 512 + (i + 1) * 128],
                                             cWrT[:, dc * 8:(dc + 1) * 8],
                                             start=(dc == 0), stop=False)
                                inst = t.matmul(pQL[i][:, 0:8], cOn1f[:], cBr[:],
                                                start=False, stop=True)
                                a_lg.bump(inst, 1)

        phaseA("1", d_x, NGRP, own=False)
        phaseA("2", d_xo, NOG, own=True)
        st_w.close()
        A_SP = c_sp.n
        A_SQ = (c_sq[0].n, c_sq[1].n)
        assert A_SP == 0 and A_SQ == (64 * (NGRP + NOG) // 2,) * 2
        assert a_lgs.n == 32 and a_bar.n == 2 * (NGRP + NOG)
        A_BAR = a_bar.n


        with ExitStack() as str_:
            rMx = str_.enter_context(nc.sbuf_tensor("rMx", [128, 32 * 8], F32))
            rNm = str_.enter_context(nc.sbuf_tensor("rNm", [128, 32], F32))
            rE2 = str_.enter_context(nc.sbuf_tensor("rE2", [128, 32], F32))
            rRc = str_.enter_context(nc.sbuf_tensor("rRc", [128, 32], F32))
            rEx = str_.enter_context(nc.sbuf_tensor("rEx", [128, 32 * 8], F32))
            rMk = str_.enter_context(nc.sbuf_tensor("rMk", [128, 32 * 8], F32))
            rW = str_.enter_context(nc.sbuf_tensor("rW", [128, 32 * 8], F32))
            pRt = str_.enter_context(nc.psum_tensor("pRt", [128, 512], F32))

            @blk.scalar
            def _(s):
                for i2 in range(32):
                    s.wait_ge(r_w.h, 32 + i2 + 1)
                    s_rex.bump(s.activation(rE2[:, i2:i2 + 1], rMx[:, i2 * 8 + 1:i2 * 8 + 2],
                                            AF.Exp, bias=rNm[:, i2:i2 + 1]), 1)
                    s_rex.bump(s.activation(rEx[:, i2 * 8:(i2 + 1) * 8],
                                            stLG[:, i2 * 8:(i2 + 1) * 8],
                                            AF.Exp, bias=rNm[:, i2:i2 + 1]), 1)

            @blk.vector
            def _(v):
                v.wait_ge(a_lgs.h, 32)
                v.wait_ge(c_sq[0].h, A_SQ[0])
                v.wait_ge(c_sq[1].h, A_SQ[1])
                for i2 in range(32):
                    r_w.bump(v.max(out=rMx[:, i2 * 8:(i2 + 1) * 8],
                                   in_=stLG[:, i2 * 8:(i2 + 1) * 8]), 1)
                v.wait_ge(r_w.h, 32)
                for i2 in range(32):
                    r_w.bump(v.tensor_scalar(rNm[:, i2:i2 + 1], rMx[:, i2 * 8:i2 * 8 + 1],
                                             -1.0, None, ALU.mult), 1)
                for i2 in range(32):
                    v.wait_ge(s_rex.h, 2 * i2 + 1)
                    vf.bump(v.tensor_scalar(rE2[:, i2:i2 + 1], rE2[:, i2:i2 + 1],
                                            1.0, 1.0, ALU.mult, ALU.add), 1)
                v.wait_ge(vf.h, vf.n)
                for i2 in range(32):
                    vf.bump(v.reciprocal(rRc[:, i2:i2 + 1], rE2[:, i2:i2 + 1]), 1)
                for i2 in range(32):
                    vf.bump(v.tensor_scalar(rMk[:, i2 * 8:(i2 + 1) * 8],
                                            stLG[:, i2 * 8:(i2 + 1) * 8],
                                            rMx[:, i2 * 8 + 1:i2 * 8 + 2], None, ALU.is_ge), 1)
                v.wait_ge(s_rex.h, 64)
                v.wait_ge(vf.h, vf.n)
                for i2 in range(32):
                    vf.bump(v.tensor_scalar(rEx[:, i2 * 8:(i2 + 1) * 8], rEx[:, i2 * 8:(i2 + 1) * 8],
                                            rRc[:, i2:i2 + 1], None, ALU.mult), 1)
                v.wait_ge(vf.h, vf.n)
                for i2 in range(32):
                    r_w.bump(v.tensor_tensor(rW[:, i2 * 8:(i2 + 1) * 8], rEx[:, i2 * 8:(i2 + 1) * 8],
                                             rMk[:, i2 * 8:(i2 + 1) * 8], ALU.mult), 1)
                for tt in range(NTT):
                    v.wait_ge(r_t.h, 4 * (tt + 1))
                    r_wt.bump(v.tensor_copy(cWt[:, tt * 512:(tt + 1) * 512], pRt[0:8, :]), 1)

            @blk.tensor
            def _(t):
                for tt in range(NTT):
                    if tt >= 1:
                        t.wait_ge(r_wt.h, tt)
                    for i in range(4):
                        i2 = tt * 4 + i
                        t.wait_ge(r_w.h, 64 + i2 + 1)
                        inst = t.transpose(pRt[0:8, i * 128:(i + 1) * 128],
                                           rW[:, i2 * 8:(i2 + 1) * 8], cIdF[:])
                        r_t.bump(inst, 1)
        assert r_w.n == 96 and r_t.n == 32 and r_wt.n == 8 and s_rex.n == 64

        chains = []
        for tt in range(NTT):
            chains.append(("cxn", tt))
            if tt >= 1:
                for i in range(4):
                    for half in range(2):
                        chains.append(("tr", (tt - 1) * 4 + i, half, 4))
            for e in range(E):
                ge = tt * 8 + e
                chains.append(("bc", ge, 0))
                for fc in range(32):
                    chains.append(("hc", ge, fc, 8))
                for dc in range(8):
                    chains.append(("oc", ge, dc, 32))
        for i in range(4):
            for half in range(2):
                chains.append(("tr", 28 + i, half, 4))
        chains_tt = []
        cur_tt = 0
        for ch in chains:
            if ch[0] == "cxn":
                cur_tt = ch[1]
            chains_tt.append((cur_tt, ch))
        PREFIX = 1536
        cmm_tt = [0] * NTT
        for tt, ch in chains_tt:
            if ch[0] in ("tr", "hc", "oc"):
                cmm_tt[tt] += ch[-1]
        w_tt = [1.9, 1.35, 1.1, 0.95, 0.85, 0.8, 0.75, 0.7]
        rem = 4096 - PREFIX
        denom = sum(w_tt[t] * cmm_tt[t] for t in range(NTT))
        rate_tt = [rem * w_tt[t] / denom for t in range(NTT)]
        EV = []
        pend = []
        bu = 0

        def push(ev):
            EV.append(ev)
            for p in pend:
                p[0] -= 1
            ready = [p for p in pend if p[0] <= 0]
            pend[:] = [p for p in pend if p[0] > 0]
            for p in ready:
                EV.append(p[1])

        def emit_unit(k, u):
            if u == 0 and k + 2 < 128:
                push(("bq", k + 2))
            push(("bu", k, u))
            if u == 31:
                push(("bev", k))
                pend.append([2, ("bnc", k)])
                if k % 16 == 15:
                    h = k // 16
                    pend.extend([[3, ("d8", h)], [5, ("pdb", h, 0)], [6, ("rcp", h, 0)],
                                 [7, ("scr", h)], [8, ("nrm", h)], [9, ("pdb", h, 1)],
                                 [10, ("rcp", h, 1)], [13, ("sadd", h)], [23, ("sst", h)]])

        push(("bq", 0))
        push(("bq", 1))
        while bu < PREFIX:
            emit_unit(bu // 32, bu % 32)
            bu += 1
        pend.sort(key=lambda p: p[0])
        for p in pend:
            EV.append(p[1])
        pend.clear()
        acc = 0.0
        for tt, ch in chains_tt:
            push(ch)
            if ch[0] in ("tr", "hc", "oc"):
                acc += ch[-1] * rate_tt[tt]
            while acc >= 1.0 and bu < 4096:
                acc -= 1.0
                emit_unit(bu // 32, bu % 32)
                bu += 1
        while bu < 4096:
            emit_unit(bu // 32, bu % 32)
            bu += 1
        pend.sort(key=lambda p: p[0])
        for p in pend:
            EV.append(p[1])

        with ExitStack() as stm:
            bQt = stm.enter_context(nc.sbuf_tensor("bQt", [128, 4 * 512], BF16))
            bEx = stm.enter_context(nc.sbuf_tensor("bEx", [128, 4 * 512], BF16))
            bXr = stm.enter_context(nc.sbuf_tensor("bXr", [128, 4 * 1024], BF16))
            bScr = stm.enter_context(nc.sbuf_tensor("bScr", [128, 4 * 1024], BF16))
            bDiv = stm.enter_context(nc.sbuf_tensor("bDiv", [128, 1024], BF16))
            bAssy = stm.enter_context(nc.sbuf_tensor("bAssy", [65, 2 * 512], BF16))
            bD8 = stm.enter_context(nc.sbuf_tensor("bD8", [8, 1024], BF16))
            cXnT = stm.enter_context(nc.sbuf_tensor("cXnT", [128, 8 * 512], BF16))
            cW1 = [stm.enter_context(nc.sbuf_tensor(f"cW1_{i}", [128, 4 * 8 * 128], BF16)) for i in range(2)]
            cW2 = [stm.enter_context(nc.sbuf_tensor(f"cW2_{i}", [128, 32 * 128], BF16)) for i in range(2)]
            cHW = stm.enter_context(nc.sbuf_tensor("cHW", [128, 32 * 512], BF16))
            cHR = stm.enter_context(nc.sbuf_tensor("cHR", [128, 2 * 512], BF16))
            cWbc = stm.enter_context(nc.sbuf_tensor("cWbc", [128, 2 * 512], BF16))
            cAcc = stm.enter_context(nc.sbuf_tensor("cAcc", [128, 8 * 512], F32))
            sT2 = stm.enter_context(nc.sbuf_tensor("sT2", [128, 1024], F32))
            pH = [stm.enter_context(nc.psum_tensor(f"pH{i}", [128, 512], F32)) for i in range(2)]
            pO = stm.enter_context(nc.psum_tensor("pO", [128, 512], F32))
            pT2m = stm.enter_context(nc.psum_tensor("pT2m", [128, 512], F32))
            pSc = [stm.enter_context(nc.psum_tensor(f"pSc{i}", [128, 512], F32)) for i in range(2)]
            pAv = [stm.enter_context(nc.psum_tensor(f"pAv{i}", [128, 512], F32)) for i in range(2)]

            @blk.gpsimd
            def _(g):
                def cast_pair(e):
                    g.wait_ge(c_w1.h, c_w1.n)
                    g.dma_start(out=d_w1b[e], in_=d_w1t[e]).then_inc(c_w1.h, 16)
                    c_w1.n += 16
                    g.wait_ge(c_w2.h, c_w2.n)
                    g.dma_start(out=d_w2b[e], in_=d_w2t[e]).then_inc(c_w2.h, 16)
                    c_w2.n += 16

                def issue_xr(h):
                    if h >= 1:
                        g.wait_ge(b_sg.h, 4 * h)
                    g.wait_ge(c_xr.h, 16 * h)
                    c_xr.bump(g.dma_start(
                        out=bXr[:].rearrange("p (i c) -> p i c", i=4),
                        in_=d_xo[h * 512:(h + 1) * 512, :].rearrange("(i p) c -> p i c", p=128)), 16)

                g.wait_ge(a_bar.h, A_BAR)
                g.wait_ge(c_sq[0].h, A_SQ[0])
                g.wait_ge(c_sq[1].h, A_SQ[1])
                g.wait_ge(r_wt.h, 8)
                issue_xr(0)
                cast_pair(0)
                issue_xr(1)
                cast_pair(1)
                cast_pair(2)
                issue_xr(2)
                cast_pair(3)
                cast_pair(4)
                cast_pair(5)
                cast_pair(6)
                cast_pair(7)
                for h in range(3, 8):
                    issue_xr(h)
                assert PREFIX >= 1536

            @blk.sync
            def _(sync):
                sync.wait_ge(a_bar.h, A_BAR)
                sync.wait_ge(c_sq[0].h, A_SQ[0])
                sync.wait_ge(c_sq[1].h, A_SQ[1])
                sync.wait_ge(r_wt.h, 8)
                next_w1 = [0]
                next_w2 = [0]

                def issue_w1():
                    ug = next_w1[0]
                    e_mod = (ug // 8) % 8
                    sync.wait_ge(c_w1.h, 16 * (e_mod + 1))
                    if ug >= 2:
                        sync.wait_ge(f_h.h, 4 * ug - 4)
                    c_w1l[ug % 2].bump(sync.dma_start(
                        out=cW1[ug % 2][:], in_=d_w1b[(ug // 8) % 8, ug % 8]), 16)
                    next_w1[0] += 1

                def issue_w2():
                    vg = next_w2[0]
                    e_mod = (vg // 8) % 8
                    sync.wait_ge(c_w2.h, 16 * (e_mod + 1))
                    if vg >= 1:
                        sync.wait_ge(f_o.h, vg - 1)
                    c_w2l[vg % 2].bump(sync.dma_start(
                        out=cW2[vg % 2][:],
                        in_=d_w2b[(vg // 8) % 8, vg % 8].rearrange("p a q -> p (a q)")), 16)
                    next_w2[0] += 1

                for ev in EV:
                    kind = ev[0]
                    if kind == "cxn":
                        tt = ev[1]
                        if tt >= 1:
                            sync.wait_ge(f_h.h, 256 * tt)
                        c_cxn[0].bump(sync.dma_start(
                            out=cXnT[:].rearrange("p (a t) -> p a t", a=8),
                            in_=d_xnts[:, :, tt * 512:(tt + 1) * 512]), 16)
                    elif kind == "hc":
                        ge, fc = ev[1], ev[2]
                        if fc % 4 == 0:
                            ug_cur = ge * 8 + fc // 4
                            while next_w1[0] <= min(ug_cur + 2, 511):
                                issue_w1()
                    elif kind == "oc":
                        ge, dc = ev[1], ev[2]
                        vg_cur = ge * 8 + dc
                        while next_w2[0] <= min(vg_cur + 2, 511):
                            issue_w2()
                    elif kind == "tr":
                        gi, half = ev[1], ev[2]
                        if half == 1:
                            sync.wait_ge(f_t2c.h, 2 * gi + 2)
                            c_t2s.bump(sync.dma_start(
                                out=d_t2[gi * 128:(gi + 1) * 128, :], in_=sT2[:]), 16)
                    elif kind == "bq":
                        k = ev[1]
                        hp2, a2, qt2 = k // 32, (k // 16) % 2, k % 16
                        if k >= 4:
                            sync.wait_ge(b_sc.h, 32 * (k - 3))
                        c_qT[k % 4].bump(sync.dma_start(
                            out=bQt[64 * a2:64 * a2 + 64, (k % 4) * 512:(k % 4 + 1) * 512],
                            in_=d_qts[hp2, 64 * a2:64 * a2 + 64, qt2 * 512:(qt2 + 1) * 512]), 16)

            @blk.tensor
            def _(t):
                for ev in EV:
                    kind = ev[0]
                    if kind == "bc":
                        ge = ev[1]
                        tt, e = ge // 8, ge % 8
                        t.wait_ge(r_wt.h, 8)
                        if ge >= 1:
                            t.wait_ge(f_wb.h, ge)
                        if e == 0 and tt >= 1:
                            t.wait_ge(f_t2c.h, 8 * tt)
                        t_bc.bump(t.matmul(pT2m[:], cOneHot[:, e * 128:(e + 1) * 128],
                                           cWt[:, tt * 512:(tt + 1) * 512],
                                           start=True, stop=True), 1)
                    elif kind == "hc":
                        ge, fc = ev[1], ev[2]
                        tt, e = ge // 8, ge % 8
                        gf = ge * 32 + fc
                        ug = ge * 8 + fc // 4
                        if e == 0 and fc == 0:
                            t.wait_ge(c_cxn[0].h, 16 * (tt + 1))
                        if fc % 4 == 0:
                            t.wait_ge(c_w1l[ug % 2].h, 16 * (ug // 2 + 1))
                        if gf >= 2:
                            t.wait_ge(f_hr.h, gf - 1)
                        for dc in range(8):
                            inst = t.matmul(pH[fc % 2][:],
                                            cW1[ug % 2][:, ((fc % 4) * 8 + dc) * 128:
                                                          ((fc % 4) * 8 + dc + 1) * 128],
                                            cXnT[:, dc * 512:(dc + 1) * 512],
                                            start=(dc == 0), stop=(dc == 7))
                        f_h.bump(inst, 1)
                    elif kind == "oc":
                        ge, dc = ev[1], ev[2]
                        tt, e = ge // 8, ge % 8
                        vg = ge * 8 + dc
                        t.wait_ge(c_w2l[vg % 2].h, 16 * (vg // 2 + 1))
                        if dc == 0:
                            t.wait_ge(f_hw.h, 32 * (ge + 1))
                        if vg >= 1:
                            t.wait_ge(f_oe.h, vg)
                        first = True
                        if e == 0:
                            t.matmul(pO[:], cB2[:, dc * 128:(dc + 1) * 128],
                                     cWt[:, tt * 512:(tt + 1) * 512],
                                     start=True, stop=False)
                            first = False
                        for fc in range(32):
                            inst = t.matmul(pO[:],
                                            cW2[vg % 2][:, fc * 128:(fc + 1) * 128],
                                            cHW[:, fc * 512:(fc + 1) * 512],
                                            start=(first and fc == 0), stop=(fc == 31))
                        f_o.bump(inst, 1)
                    elif kind == "tr":
                        gi, half = ev[1], ev[2]
                        tt, i = gi // 4, gi % 4
                        tridx = gi * 2 + half
                        if i == 0 and half == 0:
                            t.wait_ge(f_oe.h, 64 * (tt + 1))
                            t.wait_ge(f_wb.h, 8 * (tt + 1))
                        t.wait_ge(f_t2c.h, tridx)
                        for dcq in range(4):
                            dc = half * 4 + dcq
                            inst = t.transpose(pT2m[:, dcq * 128:(dcq + 1) * 128],
                                               cAcc[:, dc * 512 + i * 128: dc * 512 + (i + 1) * 128],
                                               cIdF[:])
                        f_t2.bump(inst, 1)
                    elif kind == "bu":
                        k, u = ev[1], ev[2]
                        h = k // 16
                        hp2, a2, qt2 = h // 2, h % 2, k % 16
                        gu = k * 32 + u
                        if gu == 0:
                            t.wait_ge(r_wt.h, 8)
                            t.wait_ge(c_vo.h, 1)
                        if gu >= 2:
                            t.wait_ge(b_ex.h, gu - 1)
                        if u == 0:
                            t.wait_ge(c_qT[k % 4].h, 16 * (k // 4 + 1))
                            if qt2 >= 1:
                                t.wait_ge(b_ae.h, k)
                            elif h >= 1:
                                t.wait_ge(b_ae.h, 16 * (h - 1) if h >= 2 else 0)
                                if h >= 2:
                                    t.wait_ge(b_rc.h, 2 * (h - 1))
                        b_sc.bump(t.matmul(
                            pSc[u % 2][:],
                            KT[64 * a2:64 * a2 + 64,
                               hp2 * TKV + u * 128: hp2 * TKV + (u + 1) * 128],
                            bQt[64 * a2:64 * a2 + 64, (k % 4) * 512:(k % 4 + 1) * 512],
                            start=True, stop=True), 1)

                        def emit_av(um, stop):
                            gum = k * 32 + um
                            inst = t.matmul(
                                pAv[a2][0:65, :],
                                vAll[:, (h * 32 + um) * 65:(h * 32 + um + 1) * 65],
                                bEx[:, (gum % 4) * 512:(gum % 4 + 1) * 512],
                                start=(um == 0), stop=stop)
                            b_av.bump(inst, 1)
                        if u >= 2:
                            emit_av(u - 2, False)
                        if u == 31:
                            for um in (30, 31):
                                t.wait_ge(b_ex.h, k * 32 + um + 1)
                                emit_av(um, um == 31)
                    elif kind == "pdb":
                        h, half = ev[1], ev[2]
                        if half == 0:
                            t.wait_ge(c_d8.h, 16 * (h + 1))
                            t.wait_ge(b_ae.h, 16 * (h + 1))
                        else:
                            t.wait_ge(b_rc.h, 2 * h + 1)
                        b_dn.bump(t.matmul(pAv[h % 2][:], cRep8[:],
                                           bD8[:, half * 512:(half + 1) * 512],
                                           start=True, stop=True), 1)

            @blk.scalar
            def _(s):
                for ev in EV:
                    kind = ev[0]
                    if kind == "hc":
                        ge, fc = ev[1], ev[2]
                        e = ge % 8
                        gf = ge * 32 + fc
                        s.wait_ge(f_h.h, gf + 1)
                        if gf >= 2:
                            s.wait_ge(f_hw.h, gf - 1)
                        f_hr.bump(s.activation(cHR[:, (fc % 2) * 512:(fc % 2 + 1) * 512],
                                               pH[fc % 2][:], AF.Relu,
                                               bias=cB1[:, e * 32 + fc:e * 32 + fc + 1]), 1)
                    elif kind == "bu":
                        k, u = ev[1], ev[2]
                        gu = k * 32 + u
                        s.wait_ge(b_sc.h, gu + 1)
                        if gu >= 4:
                            s.wait_ge(b_av.h, gu - 3)
                        b_ex.bump(s.activation(bEx[:, (gu % 4) * 512:(gu % 4 + 1) * 512],
                                               pSc[u % 2][:], AF.Exp, scale=0.125), 1)
                    elif kind == "d8":
                        h = ev[1]
                        s.wait_ge(c_bnD[0].h, 128 * (h + 1))
                        s.wait_ge(c_bnD[1].h, 128 * (h + 1))
                        if h >= 1:
                            s.wait_ge(b_dn.h, 2 * h)
                        c_d8.bump(s.dma_start(
                            out=bD8[:], in_=d_bncr2[h % 2].rearrange("(j c) -> j c", j=8)), 16)
                    elif kind == "scr":
                        h = ev[1]
                        s.wait_ge(c_bnA[0].h, 128 * (h + 1))
                        s.wait_ge(c_bnA[1].h, 128 * (h + 1))
                        if h >= 1:
                            s.wait_ge(b_sg.h, 4 * h)
                            s.wait_ge(c_sp.h, A_SP + 16 * h)
                        c_scr.bump(s.dma_start(
                            out=bScr[:].rearrange("p (i c) -> p i c", i=4),
                            in_=d_bnc[64 * (h % 2):64 * (h % 2) + 64, :]
                                .rearrange("d (j c) -> (d j) c", j=8)
                                .rearrange("(i p) c -> p i c", p=128)), 16)
                    elif kind == "nrm":
                        h = ev[1]
                        s.wait_ge(c_xr.h, 16 * (h + 1))
                        for i in range(4):
                            ti = h * 4 + i
                            b_xn.bump(s.activation(bXr[:, i * 1024:(i + 1) * 1024],
                                                   bXr[:, i * 1024:(i + 1) * 1024],
                                                   AF.Identity, bias=stBL[:, ti:ti + 1],
                                                   scale=stRS[:, ti:ti + 1]), 1)
                    elif kind == "bnc":
                        k = ev[1]
                        h, qt2 = k // 16, k % 16
                        s.wait_ge(b_ae.h, k + 1)
                        if h >= 2:
                            s.wait_ge(c_d8.h, 16 * (h - 1))
                            s.wait_ge(c_scr.h, 16 * (h - 1))
                        c_bnA[k % 2].bump(s.dma_start(
                            out=d_bnc[64 * (h % 2):64 * (h % 2) + 64, qt2 * 512:(qt2 + 1) * 512],
                            in_=bAssy[0:64, (k % 2) * 512:(k % 2 + 1) * 512]), 16)
                        c_bnD[k % 2].bump(s.dma_start(
                            out=d_bncr2[h % 2:h % 2 + 1, qt2 * 512:(qt2 + 1) * 512],
                            in_=bAssy[64:65, (k % 2) * 512:(k % 2 + 1) * 512]), 16)
                    elif kind == "sst":
                        h = ev[1]
                        s.wait_ge(b_sg.h, 4 * (h + 1))
                        s.wait_ge(c_sp.h, A_SP + 16 * h)
                        s.dma_start(out=d_ss[h * 512:(h + 1) * 512, :].rearrange("(i p) c -> p i c", p=128),
                                    in_=bScr[:].rearrange("p (i c) -> p i c", i=4)).then_inc(c_sp.h, 16)
                        c_sp.n += 16
                s.wait_ge(b_ex.h, 4096)
                s.wait_ge(f_hr.h, 2048)
                s.wait_ge(b_xn.h, 32)
                s.wait_ge(c_sp.h, A_SP + 16 * 8)
                s.wait_ge(c_bnA[0].h, 16 * 64)
                s.wait_ge(c_bnA[1].h, 16 * 64)
                s.wait_ge(c_bnD[0].h, 16 * 64)
                s.wait_ge(c_bnD[1].h, 16 * 64)
                m_sd.bump(s.memzero(cDmy[:]), 1)

            @blk.vector
            def _(v):
                for ev in EV:
                    kind = ev[0]
                    if kind == "bc":
                        ge = ev[1]
                        v.wait_ge(t_bc.h, ge + 1)
                        if ge >= 2:
                            v.wait_ge(f_hw.h, 32 * (ge - 1))
                        f_wb.bump(v.tensor_copy(cWbc[:, (ge % 2) * 512:(ge % 2 + 1) * 512],
                                                pT2m[:]), 1)
                    elif kind == "hc":
                        ge, fc = ev[1], ev[2]
                        gf = ge * 32 + fc
                        v.wait_ge(f_hr.h, gf + 1)
                        if fc == 0:
                            v.wait_ge(f_wb.h, ge + 1)
                            if ge >= 1:
                                v.wait_ge(f_o.h, ge * 8)
                        f_hw.bump(v.tensor_tensor(cHW[:, fc * 512:(fc + 1) * 512],
                                                  cHR[:, (fc % 2) * 512:(fc % 2 + 1) * 512],
                                                  cWbc[:, (ge % 2) * 512:(ge % 2 + 1) * 512],
                                                  ALU.mult), 1)
                    elif kind == "oc":
                        ge, dc = ev[1], ev[2]
                        tt, e = ge // 8, ge % 8
                        vg = ge * 8 + dc
                        v.wait_ge(f_o.h, vg + 1)
                        if e == 0:
                            if dc == 0 and tt >= 1:
                                v.wait_ge(f_t2.h, 8 * tt)
                            inst = v.tensor_copy(cAcc[:, dc * 512:(dc + 1) * 512], pO[:])
                        else:
                            inst = v.tensor_tensor(cAcc[:, dc * 512:(dc + 1) * 512],
                                                   pO[:], cAcc[:, dc * 512:(dc + 1) * 512],
                                                   ALU.add)
                        f_oe.bump(inst, 1)
                    elif kind == "tr":
                        gi, half = ev[1], ev[2]
                        tridx = gi * 2 + half
                        v.wait_ge(f_t2.h, tridx + 1)
                        if gi >= 1 and half == 0:
                            v.wait_ge(c_t2s.h, 16 * gi)
                        f_t2c.bump(v.tensor_copy(sT2[:, half * 512:(half + 1) * 512],
                                                 pT2m[:]), 1)
                    elif kind == "bev":
                        k = ev[1]
                        a2 = (k // 16) % 2
                        v.wait_ge(b_av.h, 32 * (k + 1))
                        if k >= 2:
                            v.wait_ge(c_bnA[k % 2].h, 16 * (k // 2))
                            v.wait_ge(c_bnD[k % 2].h, 16 * (k // 2))
                        b_ae.bump(v.tensor_copy(bAssy[0:65, (k % 2) * 512:(k % 2 + 1) * 512],
                                                pAv[a2][0:65, :]), 1)
                    elif kind == "rcp":
                        h, half = ev[1], ev[2]
                        v.wait_ge(b_dn.h, 2 * h + half + 1)
                        with nc.allow_low_precision(reason="1/den in bf16 is within tolerance"):
                            b_rc.bump(v.reciprocal(bDiv[:, half * 512:(half + 1) * 512],
                                                   pAv[h % 2][:]), 1)
                    elif kind == "sadd":
                        h = ev[1]
                        v.wait_ge(b_rc.h, 2 * (h + 1))
                        v.wait_ge(c_scr.h, 16 * (h + 1))
                        for i in range(4):
                            vf.bump(v.tensor_tensor(bScr[:, i * 1024:(i + 1) * 1024],
                                                    bScr[:, i * 1024:(i + 1) * 1024],
                                                    bDiv[:], ALU.mult), 1)
                        v.wait_ge(vf.h, vf.n)
                        for i in range(4):
                            v.wait_ge(b_xn.h, 4 * h + i + 1)
                            b_sg.bump(v.tensor_tensor(bScr[:, i * 1024:(i + 1) * 1024],
                                                      bScr[:, i * 1024:(i + 1) * 1024],
                                                      bXr[:, i * 1024:(i + 1) * 1024],
                                                      ALU.add), 1)
                v.wait_ge(f_t2c.h, 64)
                v.wait_ge(b_sg.h, 32)
                v.wait_ge(f_oe.h, 512)
                v.wait_ge(b_ae.h, 128)
                v.wait_ge(vf.h, vf.n)
                m_vd.bump(v.engine_nop(), 1)

        assert b_sc.n == 4096 and b_ex.n == 4096 and b_av.n == 4096
        assert b_ae.n == 128 and c_bnA[0].n == 1024 and c_bnD[1].n == 1024
        assert b_dn.n == 16 and b_rc.n == 16 and b_xn.n == 32 and b_sg.n == 32
        assert f_h.n == 2048 and f_hr.n == 2048 and f_hw.n == 2048
        assert f_o.n == 512 and f_oe.n == 512
        assert t_bc.n == 64 and f_wb.n == 64 and f_t2.n == 64 and f_t2c.n == 64
        assert c_t2s.n == 32 * 16 and c_sp.n == A_SP + 128
        st_w2 = c_w1.n

        st_ab.close()

        with ExitStack() as stf:
            cGamF = stf.enter_context(nc.sbuf_tensor("cGamF", [128, D], F32))
            cBetF = stf.enter_context(nc.sbuf_tensor("cBetF", [128, D], F32))
            cS = stf.enter_context(nc.sbuf_tensor("cS", [128, 4 * 1024], BF16))
            cX = stf.enter_context(nc.sbuf_tensor("cX", [128, 4 * 1024], F32))
            cXn2 = stf.enter_context(nc.sbuf_tensor("cXn2", [128, 4 * 1024], F32))
            cT2 = stf.enter_context(nc.sbuf_tensor("cT2", [128, 4 * 1024], F32))
            cTh = stf.enter_context(nc.sbuf_tensor("cTh", [128, 4 * 1024], F32))
            cOut = stf.enter_context(nc.sbuf_tensor("cOut", [128, 4 * 1024], F32))
            nSm = stf.enter_context(nc.sbuf_tensor("nSm", [128, 4], F32))
            nNm = stf.enter_context(nc.sbuf_tensor("nNm", [128, 4], F32))
            nV2 = stf.enter_context(nc.sbuf_tensor("nV2", [128, 4], F32))
            nVar = stf.enter_context(nc.sbuf_tensor("nVar", [128, 4], F32))
            nRS = stf.enter_context(nc.sbuf_tensor("nRS", [128, 4], F32))
            nBL = stf.enter_context(nc.sbuf_tensor("nBL", [128, 4], F32))
            c_gb = sem("c_gb")

            @blk.sync
            def _(sync):
                sync.wait_ge(m_vd.h, 1)
                sync.wait_ge(m_sd.h, 1)
                sync.wait_ge(f_t2.h, 64)
                c_gb.bump(sync.dma_start(out=cGamF[:], in_=d_gam), 16)
                c_gb.bump(sync.dma_start(out=cBetF[:], in_=d_bet), 16)
                for gs in range(32):
                    sl = gs % 4
                    sync.wait_ge(c_t2s.h, 16 * (gs + 1))
                    if gs >= 4:
                        sync.wait_ge(f_fin.h, gs - 3)
                    sync.wait_ge(c_t2l[gs % 2].h, 16 * (gs // 2))
                    c_t2l[gs % 2].bump(sync.dma_start(
                        out=cT2[:, sl * 1024:(sl + 1) * 1024],
                        in_=d_t2[gs * 128:(gs + 1) * 128, :]), 16)
                sync.wait_ge(c_out[0].h, 16 * 16)
                sync.wait_ge(c_out[1].h, 16 * 16)

            @blk.gpsimd
            def _(g):
                g.wait_ge(m_vd.h, 1)
                g.wait_ge(m_sd.h, 1)
                g.wait_ge(f_t2.h, 64)
                for gs in range(32):
                    sl = gs % 4
                    tt = gs // 4
                    if gs >= 4:
                        g.wait_ge(f_fin.h, gs - 3)
                    g.wait_ge(c_sp.h, A_SP + 16 * (tt + 1))
                    g.wait_ge(c_cs[gs % 2].h, 16 * (gs // 2))
                    c_cs[gs % 2].bump(g.dma_start(
                        out=cS[:, sl * 1024:(sl + 1) * 1024],
                        in_=d_ss[gs * 128:(gs + 1) * 128, :]), 16)
                    g.wait_ge(c_cxx[gs % 2].h, 16 * (gs // 2))
                    c_cxx[gs % 2].bump(g.dma_start(
                        out=cX[:, sl * 1024:(sl + 1) * 1024],
                        in_=d_xo[gs * 128:(gs + 1) * 128, :]), 16)

            @blk.scalar
            def _(s):
                ln_cnt = {}

                def S(stage, gs):
                    sl = gs % 4
                    lo, hi = sl * 1024, (sl + 1) * 1024
                    if stage == 0:
                        s.wait_ge(c_cxx[gs % 2].h, 16 * (gs // 2 + 1))
                        if gs >= 4:
                            s.wait_ge(f_t2a.h, gs - 3)
                        s_xn2.bump(s.activation(cXn2[:, lo:hi], cX[:, lo:hi],
                                                AF.Identity, bias=stBL[:, gs:gs + 1],
                                                scale=stRS[:, gs:gs + 1]), 1)
                    elif stage == 1:
                        s.wait_ge(f_t2a.h, gs + 1)
                        s.wait_ge(n_nm.h, gs + 1)
                        if gs >= 4:
                            s.wait_ge(f_fin.h, gs - 3)
                        s_sq.bump(s.activation(cTh[:, lo:hi], cT2[:, lo:hi],
                                               AF.Square, bias=nNm[:, sl:sl + 1]), 1)
                    elif stage == 2:
                        s.wait_ge(n_var.h, gs + 1)
                        ln_cnt[gs] = sf.bump(s.activation(nRS[:, sl:sl + 1],
                                                          nVar[:, sl:sl + 1], AF.Ln), 1)
                    elif stage == 3:
                        s.wait_ge(sf.h, ln_cnt[gs])
                        s_rs.bump(s.activation(nRS[:, sl:sl + 1], nRS[:, sl:sl + 1],
                                               AF.Exp, scale=-0.5), 1)
                    elif stage == 4:
                        s.wait_ge(f_bl.h, gs + 1)
                        s_A.bump(s.activation(cT2[:, lo:hi], cT2[:, lo:hi],
                                              AF.Identity, bias=nBL[:, sl:sl + 1],
                                              scale=nRS[:, sl:sl + 1]), 1)
                    elif stage == 5:
                        s.wait_ge(f_y.h, gs + 1)
                        s_tanh.bump(s.activation(cTh[:, lo:hi], cT2[:, lo:hi],
                                                 AF.Tanh, scale=0.5), 1)
                    elif stage == 6:
                        s.wait_ge(f_fin.h, gs + 1)
                        s.wait_ge(c_out[gs % 2].h, 16 * (gs // 2))
                        c_out[gs % 2].bump(s.dma_start(
                            out=d_y[gs * 128:(gs + 1) * 128, :],
                            in_=cOut[:, lo:hi]), 16)

                for p in range(16):
                    for stage in range(7):
                        for j in range(2):
                            S(stage, 2 * p + j)

            @blk.vector
            def _(v):
                vfc = {}

                def V(stage, gs):
                    sl = gs % 4
                    lo, hi = sl * 1024, (sl + 1) * 1024
                    if stage == 0:
                        v.wait_ge(c_t2l[gs % 2].h, 16 * (gs // 2 + 1))
                        v.wait_ge(s_xn2.h, gs + 1)
                        v.wait_ge(c_cs[gs % 2].h, 16 * (gs // 2 + 1))
                        v.wait_ge(c_cxx[gs % 2].h, 16 * (gs // 2 + 1))
                        if gs >= 4:
                            v.wait_ge(s_tanh.h, gs - 3)
                        f_t2a.bump(v.tensor_tensor(cT2[:, lo:hi], cT2[:, lo:hi],
                                                   cXn2[:, lo:hi], ALU.add), 1)
                    elif stage == 1:
                        v.wait_ge(f_t2a.h, gs + 1)
                        vfc[(gs, 0)] = vf.bump(v.reduce_sum(nSm[:, sl:sl + 1],
                                                            cT2[:, lo:hi], axis=AX.X), 1)
                    elif stage == 2:
                        v.wait_ge(vf.h, vfc[(gs, 0)])
                        n_nm.bump(v.tensor_scalar(nNm[:, sl:sl + 1], nSm[:, sl:sl + 1],
                                                  -1.0 / 1024, None, ALU.mult), 1)
                    elif stage == 3:
                        v.wait_ge(s_sq.h, gs + 1)
                        vfc[(gs, 1)] = vf.bump(v.reduce_sum(nV2[:, sl:sl + 1],
                                                            cTh[:, lo:hi], axis=AX.X), 1)
                    elif stage == 4:
                        v.wait_ge(vf.h, vfc[(gs, 1)])
                        n_var.bump(v.tensor_scalar(nVar[:, sl:sl + 1], nV2[:, sl:sl + 1],
                                                   1.0 / 1024, 1e-5, ALU.mult, ALU.add), 1)
                    elif stage == 5:
                        v.wait_ge(s_rs.h, gs + 1)
                        v.wait_ge(n_nm.h, gs + 1)
                        f_bl.bump(v.tensor_tensor(nBL[:, sl:sl + 1], nNm[:, sl:sl + 1],
                                                  nRS[:, sl:sl + 1], ALU.mult), 1)
                    elif stage == 6:
                        v.wait_ge(s_A.h, gs + 1)
                        vfc[(gs, 2)] = vf.bump(v.tensor_tensor(cT2[:, lo:hi], cT2[:, lo:hi],
                                                               cGamF[:], ALU.mult), 1)
                    elif stage == 7:
                        v.wait_ge(vf.h, vfc[(gs, 2)])
                        f_y.bump(v.tensor_tensor(cT2[:, lo:hi], cT2[:, lo:hi],
                                                 cBetF[:], ALU.add), 1)
                    elif stage == 8:
                        v.wait_ge(s_tanh.h, gs + 1)
                        vfc[(gs, 3)] = vf.bump(v.tensor_scalar(cTh[:, lo:hi], cTh[:, lo:hi],
                                                               0.5, 0.5, ALU.mult, ALU.add), 1)
                    elif stage == 9:
                        v.wait_ge(vf.h, vfc[(gs, 3)])
                        v.wait_ge(f_y.h, gs + 1)
                        vfc[(gs, 4)] = vf.bump(v.tensor_tensor(cTh[:, lo:hi], cT2[:, lo:hi],
                                                               cTh[:, lo:hi], ALU.mult), 1)
                    elif stage == 10:
                        v.wait_ge(vf.h, vfc[(gs, 4)])
                        vfc[(gs, 5)] = vf.bump(v.tensor_tensor(cTh[:, lo:hi], cS[:, lo:hi],
                                                               cTh[:, lo:hi], ALU.mult), 1)
                    elif stage == 11:
                        v.wait_ge(vf.h, vfc[(gs, 5)])
                        if gs >= 4:
                            v.wait_ge(c_out[gs % 2].h, 16 * (gs // 2 - 1))
                        f_fin.bump(v.tensor_tensor(cOut[:, lo:hi], cX[:, lo:hi],
                                                   cTh[:, lo:hi], ALU.add), 1)

                v.wait_ge(c_gb.h, 32)
                for p in range(16):
                    for stage in range(12):
                        for j in range(2):
                            V(stage, 2 * p + j)

        assert f_fin.n == 32
        assert c_out[0].n == 16 * 16 and c_out[1].n == 16 * 16

    return nc


def _prep_maps(inputs):
    x = np.ascontiguousarray(np.asarray(inputs["L0_attn_queries"]), dtype=np.float32)
    keys = np.ascontiguousarray(np.asarray(inputs["L0_attn_keys"])[0], dtype=np.float32)
    values = np.ascontiguousarray(np.asarray(inputs["L0_attn_values"])[0], dtype=np.float32)
    Wq = np.asarray(inputs["L0_attn_Wq_w"], dtype=np.float32)
    bq = np.asarray(inputs["L0_attn_Wq_b"], dtype=np.float32)
    Wk = np.asarray(inputs["L0_attn_Wk_w"], dtype=np.float32)
    bk = np.asarray(inputs["L0_attn_Wk_b"], dtype=np.float32)
    Wv = np.asarray(inputs["L0_attn_Wv_w"], dtype=np.float32)
    bv = np.asarray(inputs["L0_attn_Wv_b"], dtype=np.float32)
    Wr = np.asarray(inputs["L0_ffn_Wr"], dtype=np.float32)
    br = np.asarray(inputs["L0_ffn_br"], dtype=np.float32)
    W1 = np.asarray(inputs["L0_ffn_W1"], dtype=np.float32)
    b1 = np.asarray(inputs["L0_ffn_b1"], dtype=np.float32)
    W2 = np.asarray(inputs["L0_ffn_W2"], dtype=np.float32)
    b2 = np.asarray(inputs["L0_ffn_b2"], dtype=np.float32)
    gam = np.asarray(inputs["L0_ffn_gamma"], dtype=np.float32)
    bet = np.asarray(inputs["L0_ffn_beta"], dtype=np.float32)

    keysT = np.ascontiguousarray(keys.T)
    valuesT = np.ascontiguousarray(values.T)
    w1t = np.ascontiguousarray(
        W1.reshape(E, 8, 4, 128, 8, 128).transpose(0, 1, 5, 2, 4, 3).reshape(E, 8, 128, 4096))
    w2t = np.ascontiguousarray(W2.reshape(E, 8, 128, 32, 128).transpose(0, 1, 4, 3, 2))
    wrT = np.ascontiguousarray(Wr.T.reshape(8, 128, 8).transpose(1, 0, 2))
    b1t = np.ascontiguousarray(b1.reshape(E, 32, 128).transpose(2, 0, 1))
    brf = np.ascontiguousarray(br.reshape(1, 8))
    gamb = np.ascontiguousarray(np.broadcast_to(gam, (128, D)))
    betb = np.ascontiguousarray(np.broadcast_to(bet, (128, D)))
    idf = np.eye(128, dtype=np.float32)
    ones1f = np.ones((1, 128), dtype=np.float32)
    ones1b = np.ones((1, 128), dtype=BF)
    onehot = np.zeros((8, 8 * 128), dtype=BF)
    for e in range(8):
        onehot[e, e * 128:(e + 1) * 128] = 1.0
    rep8 = np.zeros((8, 128), dtype=BF)
    for m in range(128):
        rep8[m % 8, m] = 1.0

    maps = []
    for c in range(8):
        b, half = c // 2, c % 2
        cols = slice(half * 512, (half + 1) * 512)
        m = dict(
            x=x[b],
            xo=np.ascontiguousarray(x[b, half * 4096:(half + 1) * 4096]),
            keysT=keysT, valuesT=valuesT,
            wqT=np.ascontiguousarray(Wq.T[:, cols].reshape(8, 128, 4, 128).transpose(1, 0, 2, 3)),
            wkT=np.ascontiguousarray(Wk.T[:, cols].reshape(8, 128, 4, 128).transpose(1, 0, 2, 3)),
            wvT=np.ascontiguousarray(Wv.T[:, cols].reshape(8, 128, 4, 128).transpose(1, 0, 2, 3)),
            bq=np.ascontiguousarray(bq[cols].reshape(4, 128).T),
            bk=np.ascontiguousarray(bk[cols].reshape(4, 128).T),
            bvr=np.ascontiguousarray(bv[cols].reshape(1, 512)),
            wrT=wrT, br=brf, w1t=w1t, w2t=w2t, b1=b1t, b2=np.ascontiguousarray(b2),
            gam=gamb, bet=betb, idf=idf, ones1f=ones1f, ones1b=ones1b,
            onehot=onehot, rep8=rep8,
        )
        maps.append(m)
    return maps


_DT_EXPECT = dict(x="float32", xo="float32", keysT="float32", valuesT="float32",
                  wqT="float32", wkT="float32", wvT="float32", bq="float32",
                  bk="float32", bvr="float32", wrT="float32", br="float32",
                  w1t="float32", w2t="float32", b1="float32", b2="float32",
                  gam="float32", bet="float32", idf="float32", ones1f="float32",
                  ones1b="bfloat16", onehot="bfloat16", rep8="bfloat16")


def kernel(**inputs):
    maps = _prep_maps(inputs)
    for m in maps:
        for k, v in m.items():
            assert str(v.dtype) == _DT_EXPECT[k], (k, v.dtype)
    nc = build_program()
    res = run_bass_kernel_spmd(nc, maps, core_ids=list(range(8)))
    out = np.empty((4, S, D), dtype=np.float32)
    for c in range(8):
        b, half = c // 2, c % 2
        out[b, half * 4096:(half + 1) * 4096] = np.asarray(res.results[c]["y"])
    if _DBG:
        kernel._last = res
    return out


# version 19
# speedup vs baseline: 1.1067x; 1.0462x over previous
import numpy as np
import ml_dtypes
from contextlib import ExitStack

import concourse.bass as bass
import concourse.mybir as mybir
from concourse.bass_utils import run_bass_kernel_spmd

F32 = mybir.dt.float32
BF16 = mybir.dt.bfloat16
AF = mybir.ActivationFunctionType
ALU = mybir.AluOpType
AX = mybir.AxisListType
BF = ml_dtypes.bfloat16

_DBG = False

S = 8192
D = 1024
TKV = 4096
NGRP = 16
NOG = 8
E = 8
NTT = 8
NQT = 16
NGR = 16


class Ctr:
    def __init__(self, stack, nc, name):
        self.h = stack.enter_context(nc.semaphore(name))
        self.n = 0
    def bump(self, inst, k=1):
        inst.then_inc(self.h, k)
        self.n += k
        return self.n


def build_program():
    nc = bass.Bass("TRN2", target_bir_lowering=False)
    DT = nc.dram_tensor
    sk = "ExternalOutput" if _DBG else "Internal"

    d_x = DT("x", [S, D], F32, kind="ExternalInput").ap()
    d_xo = DT("xo", [4096, D], F32, kind="ExternalInput").ap()
    d_keysT = DT("keysT", [D, TKV], F32, kind="ExternalInput").ap()
    d_valuesT = DT("valuesT", [D, TKV], F32, kind="ExternalInput").ap()
    d_wqT = DT("wqT", [128, 8, 4, 128], F32, kind="ExternalInput").ap()
    d_wkT = DT("wkT", [128, 8, 4, 128], F32, kind="ExternalInput").ap()
    d_wvT = DT("wvT", [128, 8, 4, 128], F32, kind="ExternalInput").ap()
    d_bq = DT("bq", [128, 4], F32, kind="ExternalInput").ap()
    d_bk = DT("bk", [128, 4], F32, kind="ExternalInput").ap()
    d_bvr = DT("bvr", [1, 512], F32, kind="ExternalInput").ap()
    d_wrT = DT("wrT", [128, 8, 8], F32, kind="ExternalInput").ap()
    d_br = DT("br", [1, 8], F32, kind="ExternalInput").ap()
    d_w1t = DT("w1t", [E, 8, 128, 4096], F32, kind="ExternalInput").ap()
    d_w2t = DT("w2t", [E, 8, 128, 32, 128], F32, kind="ExternalInput").ap()
    d_b1 = DT("b1", [128, E, 32], F32, kind="ExternalInput").ap()
    d_b2 = DT("b2", [8, D], F32, kind="ExternalInput").ap()
    d_gam = DT("gam", [128, D], F32, kind="ExternalInput").ap()
    d_bet = DT("bet", [128, D], F32, kind="ExternalInput").ap()
    d_idf = DT("idf", [128, 128], F32, kind="ExternalInput").ap()
    d_ones1f = DT("ones1f", [1, 128], F32, kind="ExternalInput").ap()
    d_ones1b = DT("ones1b", [1, 128], BF16, kind="ExternalInput").ap()
    d_onehot = DT("onehot", [8, 8 * 128], BF16, kind="ExternalInput").ap()
    d_rep8 = DT("rep8", [8, 128], BF16, kind="ExternalInput").ap()

    d_y = DT("y", [4096, D], F32, kind="ExternalOutput").ap()
    d_t2 = DT("t2", [4096, D], F32, kind="Internal").ap()
    d_qts = DT("qts", [4, 128, S], BF16, kind=sk).ap()
    d_xnts = DT("xnts", [128, 8, 4096], BF16, kind=sk).ap()
    d_ss = DT("ss", [4096, D], BF16, kind=sk).ap()
    d_w1b = DT("w1b", [E, 8, 128, 4096], BF16, kind="Internal").ap()
    d_w2b = DT("w2b", [E, 8, 128, 32, 128], BF16, kind="Internal").ap()
    d_bnc = DT("bnc", [128, S], BF16, kind="Internal").ap()
    d_bncr2 = DT("bncr2", [2, S], BF16, kind="Internal").ap()

    with ExitStack() as st0:
        def sb(name, shape, dt):
            return st0.enter_context(nc.sbuf_tensor(name, shape, dt))
        def sem(name):
            return Ctr(st0, nc, name)

        cIdF = sb("cIdF", [128, 128], F32)
        cOn1f = sb("cOn1f", [1, 128], F32)
        cOn1b = sb("cOn1b", [1, 128], BF16)
        cOneHot = sb("cOneHot", [8, 8 * 128], BF16)
        cRep8 = sb("cRep8", [8, 128], BF16)
        cWrT = sb("cWrT", [128, 8 * 8], F32)
        cBr = sb("cBr", [1, 8], F32)
        cBq = sb("cBq", [128, 4], F32)
        cBk = sb("cBk", [128, 4], F32)
        cBvR = sb("cBvR", [1, 512], BF16)
        cB1 = sb("cB1", [128, E * 32], F32)
        cB2 = sb("cB2", [8, D], BF16)
        stRS = sb("stRS", [128, 32], F32)
        stBL = sb("stBL", [128, 32], F32)
        stLG = sb("stLG", [128, 32 * 8], F32)
        cDmy = sb("cDmy", [1, 4], F32)
        st_ab = ExitStack()
        KT = st_ab.enter_context(nc.sbuf_tensor("KT", [128, 4 * TKV], BF16))
        vAll = st_ab.enter_context(nc.sbuf_tensor("vAll", [128, 8 * 32 * 65], BF16))
        cWt = st_ab.enter_context(nc.sbuf_tensor("cWt", [8, 8 * 512], BF16))
        st_w = ExitStack()
        cWq = st_w.enter_context(nc.sbuf_tensor("cWq", [128, 8 * 4 * 128], BF16))
        cWk = st_w.enter_context(nc.sbuf_tensor("cWk", [128, 8 * 4 * 128], BF16))
        cWv = st_w.enter_context(nc.sbuf_tensor("cWv", [128, 8 * 4 * 128], BF16))

        c_const = sem("c_const")
        c_constg = sem("c_constg")
        c_kv = sem("c_kv")
        c_x = [sem("c_x0"), sem("c_x1")]
        c_sp = sem("c_sp")
        c_sq = [sem("c_sq0"), sem("c_sq1")]
        c_qT = [sem("c_qT0"), sem("c_qT1"), sem("c_qT2"), sem("c_qT3")]
        c_xr = sem("c_xr"); c_d8 = sem("c_d8"); c_scr = sem("c_scr")
        c_w1 = sem("c_w1")
        c_w2 = sem("c_w2")
        c_w1l = [sem("c_w1l0"), sem("c_w1l1"), sem("c_w1l2")]
        c_w2l = [sem("c_w2l0"), sem("c_w2l1"), sem("c_w2l2")]
        c_cxn = [sem("c_cxn0"), sem("c_cxn1")]
        c_cs = [sem("c_cs0"), sem("c_cs1")]
        c_cxx = [sem("c_cxx0"), sem("c_cxx1")]
        c_out = [sem("c_out0"), sem("c_out1")]
        k_mm = sem("k_mm"); k_ev = sem("k_ev")
        a_mu = sem("a_mu"); a_sq = sem("a_sq"); a_var = sem("a_var")
        a_rs = sem("a_rs"); a_bl = sem("a_bl"); a_xn = sem("a_xn")
        a_tr = sem("a_tr"); a_ev = sem("a_ev"); a_q = sem("a_q")
        a_qe = sem("a_qe"); a_lg = sem("a_lg"); a_lgs = sem("a_lgs")
        a_bar = sem("a_bar")
        c_bn = sem("c_bn")
        b_sc = sem("b_sc"); b_ex = sem("b_ex"); b_av = sem("b_av")
        b_ae = sem("b_ae"); b_ve = sem("b_ve"); b_dn = sem("b_dn")
        b_rc = sem("b_rc"); b_xn = sem("b_xn"); b_sg = sem("b_sg")
        f_h = sem("f_h"); f_hr = sem("f_hr"); f_hw = sem("f_hw")
        f_o = sem("f_o"); f_oe = sem("f_oe")
        t_wb = sem("t_wb"); f_wb = sem("f_wb")
        r_w = sem("r_w"); r_t = sem("r_t"); r_wt = sem("r_wt")
        s_rex = sem("s_rex"); s_xn2 = sem("s_xn2")
        n_nm = sem("n_nm"); n_var = sem("n_var")
        s_sq = sem("s_sq"); s_rs = sem("s_rs"); s_A = sem("s_A"); s_tanh = sem("s_tanh")
        vf = sem("vf"); sf = sem("sf")
        f_t2 = sem("f_t2"); f_t2a = sem("f_t2a"); f_bl = sem("f_bl")
        f_vf = sem("f_vf")
        f_y = sem("f_y"); f_fin = sem("f_fin")
        c_vo = sem("c_vo"); t_bc = sem("t_bc"); f_t2c = sem("f_t2c")
        c_t2s = sem("c_t2s"); c_t2l = [sem("c_t2l0"), sem("c_t2l1")]
        c_bnA = [sem("c_bnA0"), sem("c_bnA1")]
        c_bnD = [sem("c_bnD0"), sem("c_bnD1")]
        m_vd = sem("m_vd"); m_sd = sem("m_sd")

        blk = st0.enter_context(nc.Block())

        @blk.sync
        def _(sync):
            for dst, src in [(cIdF, d_idf), (cOn1f, d_ones1f), (cOn1b, d_ones1b),
                             (cOneHot, d_onehot), (cRep8, d_rep8), (cBr, d_br),
                             (cBq, d_bq), (cBk, d_bk)]:
                c_const.bump(sync.dma_start(out=dst[:], in_=src), 16)
            c_const.bump(sync.dma_start(
                out=cWrT[:].rearrange("p (a b) -> p a b", a=8), in_=d_wrT), 16)
            c_const.bump(sync.dma_start(
                out=cB1[:].rearrange("p (a b) -> p a b", a=E), in_=d_b1), 16)

        @blk.gpsimd
        def _(g):
            for dst, src in [(cWq, d_wqT), (cWk, d_wkT), (cWv, d_wvT)]:
                for half in range(2):
                    c_constg.bump(g.dma_start(
                        out=dst[:, half * 2048:(half + 1) * 2048]
                            .rearrange("p (a b c) -> p a b c", a=4, b=4),
                        in_=src[:, half * 4:(half + 1) * 4]), 16)
            c_constg.bump(g.dma_start(out=cBvR[:], in_=d_bvr), 16)
            c_constg.bump(g.dma_start(out=cB2[:], in_=d_b2), 16)
        N_CONST = 10

        @blk.vector
        def _(v):
            c_vo.bump(v.memset(
                vAll[:].rearrange("p (h t e) -> p h t e", h=8, t=32)[:, :, :, 64:65], 1.0), 1)

        with ExitStack() as stkv:
            kvK = [stkv.enter_context(nc.sbuf_tensor(f"kvK{i}", [128, 8 * 512], BF16)) for i in range(2)]
            kvV = [stkv.enter_context(nc.sbuf_tensor(f"kvV{i}", [128, 8 * 512], BF16)) for i in range(2)]
            pKV = [stkv.enter_context(nc.psum_tensor(f"pKV{i}", [128, 512], F32)) for i in range(4)]

            @blk.gpsimd
            def _(g):
                for tg in range(8):
                    if tg >= 2:
                        g.wait_ge(k_mm.h, 8 * (tg - 1))
                    g.wait_ge(c_kv.h, c_kv.n)
                    for half in range(2):
                        g.dma_start(
                            out=kvK[tg % 2][:, half * 2048:(half + 1) * 2048]
                                .rearrange("p (a b) -> p a b", a=4),
                            in_=d_keysT[half * 512:(half + 1) * 512, tg * 512:(tg + 1) * 512]
                                .rearrange("(a p) t -> p a t", p=128)
                        ).then_inc(c_kv.h, 16); c_kv.n += 16
                    g.wait_ge(c_kv.h, c_kv.n)
                    for half in range(2):
                        g.dma_start(
                            out=kvV[tg % 2][:, half * 2048:(half + 1) * 2048]
                                .rearrange("p (a b) -> p a b", a=4),
                            in_=d_valuesT[half * 512:(half + 1) * 512, tg * 512:(tg + 1) * 512]
                                .rearrange("(a p) t -> p a t", p=128)
                        ).then_inc(c_kv.h, 16); c_kv.n += 16

            @blk.tensor
            def _(t):
                t.wait_ge(c_const.h, 16 * N_CONST)
                t.wait_ge(c_constg.h, 16 * 8)
                for tg in range(8):
                    t.wait_ge(c_kv.h, 64 * (tg + 1))
                    kb, vb = kvK[tg % 2], kvV[tg % 2]
                    for dkc in range(4):
                        if dkc >= 2:
                            t.wait_ge(k_ev.h, 8 * tg + dkc - 1)
                        elif tg >= 1:
                            t.wait_ge(k_ev.h, 8 * tg + dkc - 5)
                        for dc in range(8):
                            inst = t.matmul(pKV[dkc % 2][:],
                                            cWk[:, (dc * 4 + dkc) * 128:(dc * 4 + dkc + 1) * 128],
                                            kb[:, dc * 512:(dc + 1) * 512],
                                            start=(dc == 0), stop=(dc == 7))
                        k_mm.bump(inst, 1)
                    for tcl in range(4):
                        if tcl >= 2:
                            t.wait_ge(k_ev.h, 8 * tg + tcl + 3)
                        elif tg >= 1:
                            t.wait_ge(k_ev.h, 8 * tg + tcl - 1)
                        for dc in range(8):
                            t.matmul(pKV[2 + tcl % 2][:],
                                     vb[:, dc * 512 + tcl * 128: dc * 512 + (tcl + 1) * 128],
                                     cWv[:, dc * 512:(dc + 1) * 512],
                                     start=(dc == 0), stop=False)
                        inst = t.matmul(pKV[2 + tcl % 2][:], cOn1b[:], cBvR[:],
                                        start=False, stop=True)
                        k_mm.bump(inst, 1)

            @blk.scalar
            def _(s):
                for tg in range(8):
                    for dkc in range(4):
                        s.wait_ge(k_mm.h, 8 * tg + dkc + 1)
                        inst = s.activation(KT[:, dkc * TKV + tg * 512: dkc * TKV + (tg + 1) * 512],
                                            pKV[dkc % 2][:], AF.Relu, bias=cBk[:, dkc:dkc + 1])
                        k_ev.bump(inst, 1)
                    for tcl in range(4):
                        s.wait_ge(k_mm.h, 8 * tg + 4 + tcl + 1)
                        tc = tg * 4 + tcl
                        inst = s.activation(
                            vAll[:].rearrange("p (h m) -> p h m", h=8)[:, :, tc * 65:tc * 65 + 64],
                            pKV[2 + tcl % 2][:].rearrange("p (h d) -> p h d", h=8),
                            AF.Relu)
                        k_ev.bump(inst, 1)

        assert k_mm.n == 64 and k_ev.n == 64


        def phaseA(tag, src_ap, ngrp, own):
            base_bar = a_bar.n
            base_mu, base_sq, base_var = a_mu.n, a_sq.n, a_var.n
            base_rs, base_bl, base_xn = a_rs.n, a_bl.n, a_xn.n
            base_tr, base_ev, base_q = a_tr.n, a_ev.n, a_q.n
            base_qe, base_lg = a_qe.n, a_lg.n
            base_lgs = a_lgs.n
            base_cx = [c_x[0].n, c_x[1].n]
            base_sp = c_sp.n
            base_cq = [c_sq[0].n, c_sq[1].n]
            BAR = 2
            tprog = (a_lg if own else a_q)
            base_tp = tprog.n
            EVB = 8 if own else 4
            EVG = 8 * EVB

            with ExitStack() as sta:
                aX = [sta.enter_context(nc.sbuf_tensor(f"aX{tag}{i}", [128, 4 * 1024], F32)) for i in range(2)]
                aSq = sta.enter_context(nc.sbuf_tensor(f"aSq{tag}", [128, 4 * 1024], F32))
                aXn = sta.enter_context(nc.sbuf_tensor(f"aXn{tag}", [128, 4 * 1024], F32))
                aSm = sta.enter_context(nc.sbuf_tensor(f"aSm{tag}", [128, 4], F32))
                aNM = sta.enter_context(nc.sbuf_tensor(f"aNM{tag}", [128, 4], F32))
                aVar = sta.enter_context(nc.sbuf_tensor(f"aVar{tag}", [128, 4], F32))
                aRS = sta.enter_context(nc.sbuf_tensor(f"aRS{tag}", [128, 4], F32))
                aBL = sta.enter_context(nc.sbuf_tensor(f"aBL{tag}", [128, 4], F32))
                aXnTs = [sta.enter_context(nc.sbuf_tensor(f"aXnT{tag}{i}", [128, 8 * 512], BF16))
                         for i in range(2 if own else 1)]
                aXnT32 = sta.enter_context(nc.sbuf_tensor(f"aXnT32{tag}", [128, 8 * 512], F32)) if own else None
                aQe = [sta.enter_context(nc.sbuf_tensor(f"aQe{tag}{i}", [128, 4 * 512], BF16))
                       for i in range(2)] if not own else None
                pTr = [sta.enter_context(nc.psum_tensor(f"pTr{tag}{i}", [128, 512], F32)) for i in range(2)]
                pQL = [sta.enter_context(nc.psum_tensor(f"pQL{tag}{i}", [128, 512], F32)) for i in range(4)]

                @blk.sync
                def _(sync):
                    for g in range(ngrp):
                        if g == 0:
                            sync.wait_ge(k_ev.h, 64)
                            sync.wait_ge(a_bar.h, base_bar)
                            sync.wait_ge(a_q.h, base_q)
                            sync.wait_ge(c_sq[0].h, base_cq[0])
                            sync.wait_ge(c_sq[1].h, base_cq[1])
                        if g >= 2:
                            sync.wait_ge(a_bar.h, base_bar + BAR * (g - 1))
                            sync.wait_ge(tprog.h, base_tp + 4 * (g - 1))
                        for i in range(4):
                            c_x[g % 2].bump(sync.dma_start(
                                out=aX[g % 2][:, i * 1024:(i + 1) * 1024],
                                in_=src_ap[g * 512 + i * 128: g * 512 + (i + 1) * 128, :]
                            ), 16)

                @blk.vector
                def _(v):
                    for g in range(ngrp):
                        if g >= 1:
                            v.wait_ge(a_bar.h, base_bar + BAR * g)
                            v.wait_ge(tprog.h, base_tp + 4 * g)
                        xb = aX[g % 2]
                        v.wait_ge(c_x[g % 2].h, base_cx[g % 2] + 64 * (g // 2 + 1))
                        for i in range(4):
                            vf.bump(v.reduce_sum(aSm[:, i:i + 1], xb[:, i * 1024:(i + 1) * 1024], axis=AX.X), 1)
                        v.wait_ge(vf.h, vf.n)
                        for i in range(4):
                            inst = v.tensor_scalar(aNM[:, i:i + 1], aSm[:, i:i + 1],
                                                   -1.0 / 1024, None, ALU.mult)
                            a_mu.bump(inst, 1)
                        for i in range(4):
                            v.wait_ge(a_sq.h, base_sq + 4 * g + i + 1)
                            vf.bump(v.reduce_sum(aVar[:, i:i + 1], aSq[:, i * 1024:(i + 1) * 1024], axis=AX.X), 1)
                        v.wait_ge(vf.h, vf.n)
                        for i in range(4):
                            inst = v.tensor_scalar(aVar[:, i:i + 1], aVar[:, i:i + 1],
                                                   1.0 / 1024, 1e-5, ALU.mult, ALU.add)
                            a_var.bump(inst, 1)
                        v.wait_ge(a_mu.h, base_mu + 4 * g + 4)
                        for i in range(4):
                            v.wait_ge(a_rs.h, base_rs + 4 * g + i + 1)
                            inst = v.tensor_tensor(aBL[:, i:i + 1], aNM[:, i:i + 1],
                                                   aRS[:, i:i + 1], ALU.mult)
                            a_bl.bump(inst, 1)
                        if own:
                            v.wait_ge(a_bl.h, base_bl + 4 * g + 4)
                            vf.bump(v.tensor_copy(stRS[:, 4 * g:4 * g + 4], aRS[:]), 1)
                            vf.bump(v.tensor_copy(stBL[:, 4 * g:4 * g + 4], aBL[:]), 1)
                        if own and g >= 2:
                            v.wait_ge(c_sq[g % 2].h, base_cq[g % 2] + 64 * (g // 2))
                        xnt = aXnTs[g % 2] if own else aXnTs[0]
                        for i in range(4):
                            for bank in range(2):
                                v.wait_ge(a_tr.h, base_tr + 8 * g + 2 * i + bank + 1)
                                p = pTr[bank]
                                for dcq in range(4):
                                    dc = bank * 4 + dcq
                                    a_ev.bump(v.tensor_copy(
                                        xnt[:, dc * 512 + i * 128: dc * 512 + (i + 1) * 128],
                                        p[:, dcq * 128:(dcq + 1) * 128]), 1)
                                    if own:
                                        a_ev.bump(v.tensor_copy(
                                            aXnT32[:, dc * 512 + i * 128: dc * 512 + (i + 1) * 128],
                                            p[:, dcq * 128:(dcq + 1) * 128]), 1)
                        if own:
                            for i in range(4):
                                v.wait_ge(a_lg.h, base_lg + 4 * g + i + 1)
                                inst = v.tensor_copy(stLG[:, (4 * g + i) * 8:(4 * g + i + 1) * 8],
                                                     pQL[i][:, 0:8])
                                a_lgs.bump(inst, 1)
                        v.wait_ge(a_ev.h, base_ev + EVG * (g + 1))
                        if own:
                            v.wait_ge(a_lgs.h, base_lgs + 4 * (g + 1))
                        v.wait_ge(vf.h, vf.n)
                        inst = v.engine_nop()
                        a_bar.bump(inst, 1)

                @blk.scalar
                def _(s):
                    for g in range(ngrp):
                        if g >= 1:
                            s.wait_ge(a_bar.h, base_bar + BAR * g)
                            s.wait_ge(tprog.h, base_tp + 4 * g)
                        xb = aX[g % 2]
                        for i in range(4):
                            s.wait_ge(a_mu.h, base_mu + 4 * g + i + 1)
                            inst = s.activation(aSq[:, i * 1024:(i + 1) * 1024],
                                                xb[:, i * 1024:(i + 1) * 1024],
                                                AF.Square, bias=aNM[:, i:i + 1])
                            a_sq.bump(inst, 1)
                        for i in range(4):
                            s.wait_ge(a_var.h, base_var + 4 * g + i + 1)
                            sf.bump(s.activation(aRS[:, i:i + 1], aVar[:, i:i + 1], AF.Ln), 1)
                        s.wait_ge(sf.h, sf.n)
                        for i in range(4):
                            inst = s.activation(aRS[:, i:i + 1], aRS[:, i:i + 1], AF.Exp, scale=-0.5)
                            a_rs.bump(inst, 1)
                        for i in range(4):
                            s.wait_ge(a_bl.h, base_bl + 4 * g + i + 1)
                            inst = s.activation(aXn[:, i * 1024:(i + 1) * 1024],
                                                xb[:, i * 1024:(i + 1) * 1024],
                                                AF.Identity, bias=aBL[:, i:i + 1], scale=aRS[:, i:i + 1])
                            a_xn.bump(inst, 1)
                        if not own:
                            if g >= 2:
                                s.wait_ge(c_sq[g % 2].h, base_cq[g % 2] + 64 * (g // 2))
                            for dqc in range(4):
                                s.wait_ge(a_q.h, base_q + 4 * g + dqc + 1)
                                inst = s.activation(aQe[g % 2][:, dqc * 512:(dqc + 1) * 512], pQL[dqc][:],
                                                    AF.Relu, bias=cBq[:, dqc:dqc + 1])
                                a_qe.bump(inst, 1)
                            s.wait_ge(a_qe.h, base_qe + 4 * (g + 1))
                            for ai in range(4):
                                s.dma_start(
                                    out=d_qts[ai, :, g * 512:(g + 1) * 512],
                                    in_=aQe[g % 2][:, ai * 512:(ai + 1) * 512]
                                ).then_inc(c_sq[g % 2].h, 16); c_sq[g % 2].n += 16
                        else:
                            s.wait_ge(a_ev.h, base_ev + EVG * (g + 1))
                            for j in range(4):
                                s.dma_start(
                                    out=d_xnts[:, 2 * j:2 * j + 2, g * 512:(g + 1) * 512],
                                    in_=aXnTs[g % 2][:, (2 * j) * 512:(2 * j + 2) * 512]
                                        .rearrange("p (a t) -> p a t", a=2)
                                ).then_inc(c_sq[g % 2].h, 16); c_sq[g % 2].n += 16
                        a_bar.bump(s.memzero(cDmy[:]), 1)

                @blk.tensor
                def _(t):
                    for g in range(ngrp):
                        if g == 0:
                            t.wait_ge(k_ev.h, 64)
                            t.wait_ge(a_bar.h, base_bar)
                            t.wait_ge(a_q.h, base_q)
                        if g >= 1:
                            t.wait_ge(a_bar.h, base_bar + BAR * g)
                        for i in range(4):
                            t.wait_ge(a_xn.h, base_xn + 4 * g + i + 1)
                            for bank in range(2):
                                prev = 2 * i + bank - 2
                                if prev >= 0:
                                    t.wait_ge(a_ev.h, base_ev + EVG * g + EVB * (prev + 1))
                                p = pTr[bank]
                                for dcq in range(4):
                                    dc = bank * 4 + dcq
                                    inst = t.transpose(p[:, dcq * 128:(dcq + 1) * 128],
                                                       aXn[:, i * 1024 + dc * 128: i * 1024 + (dc + 1) * 128],
                                                       cIdF[:])
                                a_tr.bump(inst, 1)
                        t.wait_ge(a_ev.h, base_ev + EVG * (g + 1))
                        if not own:
                            for dqc in range(4):
                                for dc in range(8):
                                    inst = t.matmul(pQL[dqc][:],
                                                    cWq[:, (dc * 4 + dqc) * 128:(dc * 4 + dqc + 1) * 128],
                                                    aXnTs[0][:, dc * 512:(dc + 1) * 512],
                                                    start=(dc == 0), stop=(dc == 7))
                                a_q.bump(inst, 1)
                        else:
                            for i in range(4):
                                for dc in range(8):
                                    t.matmul(pQL[i][:, 0:8],
                                             aXnT32[:, dc * 512 + i * 128: dc * 512 + (i + 1) * 128],
                                             cWrT[:, dc * 8:(dc + 1) * 8],
                                             start=(dc == 0), stop=False)
                                inst = t.matmul(pQL[i][:, 0:8], cOn1f[:], cBr[:],
                                                start=False, stop=True)
                                a_lg.bump(inst, 1)

        phaseA("1", d_x, NGRP, own=False)
        phaseA("2", d_xo, NOG, own=True)
        st_w.close()
        A_SP = c_sp.n
        A_SQ = (c_sq[0].n, c_sq[1].n)
        assert A_SP == 0 and A_SQ == (64 * (NGRP + NOG) // 2,) * 2
        assert a_lgs.n == 32 and a_bar.n == 2 * (NGRP + NOG)
        A_BAR = a_bar.n


        with ExitStack() as str_:
            rMx = str_.enter_context(nc.sbuf_tensor("rMx", [128, 32 * 8], F32))
            rNm = str_.enter_context(nc.sbuf_tensor("rNm", [128, 32], F32))
            rE2 = str_.enter_context(nc.sbuf_tensor("rE2", [128, 32], F32))
            rRc = str_.enter_context(nc.sbuf_tensor("rRc", [128, 32], F32))
            rEx = str_.enter_context(nc.sbuf_tensor("rEx", [128, 32 * 8], F32))
            rMk = str_.enter_context(nc.sbuf_tensor("rMk", [128, 32 * 8], F32))
            rW = str_.enter_context(nc.sbuf_tensor("rW", [128, 32 * 8], F32))
            pRt = str_.enter_context(nc.psum_tensor("pRt", [128, 512], F32))

            @blk.scalar
            def _(s):
                for i2 in range(32):
                    s.wait_ge(r_w.h, 32 + i2 + 1)
                    s_rex.bump(s.activation(rE2[:, i2:i2 + 1], rMx[:, i2 * 8 + 1:i2 * 8 + 2],
                                            AF.Exp, bias=rNm[:, i2:i2 + 1]), 1)
                    s_rex.bump(s.activation(rEx[:, i2 * 8:(i2 + 1) * 8],
                                            stLG[:, i2 * 8:(i2 + 1) * 8],
                                            AF.Exp, bias=rNm[:, i2:i2 + 1]), 1)

            @blk.vector
            def _(v):
                v.wait_ge(a_lgs.h, 32)
                v.wait_ge(c_sq[0].h, A_SQ[0])
                v.wait_ge(c_sq[1].h, A_SQ[1])
                for i2 in range(32):
                    r_w.bump(v.max(out=rMx[:, i2 * 8:(i2 + 1) * 8],
                                   in_=stLG[:, i2 * 8:(i2 + 1) * 8]), 1)
                v.wait_ge(r_w.h, 32)
                for i2 in range(32):
                    r_w.bump(v.tensor_scalar(rNm[:, i2:i2 + 1], rMx[:, i2 * 8:i2 * 8 + 1],
                                             -1.0, None, ALU.mult), 1)
                for i2 in range(32):
                    v.wait_ge(s_rex.h, 2 * i2 + 1)
                    vf.bump(v.tensor_scalar(rE2[:, i2:i2 + 1], rE2[:, i2:i2 + 1],
                                            1.0, 1.0, ALU.mult, ALU.add), 1)
                v.wait_ge(vf.h, vf.n)
                for i2 in range(32):
                    vf.bump(v.reciprocal(rRc[:, i2:i2 + 1], rE2[:, i2:i2 + 1]), 1)
                for i2 in range(32):
                    vf.bump(v.tensor_scalar(rMk[:, i2 * 8:(i2 + 1) * 8],
                                            stLG[:, i2 * 8:(i2 + 1) * 8],
                                            rMx[:, i2 * 8 + 1:i2 * 8 + 2], None, ALU.is_ge), 1)
                v.wait_ge(s_rex.h, 64)
                v.wait_ge(vf.h, vf.n)
                for i2 in range(32):
                    vf.bump(v.tensor_scalar(rEx[:, i2 * 8:(i2 + 1) * 8], rEx[:, i2 * 8:(i2 + 1) * 8],
                                            rRc[:, i2:i2 + 1], None, ALU.mult), 1)
                v.wait_ge(vf.h, vf.n)
                for i2 in range(32):
                    r_w.bump(v.tensor_tensor(rW[:, i2 * 8:(i2 + 1) * 8], rEx[:, i2 * 8:(i2 + 1) * 8],
                                             rMk[:, i2 * 8:(i2 + 1) * 8], ALU.mult), 1)
                for tt in range(NTT):
                    v.wait_ge(r_t.h, 4 * (tt + 1))
                    r_wt.bump(v.tensor_copy(cWt[:, tt * 512:(tt + 1) * 512], pRt[0:8, :]), 1)

            @blk.tensor
            def _(t):
                for tt in range(NTT):
                    if tt >= 1:
                        t.wait_ge(r_wt.h, tt)
                    for i in range(4):
                        i2 = tt * 4 + i
                        t.wait_ge(r_w.h, 64 + i2 + 1)
                        inst = t.transpose(pRt[0:8, i * 128:(i + 1) * 128],
                                           rW[:, i2 * 8:(i2 + 1) * 8], cIdF[:])
                        r_t.bump(inst, 1)
        assert r_w.n == 96 and r_t.n == 32 and r_wt.n == 8 and s_rex.n == 64

        chains = []
        for tt in range(NTT):
            chains.append(("cxn", tt))
            if tt >= 1:
                for i in range(4):
                    for half in range(2):
                        chains.append(("tr", (tt - 1) * 4 + i, half, 4))
            for e in range(E):
                ge = tt * 8 + e
                chains.append(("bc", ge, 0))
                for fc in range(32):
                    chains.append(("hc", ge, fc, 8))
                for dc in range(8):
                    chains.append(("oc", ge, dc, 32))
        for i in range(4):
            for half in range(2):
                chains.append(("tr", 28 + i, half, 4))
        chains_tt = []
        cur_tt = 0
        for ch in chains:
            if ch[0] == "cxn":
                cur_tt = ch[1]
            chains_tt.append((cur_tt, ch))
        PREFIX = 0
        cmm_tt = [0] * NTT
        for tt, ch in chains_tt:
            if ch[0] in ("tr", "hc", "oc"):
                cmm_tt[tt] += ch[-1]
        w_tt = [1.0] * 8
        rem = 4096 - PREFIX
        denom = sum(w_tt[t] * cmm_tt[t] for t in range(NTT))
        rate_tt = [rem * w_tt[t] / denom for t in range(NTT)]
        EV = []
        pend = []
        bu = 0

        def push(ev):
            EV.append(ev)
            for p in pend:
                p[0] -= 1
            ready = [p for p in pend if p[0] <= 0]
            pend[:] = [p for p in pend if p[0] > 0]
            for p in ready:
                EV.append(p[1])

        def emit_unit(k, u):
            if u == 0 and k + 2 < 128:
                push(("bq", k + 2))
            push(("bu", k, u))
            if u == 31:
                push(("bev", k))
                pend.append([2, ("bnc", k)])
                if k % 16 == 15:
                    h = k // 16
                    pend.extend([[3, ("d8", h)], [5, ("pdb", h, 0)], [6, ("rcp", h, 0)],
                                 [7, ("scr", h)], [8, ("nrm", h)], [9, ("pdb", h, 1)],
                                 [10, ("rcp", h, 1)], [13, ("sadd", h)], [23, ("sst", h)]])

        push(("bq", 0))
        push(("bq", 1))
        while bu < PREFIX:
            emit_unit(bu // 32, bu % 32)
            bu += 1
        pend.sort(key=lambda p: p[0])
        for p in pend:
            EV.append(p[1])
        pend.clear()
        acc = 0.0
        for tt, ch in chains_tt:
            push(ch)
            if ch[0] in ("tr", "hc", "oc"):
                acc += ch[-1] * rate_tt[tt]
            while acc >= 1.0 and bu < 4096:
                acc -= 1.0
                emit_unit(bu // 32, bu % 32)
                bu += 1
        while bu < 4096:
            emit_unit(bu // 32, bu % 32)
            bu += 1
        pend.sort(key=lambda p: p[0])
        for p in pend:
            EV.append(p[1])

        with ExitStack() as stm:
            bQt = stm.enter_context(nc.sbuf_tensor("bQt", [128, 4 * 512], BF16))
            bEx = stm.enter_context(nc.sbuf_tensor("bEx", [128, 4 * 512], BF16))
            bXr = stm.enter_context(nc.sbuf_tensor("bXr", [128, 4 * 1024], BF16))
            bScr = stm.enter_context(nc.sbuf_tensor("bScr", [128, 4 * 1024], BF16))
            bDiv = stm.enter_context(nc.sbuf_tensor("bDiv", [128, 1024], BF16))
            bAssy = stm.enter_context(nc.sbuf_tensor("bAssy", [65, 2 * 512], BF16))
            bD8 = stm.enter_context(nc.sbuf_tensor("bD8", [8, 1024], BF16))
            cXnT = stm.enter_context(nc.sbuf_tensor("cXnT", [128, 8 * 512], BF16))
            cW1 = [stm.enter_context(nc.sbuf_tensor(f"cW1_{i}", [128, 4 * 8 * 128], BF16)) for i in range(2)]
            cW2 = [stm.enter_context(nc.sbuf_tensor(f"cW2_{i}", [128, 32 * 128], BF16)) for i in range(2)]
            cHW = stm.enter_context(nc.sbuf_tensor("cHW", [128, 32 * 512], BF16))
            cHR = stm.enter_context(nc.sbuf_tensor("cHR", [128, 2 * 512], BF16))
            cWbc = stm.enter_context(nc.sbuf_tensor("cWbc", [128, 2 * 512], BF16))
            cAcc = stm.enter_context(nc.sbuf_tensor("cAcc", [128, 8 * 512], F32))
            sT2 = stm.enter_context(nc.sbuf_tensor("sT2", [128, 1024], F32))
            pH = [stm.enter_context(nc.psum_tensor(f"pH{i}", [128, 512], F32)) for i in range(2)]
            pO = stm.enter_context(nc.psum_tensor("pO", [128, 512], F32))
            pT2m = stm.enter_context(nc.psum_tensor("pT2m", [128, 512], F32))
            pSc = [stm.enter_context(nc.psum_tensor(f"pSc{i}", [128, 512], F32)) for i in range(2)]
            pAv = [stm.enter_context(nc.psum_tensor(f"pAv{i}", [128, 512], F32)) for i in range(2)]

            @blk.gpsimd
            def _(g):
                def cast_pair(e):
                    g.wait_ge(c_w1.h, c_w1.n)
                    g.dma_start(out=d_w1b[e], in_=d_w1t[e]).then_inc(c_w1.h, 16)
                    c_w1.n += 16
                    g.wait_ge(c_w2.h, c_w2.n)
                    g.dma_start(out=d_w2b[e], in_=d_w2t[e]).then_inc(c_w2.h, 16)
                    c_w2.n += 16

                def issue_xr(h):
                    if h >= 1:
                        g.wait_ge(b_sg.h, 4 * h)
                    g.wait_ge(c_xr.h, 16 * h)
                    c_xr.bump(g.dma_start(
                        out=bXr[:].rearrange("p (i c) -> p i c", i=4),
                        in_=d_xo[h * 512:(h + 1) * 512, :].rearrange("(i p) c -> p i c", p=128)), 16)

                for e in range(E):
                    cast_pair(e)
                g.wait_ge(a_bar.h, A_BAR)
                g.wait_ge(c_sq[0].h, A_SQ[0])
                g.wait_ge(c_sq[1].h, A_SQ[1])
                g.wait_ge(r_wt.h, 8)
                for h in range(8):
                    issue_xr(h)

            @blk.sync
            def _(sync):
                sync.wait_ge(a_bar.h, A_BAR)
                sync.wait_ge(c_sq[0].h, A_SQ[0])
                sync.wait_ge(c_sq[1].h, A_SQ[1])
                sync.wait_ge(r_wt.h, 8)
                next_w1 = [0]
                next_w2 = [0]

                def issue_w1():
                    ug = next_w1[0]
                    e_mod = (ug // 8) % 8
                    sync.wait_ge(c_w1.h, 16 * (e_mod + 1))
                    if ug >= 2:
                        sync.wait_ge(f_h.h, 4 * ug - 4)
                    c_w1l[ug % 2].bump(sync.dma_start(
                        out=cW1[ug % 2][:], in_=d_w1b[(ug // 8) % 8, ug % 8]), 16)
                    next_w1[0] += 1

                def issue_w2():
                    vg = next_w2[0]
                    e_mod = (vg // 8) % 8
                    sync.wait_ge(c_w2.h, 16 * (e_mod + 1))
                    if vg >= 1:
                        sync.wait_ge(f_o.h, vg - 1)
                    c_w2l[vg % 2].bump(sync.dma_start(
                        out=cW2[vg % 2][:],
                        in_=d_w2b[(vg // 8) % 8, vg % 8].rearrange("p a q -> p (a q)")), 16)
                    next_w2[0] += 1

                for ev in EV:
                    kind = ev[0]
                    if kind == "cxn":
                        tt = ev[1]
                        if tt >= 1:
                            sync.wait_ge(f_h.h, 256 * tt)
                        c_cxn[0].bump(sync.dma_start(
                            out=cXnT[:].rearrange("p (a t) -> p a t", a=8),
                            in_=d_xnts[:, :, tt * 512:(tt + 1) * 512]), 16)
                    elif kind == "hc":
                        ge, fc = ev[1], ev[2]
                        if fc % 4 == 0:
                            ug_cur = ge * 8 + fc // 4
                            while next_w1[0] <= min(ug_cur + 2, 511):
                                issue_w1()
                    elif kind == "oc":
                        ge, dc = ev[1], ev[2]
                        vg_cur = ge * 8 + dc
                        while next_w2[0] <= min(vg_cur + 2, 511):
                            issue_w2()
                    elif kind == "tr":
                        gi, half = ev[1], ev[2]
                        if half == 1:
                            sync.wait_ge(f_t2c.h, 2 * gi + 2)
                            c_t2s.bump(sync.dma_start(
                                out=d_t2[gi * 128:(gi + 1) * 128, :], in_=sT2[:]), 16)
                    elif kind == "bq":
                        k = ev[1]
                        hp2, a2, qt2 = k // 32, (k // 16) % 2, k % 16
                        if k >= 4:
                            sync.wait_ge(b_sc.h, 32 * (k - 3))
                        c_qT[k % 4].bump(sync.dma_start(
                            out=bQt[64 * a2:64 * a2 + 64, (k % 4) * 512:(k % 4 + 1) * 512],
                            in_=d_qts[hp2, 64 * a2:64 * a2 + 64, qt2 * 512:(qt2 + 1) * 512]), 16)

            @blk.tensor
            def _(t):
                for ev in EV:
                    kind = ev[0]
                    if kind == "bc":
                        ge = ev[1]
                        tt, e = ge // 8, ge % 8
                        t.wait_ge(r_wt.h, 8)
                        if ge >= 1:
                            t.wait_ge(f_wb.h, ge)
                        if e == 0 and tt >= 1:
                            t.wait_ge(f_t2c.h, 8 * tt)
                        t_bc.bump(t.matmul(pT2m[:], cOneHot[:, e * 128:(e + 1) * 128],
                                           cWt[:, tt * 512:(tt + 1) * 512],
                                           start=True, stop=True), 1)
                    elif kind == "hc":
                        ge, fc = ev[1], ev[2]
                        tt, e = ge // 8, ge % 8
                        gf = ge * 32 + fc
                        ug = ge * 8 + fc // 4
                        if e == 0 and fc == 0:
                            t.wait_ge(c_cxn[0].h, 16 * (tt + 1))
                        if fc % 4 == 0:
                            t.wait_ge(c_w1l[ug % 2].h, 16 * (ug // 2 + 1))
                        if gf >= 2:
                            t.wait_ge(f_hr.h, gf - 1)
                        for dc in range(8):
                            inst = t.matmul(pH[fc % 2][:],
                                            cW1[ug % 2][:, ((fc % 4) * 8 + dc) * 128:
                                                          ((fc % 4) * 8 + dc + 1) * 128],
                                            cXnT[:, dc * 512:(dc + 1) * 512],
                                            start=(dc == 0), stop=(dc == 7))
                        f_h.bump(inst, 1)
                    elif kind == "oc":
                        ge, dc = ev[1], ev[2]
                        tt, e = ge // 8, ge % 8
                        vg = ge * 8 + dc
                        t.wait_ge(c_w2l[vg % 2].h, 16 * (vg // 2 + 1))
                        if dc == 0:
                            t.wait_ge(f_hw.h, 32 * (ge + 1))
                        if vg >= 1:
                            t.wait_ge(f_oe.h, vg)
                        first = True
                        if e == 0:
                            t.matmul(pO[:], cB2[:, dc * 128:(dc + 1) * 128],
                                     cWt[:, tt * 512:(tt + 1) * 512],
                                     start=True, stop=False)
                            first = False
                        for fc in range(32):
                            inst = t.matmul(pO[:],
                                            cW2[vg % 2][:, fc * 128:(fc + 1) * 128],
                                            cHW[:, fc * 512:(fc + 1) * 512],
                                            start=(first and fc == 0), stop=(fc == 31))
                        f_o.bump(inst, 1)
                    elif kind == "tr":
                        gi, half = ev[1], ev[2]
                        tt, i = gi // 4, gi % 4
                        tridx = gi * 2 + half
                        if i == 0 and half == 0:
                            t.wait_ge(f_oe.h, 64 * (tt + 1))
                            t.wait_ge(f_wb.h, 8 * (tt + 1))
                        t.wait_ge(f_t2c.h, tridx)
                        for dcq in range(4):
                            dc = half * 4 + dcq
                            inst = t.transpose(pT2m[:, dcq * 128:(dcq + 1) * 128],
                                               cAcc[:, dc * 512 + i * 128: dc * 512 + (i + 1) * 128],
                                               cIdF[:])
                        f_t2.bump(inst, 1)
                    elif kind == "bu":
                        k, u = ev[1], ev[2]
                        h = k // 16
                        hp2, a2, qt2 = h // 2, h % 2, k % 16
                        gu = k * 32 + u
                        if gu == 0:
                            t.wait_ge(r_wt.h, 8)
                            t.wait_ge(c_vo.h, 1)
                        if gu >= 2:
                            t.wait_ge(b_ex.h, gu - 1)
                        if u == 0:
                            t.wait_ge(c_qT[k % 4].h, 16 * (k // 4 + 1))
                            if qt2 >= 1:
                                t.wait_ge(b_ae.h, k)
                            elif h >= 1:
                                t.wait_ge(b_ae.h, 16 * (h - 1) if h >= 2 else 0)
                                if h >= 2:
                                    t.wait_ge(b_rc.h, 2 * (h - 1))
                        b_sc.bump(t.matmul(
                            pSc[u % 2][:],
                            KT[64 * a2:64 * a2 + 64,
                               hp2 * TKV + u * 128: hp2 * TKV + (u + 1) * 128],
                            bQt[64 * a2:64 * a2 + 64, (k % 4) * 512:(k % 4 + 1) * 512],
                            start=True, stop=True), 1)

                        def emit_av(um, stop):
                            gum = k * 32 + um
                            inst = t.matmul(
                                pAv[a2][0:65, :],
                                vAll[:, (h * 32 + um) * 65:(h * 32 + um + 1) * 65],
                                bEx[:, (gum % 4) * 512:(gum % 4 + 1) * 512],
                                start=(um == 0), stop=stop)
                            b_av.bump(inst, 1)
                        if u >= 2:
                            emit_av(u - 2, False)
                        if u == 31:
                            for um in (30, 31):
                                t.wait_ge(b_ex.h, k * 32 + um + 1)
                                emit_av(um, um == 31)
                    elif kind == "pdb":
                        h, half = ev[1], ev[2]
                        if half == 0:
                            t.wait_ge(c_d8.h, 16 * (h + 1))
                            t.wait_ge(b_ae.h, 16 * (h + 1))
                        else:
                            t.wait_ge(b_rc.h, 2 * h + 1)
                        b_dn.bump(t.matmul(pAv[h % 2][:], cRep8[:],
                                           bD8[:, half * 512:(half + 1) * 512],
                                           start=True, stop=True), 1)

            @blk.scalar
            def _(s):
                for ev in EV:
                    kind = ev[0]
                    if kind == "hc":
                        ge, fc = ev[1], ev[2]
                        e = ge % 8
                        gf = ge * 32 + fc
                        s.wait_ge(f_h.h, gf + 1)
                        if gf >= 2:
                            s.wait_ge(f_hw.h, gf - 1)
                        f_hr.bump(s.activation(cHR[:, (fc % 2) * 512:(fc % 2 + 1) * 512],
                                               pH[fc % 2][:], AF.Relu,
                                               bias=cB1[:, e * 32 + fc:e * 32 + fc + 1]), 1)
                    elif kind == "bu":
                        k, u = ev[1], ev[2]
                        gu = k * 32 + u
                        s.wait_ge(b_sc.h, gu + 1)
                        if gu >= 4:
                            s.wait_ge(b_av.h, gu - 3)
                        b_ex.bump(s.activation(bEx[:, (gu % 4) * 512:(gu % 4 + 1) * 512],
                                               pSc[u % 2][:], AF.Exp, scale=0.125), 1)
                    elif kind == "d8":
                        h = ev[1]
                        s.wait_ge(c_bnD[0].h, 128 * (h + 1))
                        s.wait_ge(c_bnD[1].h, 128 * (h + 1))
                        if h >= 1:
                            s.wait_ge(b_dn.h, 2 * h)
                        c_d8.bump(s.dma_start(
                            out=bD8[:], in_=d_bncr2[h % 2].rearrange("(j c) -> j c", j=8)), 16)
                    elif kind == "scr":
                        h = ev[1]
                        s.wait_ge(c_bnA[0].h, 128 * (h + 1))
                        s.wait_ge(c_bnA[1].h, 128 * (h + 1))
                        if h >= 1:
                            s.wait_ge(b_sg.h, 4 * h)
                            s.wait_ge(c_sp.h, A_SP + 16 * h)
                        c_scr.bump(s.dma_start(
                            out=bScr[:].rearrange("p (i c) -> p i c", i=4),
                            in_=d_bnc[64 * (h % 2):64 * (h % 2) + 64, :]
                                .rearrange("d (j c) -> (d j) c", j=8)
                                .rearrange("(i p) c -> p i c", p=128)), 16)
                    elif kind == "nrm":
                        h = ev[1]
                        s.wait_ge(c_xr.h, 16 * (h + 1))
                        for i in range(4):
                            ti = h * 4 + i
                            b_xn.bump(s.activation(bXr[:, i * 1024:(i + 1) * 1024],
                                                   bXr[:, i * 1024:(i + 1) * 1024],
                                                   AF.Identity, bias=stBL[:, ti:ti + 1],
                                                   scale=stRS[:, ti:ti + 1]), 1)
                    elif kind == "bnc":
                        k = ev[1]
                        h, qt2 = k // 16, k % 16
                        s.wait_ge(b_ae.h, k + 1)
                        if h >= 2:
                            s.wait_ge(c_d8.h, 16 * (h - 1))
                            s.wait_ge(c_scr.h, 16 * (h - 1))
                        c_bnA[k % 2].bump(s.dma_start(
                            out=d_bnc[64 * (h % 2):64 * (h % 2) + 64, qt2 * 512:(qt2 + 1) * 512],
                            in_=bAssy[0:64, (k % 2) * 512:(k % 2 + 1) * 512]), 16)
                        c_bnD[k % 2].bump(s.dma_start(
                            out=d_bncr2[h % 2:h % 2 + 1, qt2 * 512:(qt2 + 1) * 512],
                            in_=bAssy[64:65, (k % 2) * 512:(k % 2 + 1) * 512]), 16)
                    elif kind == "sst":
                        h = ev[1]
                        s.wait_ge(b_sg.h, 4 * (h + 1))
                        s.wait_ge(c_sp.h, A_SP + 16 * h)
                        s.dma_start(out=d_ss[h * 512:(h + 1) * 512, :].rearrange("(i p) c -> p i c", p=128),
                                    in_=bScr[:].rearrange("p (i c) -> p i c", i=4)).then_inc(c_sp.h, 16)
                        c_sp.n += 16
                s.wait_ge(b_ex.h, 4096)
                s.wait_ge(f_hr.h, 2048)
                s.wait_ge(b_xn.h, 32)
                s.wait_ge(c_sp.h, A_SP + 16 * 8)
                s.wait_ge(c_bnA[0].h, 16 * 64)
                s.wait_ge(c_bnA[1].h, 16 * 64)
                s.wait_ge(c_bnD[0].h, 16 * 64)
                s.wait_ge(c_bnD[1].h, 16 * 64)
                m_sd.bump(s.memzero(cDmy[:]), 1)

            @blk.vector
            def _(v):
                for ev in EV:
                    kind = ev[0]
                    if kind == "bc":
                        ge = ev[1]
                        v.wait_ge(t_bc.h, ge + 1)
                        if ge >= 2:
                            v.wait_ge(f_hw.h, 32 * (ge - 1))
                        f_wb.bump(v.tensor_copy(cWbc[:, (ge % 2) * 512:(ge % 2 + 1) * 512],
                                                pT2m[:]), 1)
                    elif kind == "hc":
                        ge, fc = ev[1], ev[2]
                        gf = ge * 32 + fc
                        v.wait_ge(f_hr.h, gf + 1)
                        if fc == 0:
                            v.wait_ge(f_wb.h, ge + 1)
                            if ge >= 1:
                                v.wait_ge(f_o.h, ge * 8)
                        f_hw.bump(v.tensor_tensor(cHW[:, fc * 512:(fc + 1) * 512],
                                                  cHR[:, (fc % 2) * 512:(fc % 2 + 1) * 512],
                                                  cWbc[:, (ge % 2) * 512:(ge % 2 + 1) * 512],
                                                  ALU.mult), 1)
                    elif kind == "oc":
                        ge, dc = ev[1], ev[2]
                        tt, e = ge // 8, ge % 8
                        vg = ge * 8 + dc
                        v.wait_ge(f_o.h, vg + 1)
                        if e == 0:
                            if dc == 0 and tt >= 1:
                                v.wait_ge(f_t2.h, 8 * tt)
                            inst = v.tensor_copy(cAcc[:, dc * 512:(dc + 1) * 512], pO[:])
                        else:
                            inst = v.tensor_tensor(cAcc[:, dc * 512:(dc + 1) * 512],
                                                   pO[:], cAcc[:, dc * 512:(dc + 1) * 512],
                                                   ALU.add)
                        f_oe.bump(inst, 1)
                    elif kind == "tr":
                        gi, half = ev[1], ev[2]
                        tridx = gi * 2 + half
                        v.wait_ge(f_t2.h, tridx + 1)
                        if gi >= 1 and half == 0:
                            v.wait_ge(c_t2s.h, 16 * gi)
                        f_t2c.bump(v.tensor_copy(sT2[:, half * 512:(half + 1) * 512],
                                                 pT2m[:]), 1)
                    elif kind == "bev":
                        k = ev[1]
                        a2 = (k // 16) % 2
                        v.wait_ge(b_av.h, 32 * (k + 1))
                        if k >= 2:
                            v.wait_ge(c_bnA[k % 2].h, 16 * (k // 2))
                            v.wait_ge(c_bnD[k % 2].h, 16 * (k // 2))
                        b_ae.bump(v.tensor_copy(bAssy[0:65, (k % 2) * 512:(k % 2 + 1) * 512],
                                                pAv[a2][0:65, :]), 1)
                    elif kind == "rcp":
                        h, half = ev[1], ev[2]
                        v.wait_ge(b_dn.h, 2 * h + half + 1)
                        with nc.allow_low_precision(reason="1/den in bf16 is within tolerance"):
                            b_rc.bump(v.reciprocal(bDiv[:, half * 512:(half + 1) * 512],
                                                   pAv[h % 2][:]), 1)
                    elif kind == "sadd":
                        h = ev[1]
                        v.wait_ge(b_rc.h, 2 * (h + 1))
                        v.wait_ge(c_scr.h, 16 * (h + 1))
                        for i in range(4):
                            vf.bump(v.tensor_tensor(bScr[:, i * 1024:(i + 1) * 1024],
                                                    bScr[:, i * 1024:(i + 1) * 1024],
                                                    bDiv[:], ALU.mult), 1)
                        v.wait_ge(vf.h, vf.n)
                        for i in range(4):
                            v.wait_ge(b_xn.h, 4 * h + i + 1)
                            b_sg.bump(v.tensor_tensor(bScr[:, i * 1024:(i + 1) * 1024],
                                                      bScr[:, i * 1024:(i + 1) * 1024],
                                                      bXr[:, i * 1024:(i + 1) * 1024],
                                                      ALU.add), 1)
                v.wait_ge(f_t2c.h, 64)
                v.wait_ge(b_sg.h, 32)
                v.wait_ge(f_oe.h, 512)
                v.wait_ge(b_ae.h, 128)
                v.wait_ge(vf.h, vf.n)
                m_vd.bump(v.engine_nop(), 1)

        assert b_sc.n == 4096 and b_ex.n == 4096 and b_av.n == 4096
        assert b_ae.n == 128 and c_bnA[0].n == 1024 and c_bnD[1].n == 1024
        assert b_dn.n == 16 and b_rc.n == 16 and b_xn.n == 32 and b_sg.n == 32
        assert f_h.n == 2048 and f_hr.n == 2048 and f_hw.n == 2048
        assert f_o.n == 512 and f_oe.n == 512
        assert t_bc.n == 64 and f_wb.n == 64 and f_t2.n == 64 and f_t2c.n == 64
        assert c_t2s.n == 32 * 16 and c_sp.n == A_SP + 128
        st_w2 = c_w1.n

        st_ab.close()

        with ExitStack() as stf:
            cGamF = stf.enter_context(nc.sbuf_tensor("cGamF", [128, D], F32))
            cBetF = stf.enter_context(nc.sbuf_tensor("cBetF", [128, D], F32))
            cS = stf.enter_context(nc.sbuf_tensor("cS", [128, 4 * 1024], BF16))
            cX = stf.enter_context(nc.sbuf_tensor("cX", [128, 4 * 1024], F32))
            cXn2 = stf.enter_context(nc.sbuf_tensor("cXn2", [128, 4 * 1024], F32))
            cT2 = stf.enter_context(nc.sbuf_tensor("cT2", [128, 4 * 1024], F32))
            cTh = stf.enter_context(nc.sbuf_tensor("cTh", [128, 4 * 1024], F32))
            cOut = stf.enter_context(nc.sbuf_tensor("cOut", [128, 4 * 1024], F32))
            nSm = stf.enter_context(nc.sbuf_tensor("nSm", [128, 4], F32))
            nNm = stf.enter_context(nc.sbuf_tensor("nNm", [128, 4], F32))
            nV2 = stf.enter_context(nc.sbuf_tensor("nV2", [128, 4], F32))
            nVar = stf.enter_context(nc.sbuf_tensor("nVar", [128, 4], F32))
            nRS = stf.enter_context(nc.sbuf_tensor("nRS", [128, 4], F32))
            nBL = stf.enter_context(nc.sbuf_tensor("nBL", [128, 4], F32))
            c_gb = sem("c_gb")

            @blk.sync
            def _(sync):
                sync.wait_ge(m_vd.h, 1)
                sync.wait_ge(m_sd.h, 1)
                sync.wait_ge(f_t2.h, 64)
                c_gb.bump(sync.dma_start(out=cGamF[:], in_=d_gam), 16)
                c_gb.bump(sync.dma_start(out=cBetF[:], in_=d_bet), 16)
                for gs in range(32):
                    sl = gs % 4
                    sync.wait_ge(c_t2s.h, 16 * (gs + 1))
                    if gs >= 4:
                        sync.wait_ge(f_fin.h, gs - 3)
                    sync.wait_ge(c_t2l[gs % 2].h, 16 * (gs // 2))
                    c_t2l[gs % 2].bump(sync.dma_start(
                        out=cT2[:, sl * 1024:(sl + 1) * 1024],
                        in_=d_t2[gs * 128:(gs + 1) * 128, :]), 16)
                sync.wait_ge(c_out[0].h, 16 * 16)
                sync.wait_ge(c_out[1].h, 16 * 16)

            @blk.gpsimd
            def _(g):
                g.wait_ge(m_vd.h, 1)
                g.wait_ge(m_sd.h, 1)
                g.wait_ge(f_t2.h, 64)
                for gs in range(32):
                    sl = gs % 4
                    tt = gs // 4
                    if gs >= 4:
                        g.wait_ge(f_fin.h, gs - 3)
                    g.wait_ge(c_sp.h, A_SP + 16 * (tt + 1))
                    g.wait_ge(c_cs[gs % 2].h, 16 * (gs // 2))
                    c_cs[gs % 2].bump(g.dma_start(
                        out=cS[:, sl * 1024:(sl + 1) * 1024],
                        in_=d_ss[gs * 128:(gs + 1) * 128, :]), 16)
                    g.wait_ge(c_cxx[gs % 2].h, 16 * (gs // 2))
                    c_cxx[gs % 2].bump(g.dma_start(
                        out=cX[:, sl * 1024:(sl + 1) * 1024],
                        in_=d_xo[gs * 128:(gs + 1) * 128, :]), 16)

            @blk.scalar
            def _(s):
                ln_cnt = {}

                def S(stage, gs):
                    sl = gs % 4
                    lo, hi = sl * 1024, (sl + 1) * 1024
                    if stage == 0:
                        s.wait_ge(c_cxx[gs % 2].h, 16 * (gs // 2 + 1))
                        if gs >= 4:
                            s.wait_ge(f_t2a.h, gs - 3)
                        s_xn2.bump(s.activation(cXn2[:, lo:hi], cX[:, lo:hi],
                                                AF.Identity, bias=stBL[:, gs:gs + 1],
                                                scale=stRS[:, gs:gs + 1]), 1)
                    elif stage == 1:
                        s.wait_ge(f_t2a.h, gs + 1)
                        s.wait_ge(n_nm.h, gs + 1)
                        if gs >= 4:
                            s.wait_ge(f_fin.h, gs - 3)
                        s_sq.bump(s.activation(cTh[:, lo:hi], cT2[:, lo:hi],
                                               AF.Square, bias=nNm[:, sl:sl + 1]), 1)
                    elif stage == 2:
                        s.wait_ge(n_var.h, gs + 1)
                        ln_cnt[gs] = sf.bump(s.activation(nRS[:, sl:sl + 1],
                                                          nVar[:, sl:sl + 1], AF.Ln), 1)
                    elif stage == 3:
                        s.wait_ge(sf.h, ln_cnt[gs])
                        s_rs.bump(s.activation(nRS[:, sl:sl + 1], nRS[:, sl:sl + 1],
                                               AF.Exp, scale=-0.5), 1)
                    elif stage == 4:
                        s.wait_ge(f_bl.h, gs + 1)
                        s_A.bump(s.activation(cT2[:, lo:hi], cT2[:, lo:hi],
                                              AF.Identity, bias=nBL[:, sl:sl + 1],
                                              scale=nRS[:, sl:sl + 1]), 1)
                    elif stage == 5:
                        s.wait_ge(f_y.h, gs + 1)
                        s_tanh.bump(s.activation(cTh[:, lo:hi], cT2[:, lo:hi],
                                                 AF.Tanh, scale=0.5), 1)
                    elif stage == 6:
                        s.wait_ge(f_fin.h, gs + 1)
                        s.wait_ge(c_out[gs % 2].h, 16 * (gs // 2))
                        c_out[gs % 2].bump(s.dma_start(
                            out=d_y[gs * 128:(gs + 1) * 128, :],
                            in_=cOut[:, lo:hi]), 16)

                for p in range(16):
                    for stage in range(7):
                        for j in range(2):
                            S(stage, 2 * p + j)

            @blk.vector
            def _(v):
                vfc = {}

                def V(stage, gs):
                    sl = gs % 4
                    lo, hi = sl * 1024, (sl + 1) * 1024
                    if stage == 0:
                        v.wait_ge(c_t2l[gs % 2].h, 16 * (gs // 2 + 1))
                        v.wait_ge(s_xn2.h, gs + 1)
                        v.wait_ge(c_cs[gs % 2].h, 16 * (gs // 2 + 1))
                        v.wait_ge(c_cxx[gs % 2].h, 16 * (gs // 2 + 1))
                        if gs >= 4:
                            v.wait_ge(s_tanh.h, gs - 3)
                        f_t2a.bump(v.tensor_tensor(cT2[:, lo:hi], cT2[:, lo:hi],
                                                   cXn2[:, lo:hi], ALU.add), 1)
                    elif stage == 1:
                        v.wait_ge(f_t2a.h, gs + 1)
                        vfc[(gs, 0)] = vf.bump(v.reduce_sum(nSm[:, sl:sl + 1],
                                                            cT2[:, lo:hi], axis=AX.X), 1)
                    elif stage == 2:
                        v.wait_ge(vf.h, vfc[(gs, 0)])
                        n_nm.bump(v.tensor_scalar(nNm[:, sl:sl + 1], nSm[:, sl:sl + 1],
                                                  -1.0 / 1024, None, ALU.mult), 1)
                    elif stage == 3:
                        v.wait_ge(s_sq.h, gs + 1)
                        vfc[(gs, 1)] = vf.bump(v.reduce_sum(nV2[:, sl:sl + 1],
                                                            cTh[:, lo:hi], axis=AX.X), 1)
                    elif stage == 4:
                        v.wait_ge(vf.h, vfc[(gs, 1)])
                        n_var.bump(v.tensor_scalar(nVar[:, sl:sl + 1], nV2[:, sl:sl + 1],
                                                   1.0 / 1024, 1e-5, ALU.mult, ALU.add), 1)
                    elif stage == 5:
                        v.wait_ge(s_rs.h, gs + 1)
                        v.wait_ge(n_nm.h, gs + 1)
                        f_bl.bump(v.tensor_tensor(nBL[:, sl:sl + 1], nNm[:, sl:sl + 1],
                                                  nRS[:, sl:sl + 1], ALU.mult), 1)
                    elif stage == 6:
                        v.wait_ge(s_A.h, gs + 1)
                        vfc[(gs, 2)] = vf.bump(v.tensor_tensor(cT2[:, lo:hi], cT2[:, lo:hi],
                                                               cGamF[:], ALU.mult), 1)
                    elif stage == 7:
                        v.wait_ge(vf.h, vfc[(gs, 2)])
                        f_y.bump(v.tensor_tensor(cT2[:, lo:hi], cT2[:, lo:hi],
                                                 cBetF[:], ALU.add), 1)
                    elif stage == 8:
                        v.wait_ge(s_tanh.h, gs + 1)
                        vfc[(gs, 3)] = vf.bump(v.tensor_scalar(cTh[:, lo:hi], cTh[:, lo:hi],
                                                               0.5, 0.5, ALU.mult, ALU.add), 1)
                    elif stage == 9:
                        v.wait_ge(vf.h, vfc[(gs, 3)])
                        v.wait_ge(f_y.h, gs + 1)
                        vfc[(gs, 4)] = vf.bump(v.tensor_tensor(cTh[:, lo:hi], cT2[:, lo:hi],
                                                               cTh[:, lo:hi], ALU.mult), 1)
                    elif stage == 10:
                        v.wait_ge(vf.h, vfc[(gs, 4)])
                        vfc[(gs, 5)] = vf.bump(v.tensor_tensor(cTh[:, lo:hi], cS[:, lo:hi],
                                                               cTh[:, lo:hi], ALU.mult), 1)
                    elif stage == 11:
                        v.wait_ge(vf.h, vfc[(gs, 5)])
                        if gs >= 4:
                            v.wait_ge(c_out[gs % 2].h, 16 * (gs // 2 - 1))
                        f_fin.bump(v.tensor_tensor(cOut[:, lo:hi], cX[:, lo:hi],
                                                   cTh[:, lo:hi], ALU.add), 1)

                v.wait_ge(c_gb.h, 32)
                for p in range(16):
                    for stage in range(12):
                        for j in range(2):
                            V(stage, 2 * p + j)

        assert f_fin.n == 32
        assert c_out[0].n == 16 * 16 and c_out[1].n == 16 * 16

    return nc


def _prep_maps(inputs):
    x = np.ascontiguousarray(np.asarray(inputs["L0_attn_queries"]), dtype=np.float32)
    keys = np.ascontiguousarray(np.asarray(inputs["L0_attn_keys"])[0], dtype=np.float32)
    values = np.ascontiguousarray(np.asarray(inputs["L0_attn_values"])[0], dtype=np.float32)
    Wq = np.asarray(inputs["L0_attn_Wq_w"], dtype=np.float32)
    bq = np.asarray(inputs["L0_attn_Wq_b"], dtype=np.float32)
    Wk = np.asarray(inputs["L0_attn_Wk_w"], dtype=np.float32)
    bk = np.asarray(inputs["L0_attn_Wk_b"], dtype=np.float32)
    Wv = np.asarray(inputs["L0_attn_Wv_w"], dtype=np.float32)
    bv = np.asarray(inputs["L0_attn_Wv_b"], dtype=np.float32)
    Wr = np.asarray(inputs["L0_ffn_Wr"], dtype=np.float32)
    br = np.asarray(inputs["L0_ffn_br"], dtype=np.float32)
    W1 = np.asarray(inputs["L0_ffn_W1"], dtype=np.float32)
    b1 = np.asarray(inputs["L0_ffn_b1"], dtype=np.float32)
    W2 = np.asarray(inputs["L0_ffn_W2"], dtype=np.float32)
    b2 = np.asarray(inputs["L0_ffn_b2"], dtype=np.float32)
    gam = np.asarray(inputs["L0_ffn_gamma"], dtype=np.float32)
    bet = np.asarray(inputs["L0_ffn_beta"], dtype=np.float32)

    keysT = np.ascontiguousarray(keys.T)
    valuesT = np.ascontiguousarray(values.T)
    w1t = np.ascontiguousarray(
        W1.reshape(E, 8, 4, 128, 8, 128).transpose(0, 1, 5, 2, 4, 3).reshape(E, 8, 128, 4096))
    w2t = np.ascontiguousarray(W2.reshape(E, 8, 128, 32, 128).transpose(0, 1, 4, 3, 2))
    wrT = np.ascontiguousarray(Wr.T.reshape(8, 128, 8).transpose(1, 0, 2))
    b1t = np.ascontiguousarray(b1.reshape(E, 32, 128).transpose(2, 0, 1))
    brf = np.ascontiguousarray(br.reshape(1, 8))
    gamb = np.ascontiguousarray(np.broadcast_to(gam, (128, D)))
    betb = np.ascontiguousarray(np.broadcast_to(bet, (128, D)))
    idf = np.eye(128, dtype=np.float32)
    ones1f = np.ones((1, 128), dtype=np.float32)
    ones1b = np.ones((1, 128), dtype=BF)
    onehot = np.zeros((8, 8 * 128), dtype=BF)
    for e in range(8):
        onehot[e, e * 128:(e + 1) * 128] = 1.0
    rep8 = np.zeros((8, 128), dtype=BF)
    for m in range(128):
        rep8[m % 8, m] = 1.0

    maps = []
    for c in range(8):
        b, half = c // 2, c % 2
        cols = slice(half * 512, (half + 1) * 512)
        m = dict(
            x=x[b],
            xo=np.ascontiguousarray(x[b, half * 4096:(half + 1) * 4096]),
            keysT=keysT, valuesT=valuesT,
            wqT=np.ascontiguousarray(Wq.T[:, cols].reshape(8, 128, 4, 128).transpose(1, 0, 2, 3)),
            wkT=np.ascontiguousarray(Wk.T[:, cols].reshape(8, 128, 4, 128).transpose(1, 0, 2, 3)),
            wvT=np.ascontiguousarray(Wv.T[:, cols].reshape(8, 128, 4, 128).transpose(1, 0, 2, 3)),
            bq=np.ascontiguousarray(bq[cols].reshape(4, 128).T),
            bk=np.ascontiguousarray(bk[cols].reshape(4, 128).T),
            bvr=np.ascontiguousarray(bv[cols].reshape(1, 512)),
            wrT=wrT, br=brf, w1t=w1t, w2t=w2t, b1=b1t, b2=np.ascontiguousarray(b2),
            gam=gamb, bet=betb, idf=idf, ones1f=ones1f, ones1b=ones1b,
            onehot=onehot, rep8=rep8,
        )
        maps.append(m)
    return maps


_DT_EXPECT = dict(x="float32", xo="float32", keysT="float32", valuesT="float32",
                  wqT="float32", wkT="float32", wvT="float32", bq="float32",
                  bk="float32", bvr="float32", wrT="float32", br="float32",
                  w1t="float32", w2t="float32", b1="float32", b2="float32",
                  gam="float32", bet="float32", idf="float32", ones1f="float32",
                  ones1b="bfloat16", onehot="bfloat16", rep8="bfloat16")


def kernel(**inputs):
    maps = _prep_maps(inputs)
    for m in maps:
        for k, v in m.items():
            assert str(v.dtype) == _DT_EXPECT[k], (k, v.dtype)
    nc = build_program()
    res = run_bass_kernel_spmd(nc, maps, core_ids=list(range(8)))
    out = np.empty((4, S, D), dtype=np.float32)
    for c in range(8):
        b, half = c // 2, c % 2
        out[b, half * 4096:(half + 1) * 4096] = np.asarray(res.results[c]["y"])
    if _DBG:
        kernel._last = res
    return out


# version 20
# speedup vs baseline: 1.1136x; 1.0062x over previous
import numpy as np
import ml_dtypes
from contextlib import ExitStack

import concourse.bass as bass
import concourse.mybir as mybir
from concourse.bass_utils import run_bass_kernel_spmd

F32 = mybir.dt.float32
BF16 = mybir.dt.bfloat16
AF = mybir.ActivationFunctionType
ALU = mybir.AluOpType
AX = mybir.AxisListType
BF = ml_dtypes.bfloat16

_DBG = False

S = 8192
D = 1024
TKV = 4096
NGRP = 16
NOG = 8
E = 8
NTT = 8
NQT = 16
NGR = 16


class Ctr:
    def __init__(self, stack, nc, name):
        self.h = stack.enter_context(nc.semaphore(name))
        self.n = 0
    def bump(self, inst, k=1):
        inst.then_inc(self.h, k)
        self.n += k
        return self.n


def build_program():
    nc = bass.Bass("TRN2", target_bir_lowering=False)
    DT = nc.dram_tensor
    sk = "ExternalOutput" if _DBG else "Internal"

    d_x = DT("x", [S, D], F32, kind="ExternalInput").ap()
    d_xo = DT("xo", [4096, D], F32, kind="ExternalInput").ap()
    d_keysT = DT("keysT", [D, TKV], F32, kind="ExternalInput").ap()
    d_valuesT = DT("valuesT", [D, TKV], F32, kind="ExternalInput").ap()
    d_wqT = DT("wqT", [128, 8, 4, 128], F32, kind="ExternalInput").ap()
    d_wkT = DT("wkT", [128, 8, 4, 128], F32, kind="ExternalInput").ap()
    d_wvT = DT("wvT", [128, 8, 4, 128], F32, kind="ExternalInput").ap()
    d_bq = DT("bq", [128, 4], F32, kind="ExternalInput").ap()
    d_bk = DT("bk", [128, 4], F32, kind="ExternalInput").ap()
    d_bvr = DT("bvr", [1, 512], F32, kind="ExternalInput").ap()
    d_wrT = DT("wrT", [128, 8, 8], F32, kind="ExternalInput").ap()
    d_br = DT("br", [1, 8], F32, kind="ExternalInput").ap()
    d_w1t = DT("w1t", [E, 8, 128, 4096], F32, kind="ExternalInput").ap()
    d_w2t = DT("w2t", [E, 8, 128, 32, 128], F32, kind="ExternalInput").ap()
    d_b1 = DT("b1", [128, E, 32], F32, kind="ExternalInput").ap()
    d_b2 = DT("b2", [8, D], F32, kind="ExternalInput").ap()
    d_gam = DT("gam", [128, D], F32, kind="ExternalInput").ap()
    d_bet = DT("bet", [128, D], F32, kind="ExternalInput").ap()
    d_idf = DT("idf", [128, 128], F32, kind="ExternalInput").ap()
    d_ones1f = DT("ones1f", [1, 128], F32, kind="ExternalInput").ap()
    d_ones1b = DT("ones1b", [1, 128], BF16, kind="ExternalInput").ap()
    d_onehot = DT("onehot", [8, 8 * 128], BF16, kind="ExternalInput").ap()
    d_rep8 = DT("rep8", [8, 128], BF16, kind="ExternalInput").ap()

    d_y = DT("y", [4096, D], F32, kind="ExternalOutput").ap()
    d_t2 = DT("t2", [4096, D], F32, kind="Internal").ap()
    d_qts = DT("qts", [4, 128, S], BF16, kind=sk).ap()
    d_xnts = DT("xnts", [128, 8, 4096], BF16, kind=sk).ap()
    d_ss = DT("ss", [4096, D], BF16, kind=sk).ap()
    d_w1b = DT("w1b", [E, 8, 128, 4096], BF16, kind="Internal").ap()
    d_w2b = DT("w2b", [E, 8, 128, 32, 128], BF16, kind="Internal").ap()
    d_bnc = DT("bnc", [128, S], BF16, kind="Internal").ap()
    d_bncr2 = DT("bncr2", [2, S], BF16, kind="Internal").ap()

    with ExitStack() as st0:
        def sb(name, shape, dt):
            return st0.enter_context(nc.sbuf_tensor(name, shape, dt))
        def sem(name):
            return Ctr(st0, nc, name)

        cIdF = sb("cIdF", [128, 128], F32)
        cOn1f = sb("cOn1f", [1, 128], F32)
        cOn1b = sb("cOn1b", [1, 128], BF16)
        cOneHot = sb("cOneHot", [8, 8 * 128], BF16)
        cRep8 = sb("cRep8", [8, 128], BF16)
        cWrT = sb("cWrT", [128, 8 * 8], F32)
        cBr = sb("cBr", [1, 8], F32)
        cBq = sb("cBq", [128, 4], F32)
        cBk = sb("cBk", [128, 4], F32)
        cBvR = sb("cBvR", [1, 512], BF16)
        cB1 = sb("cB1", [128, E * 32], F32)
        cB2 = sb("cB2", [8, D], BF16)
        stRS = sb("stRS", [128, 32], F32)
        stBL = sb("stBL", [128, 32], F32)
        stLG = sb("stLG", [128, 32 * 8], F32)
        cDmy = sb("cDmy", [1, 4], F32)
        st_ab = ExitStack()
        KT = st_ab.enter_context(nc.sbuf_tensor("KT", [128, 4 * TKV], BF16))
        vAll = st_ab.enter_context(nc.sbuf_tensor("vAll", [128, 8 * 32 * 65], BF16))
        cWt = st_ab.enter_context(nc.sbuf_tensor("cWt", [8, 8 * 512], BF16))
        st_w = ExitStack()
        cWq = st_w.enter_context(nc.sbuf_tensor("cWq", [128, 8 * 4 * 128], BF16))
        cWk = st_w.enter_context(nc.sbuf_tensor("cWk", [128, 8 * 4 * 128], BF16))
        cWv = st_w.enter_context(nc.sbuf_tensor("cWv", [128, 8 * 4 * 128], BF16))

        c_const = sem("c_const")
        c_constg = sem("c_constg")
        c_kv = sem("c_kv")
        c_x = [sem("c_x0"), sem("c_x1")]
        c_sp = sem("c_sp")
        c_sq = [sem("c_sq0"), sem("c_sq1")]
        c_qT = [sem("c_qT0"), sem("c_qT1"), sem("c_qT2"), sem("c_qT3")]
        c_xr = sem("c_xr"); c_d8 = sem("c_d8"); c_scr = sem("c_scr")
        c_w1 = sem("c_w1")
        c_w2 = sem("c_w2")
        c_w1l = [sem("c_w1l0"), sem("c_w1l1"), sem("c_w1l2")]
        c_w2l = [sem("c_w2l0"), sem("c_w2l1"), sem("c_w2l2")]
        c_cxn = [sem("c_cxn0"), sem("c_cxn1")]
        c_cs = [sem("c_cs0"), sem("c_cs1")]
        c_cxx = [sem("c_cxx0"), sem("c_cxx1")]
        c_out = [sem("c_out0"), sem("c_out1")]
        k_mm = sem("k_mm"); k_ev = sem("k_ev")
        a_mu = sem("a_mu"); a_sq = sem("a_sq"); a_var = sem("a_var")
        a_rs = sem("a_rs"); a_bl = sem("a_bl"); a_xn = sem("a_xn")
        a_tr = sem("a_tr"); a_ev = sem("a_ev"); a_q = sem("a_q")
        a_qe = sem("a_qe"); a_lg = sem("a_lg"); a_lgs = sem("a_lgs")
        a_bar = sem("a_bar")
        c_bn = sem("c_bn")
        b_sc = sem("b_sc"); b_ex = sem("b_ex"); b_av = sem("b_av")
        b_ae = sem("b_ae"); b_ve = sem("b_ve"); b_dn = sem("b_dn")
        b_rc = sem("b_rc"); b_xn = sem("b_xn"); b_sg = sem("b_sg")
        f_h = sem("f_h"); f_hr = sem("f_hr"); f_hw = sem("f_hw")
        f_o = sem("f_o"); f_oe = sem("f_oe")
        t_wb = sem("t_wb"); f_wb = sem("f_wb")
        r_w = sem("r_w"); r_t = sem("r_t"); r_wt = sem("r_wt")
        s_rex = sem("s_rex"); s_xn2 = sem("s_xn2")
        n_nm = sem("n_nm"); n_var = sem("n_var")
        s_sq = sem("s_sq"); s_rs = sem("s_rs"); s_A = sem("s_A"); s_tanh = sem("s_tanh")
        vf = sem("vf"); sf = sem("sf")
        f_t2 = sem("f_t2"); f_t2a = sem("f_t2a"); f_bl = sem("f_bl")
        f_vf = sem("f_vf")
        f_y = sem("f_y"); f_fin = sem("f_fin")
        c_vo = sem("c_vo"); t_bc = sem("t_bc"); f_t2c = sem("f_t2c")
        c_t2s = sem("c_t2s"); c_t2l = [sem("c_t2l0"), sem("c_t2l1")]
        c_bnA = [sem("c_bnA0"), sem("c_bnA1")]
        c_bnD = [sem("c_bnD0"), sem("c_bnD1")]
        m_vd = sem("m_vd"); m_sd = sem("m_sd")

        blk = st0.enter_context(nc.Block())

        @blk.sync
        def _(sync):
            for dst, src in [(cIdF, d_idf), (cOn1f, d_ones1f), (cOn1b, d_ones1b),
                             (cOneHot, d_onehot), (cRep8, d_rep8), (cBr, d_br),
                             (cBq, d_bq), (cBk, d_bk)]:
                c_const.bump(sync.dma_start(out=dst[:], in_=src), 16)
            c_const.bump(sync.dma_start(
                out=cWrT[:].rearrange("p (a b) -> p a b", a=8), in_=d_wrT), 16)
            c_const.bump(sync.dma_start(
                out=cB1[:].rearrange("p (a b) -> p a b", a=E), in_=d_b1), 16)

        @blk.gpsimd
        def _(g):
            for dst, src in [(cWq, d_wqT), (cWk, d_wkT), (cWv, d_wvT)]:
                for half in range(2):
                    c_constg.bump(g.dma_start(
                        out=dst[:, half * 2048:(half + 1) * 2048]
                            .rearrange("p (a b c) -> p a b c", a=4, b=4),
                        in_=src[:, half * 4:(half + 1) * 4]), 16)
            c_constg.bump(g.dma_start(out=cBvR[:], in_=d_bvr), 16)
            c_constg.bump(g.dma_start(out=cB2[:], in_=d_b2), 16)
        N_CONST = 10

        @blk.vector
        def _(v):
            c_vo.bump(v.memset(
                vAll[:].rearrange("p (h t e) -> p h t e", h=8, t=32)[:, :, :, 64:65], 1.0), 1)

        with ExitStack() as stkv:
            kvK = [stkv.enter_context(nc.sbuf_tensor(f"kvK{i}", [128, 8 * 512], BF16)) for i in range(2)]
            kvV = [stkv.enter_context(nc.sbuf_tensor(f"kvV{i}", [128, 8 * 512], BF16)) for i in range(2)]
            pKV = [stkv.enter_context(nc.psum_tensor(f"pKV{i}", [128, 512], F32)) for i in range(4)]

            @blk.gpsimd
            def _(g):
                for tg in range(8):
                    if tg >= 2:
                        g.wait_ge(k_mm.h, 8 * (tg - 1))
                    g.wait_ge(c_kv.h, c_kv.n)
                    for half in range(2):
                        g.dma_start(
                            out=kvK[tg % 2][:, half * 2048:(half + 1) * 2048]
                                .rearrange("p (a b) -> p a b", a=4),
                            in_=d_keysT[half * 512:(half + 1) * 512, tg * 512:(tg + 1) * 512]
                                .rearrange("(a p) t -> p a t", p=128)
                        ).then_inc(c_kv.h, 16); c_kv.n += 16
                    g.wait_ge(c_kv.h, c_kv.n)
                    for half in range(2):
                        g.dma_start(
                            out=kvV[tg % 2][:, half * 2048:(half + 1) * 2048]
                                .rearrange("p (a b) -> p a b", a=4),
                            in_=d_valuesT[half * 512:(half + 1) * 512, tg * 512:(tg + 1) * 512]
                                .rearrange("(a p) t -> p a t", p=128)
                        ).then_inc(c_kv.h, 16); c_kv.n += 16

            @blk.tensor
            def _(t):
                t.wait_ge(c_const.h, 16 * N_CONST)
                t.wait_ge(c_constg.h, 16 * 8)
                for tg in range(8):
                    t.wait_ge(c_kv.h, 64 * (tg + 1))
                    kb, vb = kvK[tg % 2], kvV[tg % 2]
                    for dkc in range(4):
                        if dkc >= 2:
                            t.wait_ge(k_ev.h, 8 * tg + dkc - 1)
                        elif tg >= 1:
                            t.wait_ge(k_ev.h, 8 * tg + dkc - 5)
                        for dc in range(8):
                            inst = t.matmul(pKV[dkc % 2][:],
                                            cWk[:, (dc * 4 + dkc) * 128:(dc * 4 + dkc + 1) * 128],
                                            kb[:, dc * 512:(dc + 1) * 512],
                                            start=(dc == 0), stop=(dc == 7))
                        k_mm.bump(inst, 1)
                    for tcl in range(4):
                        if tcl >= 2:
                            t.wait_ge(k_ev.h, 8 * tg + tcl + 3)
                        elif tg >= 1:
                            t.wait_ge(k_ev.h, 8 * tg + tcl - 1)
                        for dc in range(8):
                            t.matmul(pKV[2 + tcl % 2][:],
                                     vb[:, dc * 512 + tcl * 128: dc * 512 + (tcl + 1) * 128],
                                     cWv[:, dc * 512:(dc + 1) * 512],
                                     start=(dc == 0), stop=False)
                        inst = t.matmul(pKV[2 + tcl % 2][:], cOn1b[:], cBvR[:],
                                        start=False, stop=True)
                        k_mm.bump(inst, 1)

            @blk.scalar
            def _(s):
                for tg in range(8):
                    for dkc in range(4):
                        s.wait_ge(k_mm.h, 8 * tg + dkc + 1)
                        inst = s.activation(KT[:, dkc * TKV + tg * 512: dkc * TKV + (tg + 1) * 512],
                                            pKV[dkc % 2][:], AF.Relu, bias=cBk[:, dkc:dkc + 1])
                        k_ev.bump(inst, 1)
                    for tcl in range(4):
                        s.wait_ge(k_mm.h, 8 * tg + 4 + tcl + 1)
                        tc = tg * 4 + tcl
                        inst = s.activation(
                            vAll[:].rearrange("p (h m) -> p h m", h=8)[:, :, tc * 65:tc * 65 + 64],
                            pKV[2 + tcl % 2][:].rearrange("p (h d) -> p h d", h=8),
                            AF.Relu)
                        k_ev.bump(inst, 1)

        assert k_mm.n == 64 and k_ev.n == 64


        def phaseA(tag, src_ap, ngrp, own):
            base_bar = a_bar.n
            base_mu, base_sq, base_var = a_mu.n, a_sq.n, a_var.n
            base_rs, base_bl, base_xn = a_rs.n, a_bl.n, a_xn.n
            base_tr, base_ev, base_q = a_tr.n, a_ev.n, a_q.n
            base_qe, base_lg = a_qe.n, a_lg.n
            base_lgs = a_lgs.n
            base_cx = [c_x[0].n, c_x[1].n]
            base_sp = c_sp.n
            base_cq = [c_sq[0].n, c_sq[1].n]
            BAR = 2
            tprog = (a_lg if own else a_q)
            base_tp = tprog.n
            EVB = 8 if own else 4
            EVG = 8 * EVB

            with ExitStack() as sta:
                aX = [sta.enter_context(nc.sbuf_tensor(f"aX{tag}{i}", [128, 4 * 1024], F32)) for i in range(2)]
                aSq = sta.enter_context(nc.sbuf_tensor(f"aSq{tag}", [128, 4 * 1024], F32))
                aXn = sta.enter_context(nc.sbuf_tensor(f"aXn{tag}", [128, 4 * 1024], F32))
                aSm = sta.enter_context(nc.sbuf_tensor(f"aSm{tag}", [128, 4], F32))
                aNM = sta.enter_context(nc.sbuf_tensor(f"aNM{tag}", [128, 4], F32))
                aVar = sta.enter_context(nc.sbuf_tensor(f"aVar{tag}", [128, 4], F32))
                aRS = sta.enter_context(nc.sbuf_tensor(f"aRS{tag}", [128, 4], F32))
                aBL = sta.enter_context(nc.sbuf_tensor(f"aBL{tag}", [128, 4], F32))
                aXnTs = [sta.enter_context(nc.sbuf_tensor(f"aXnT{tag}{i}", [128, 8 * 512], BF16))
                         for i in range(2 if own else 1)]
                aXnT32 = sta.enter_context(nc.sbuf_tensor(f"aXnT32{tag}", [128, 8 * 512], F32)) if own else None
                aQe = [sta.enter_context(nc.sbuf_tensor(f"aQe{tag}{i}", [128, 4 * 512], BF16))
                       for i in range(2)] if not own else None
                pTr = [sta.enter_context(nc.psum_tensor(f"pTr{tag}{i}", [128, 512], F32)) for i in range(2)]
                pQL = [sta.enter_context(nc.psum_tensor(f"pQL{tag}{i}", [128, 512], F32)) for i in range(4)]

                @blk.sync
                def _(sync):
                    for g in range(ngrp):
                        if g == 0:
                            sync.wait_ge(k_ev.h, 64)
                            sync.wait_ge(a_bar.h, base_bar)
                            sync.wait_ge(a_q.h, base_q)
                            sync.wait_ge(c_sq[0].h, base_cq[0])
                            sync.wait_ge(c_sq[1].h, base_cq[1])
                        if g >= 2:
                            sync.wait_ge(a_bar.h, base_bar + BAR * (g - 1))
                            sync.wait_ge(tprog.h, base_tp + 4 * (g - 1))
                        for i in range(4):
                            c_x[g % 2].bump(sync.dma_start(
                                out=aX[g % 2][:, i * 1024:(i + 1) * 1024],
                                in_=src_ap[g * 512 + i * 128: g * 512 + (i + 1) * 128, :]
                            ), 16)

                @blk.vector
                def _(v):
                    for g in range(ngrp):
                        if g >= 1:
                            v.wait_ge(a_bar.h, base_bar + BAR * g)
                            v.wait_ge(tprog.h, base_tp + 4 * g)
                        xb = aX[g % 2]
                        v.wait_ge(c_x[g % 2].h, base_cx[g % 2] + 64 * (g // 2 + 1))
                        for i in range(4):
                            vf.bump(v.reduce_sum(aSm[:, i:i + 1], xb[:, i * 1024:(i + 1) * 1024], axis=AX.X), 1)
                        v.wait_ge(vf.h, vf.n)
                        for i in range(4):
                            inst = v.tensor_scalar(aNM[:, i:i + 1], aSm[:, i:i + 1],
                                                   -1.0 / 1024, None, ALU.mult)
                            a_mu.bump(inst, 1)
                        for i in range(4):
                            v.wait_ge(a_sq.h, base_sq + 4 * g + i + 1)
                            vf.bump(v.reduce_sum(aVar[:, i:i + 1], aSq[:, i * 1024:(i + 1) * 1024], axis=AX.X), 1)
                        v.wait_ge(vf.h, vf.n)
                        for i in range(4):
                            inst = v.tensor_scalar(aVar[:, i:i + 1], aVar[:, i:i + 1],
                                                   1.0 / 1024, 1e-5, ALU.mult, ALU.add)
                            a_var.bump(inst, 1)
                        v.wait_ge(a_mu.h, base_mu + 4 * g + 4)
                        for i in range(4):
                            v.wait_ge(a_rs.h, base_rs + 4 * g + i + 1)
                            inst = v.tensor_tensor(aBL[:, i:i + 1], aNM[:, i:i + 1],
                                                   aRS[:, i:i + 1], ALU.mult)
                            a_bl.bump(inst, 1)
                        if own:
                            v.wait_ge(a_bl.h, base_bl + 4 * g + 4)
                            vf.bump(v.tensor_copy(stRS[:, 4 * g:4 * g + 4], aRS[:]), 1)
                            vf.bump(v.tensor_copy(stBL[:, 4 * g:4 * g + 4], aBL[:]), 1)
                        if own and g >= 2:
                            v.wait_ge(c_sq[g % 2].h, base_cq[g % 2] + 64 * (g // 2))
                        xnt = aXnTs[g % 2] if own else aXnTs[0]
                        for i in range(4):
                            for bank in range(2):
                                v.wait_ge(a_tr.h, base_tr + 8 * g + 2 * i + bank + 1)
                                p = pTr[bank]
                                for dcq in range(4):
                                    dc = bank * 4 + dcq
                                    a_ev.bump(v.tensor_copy(
                                        xnt[:, dc * 512 + i * 128: dc * 512 + (i + 1) * 128],
                                        p[:, dcq * 128:(dcq + 1) * 128]), 1)
                                    if own:
                                        a_ev.bump(v.tensor_copy(
                                            aXnT32[:, dc * 512 + i * 128: dc * 512 + (i + 1) * 128],
                                            p[:, dcq * 128:(dcq + 1) * 128]), 1)
                        if own:
                            for i in range(4):
                                v.wait_ge(a_lg.h, base_lg + 4 * g + i + 1)
                                inst = v.tensor_copy(stLG[:, (4 * g + i) * 8:(4 * g + i + 1) * 8],
                                                     pQL[i][:, 0:8])
                                a_lgs.bump(inst, 1)
                        v.wait_ge(a_ev.h, base_ev + EVG * (g + 1))
                        if own:
                            v.wait_ge(a_lgs.h, base_lgs + 4 * (g + 1))
                        v.wait_ge(vf.h, vf.n)
                        inst = v.engine_nop()
                        a_bar.bump(inst, 1)

                @blk.scalar
                def _(s):
                    for g in range(ngrp):
                        if g >= 1:
                            s.wait_ge(a_bar.h, base_bar + BAR * g)
                            s.wait_ge(tprog.h, base_tp + 4 * g)
                        xb = aX[g % 2]
                        for i in range(4):
                            s.wait_ge(a_mu.h, base_mu + 4 * g + i + 1)
                            inst = s.activation(aSq[:, i * 1024:(i + 1) * 1024],
                                                xb[:, i * 1024:(i + 1) * 1024],
                                                AF.Square, bias=aNM[:, i:i + 1])
                            a_sq.bump(inst, 1)
                        for i in range(4):
                            s.wait_ge(a_var.h, base_var + 4 * g + i + 1)
                            sf.bump(s.activation(aRS[:, i:i + 1], aVar[:, i:i + 1], AF.Ln), 1)
                        s.wait_ge(sf.h, sf.n)
                        for i in range(4):
                            inst = s.activation(aRS[:, i:i + 1], aRS[:, i:i + 1], AF.Exp, scale=-0.5)
                            a_rs.bump(inst, 1)
                        for i in range(4):
                            s.wait_ge(a_bl.h, base_bl + 4 * g + i + 1)
                            inst = s.activation(aXn[:, i * 1024:(i + 1) * 1024],
                                                xb[:, i * 1024:(i + 1) * 1024],
                                                AF.Identity, bias=aBL[:, i:i + 1], scale=aRS[:, i:i + 1])
                            a_xn.bump(inst, 1)
                        if not own:
                            if g >= 2:
                                s.wait_ge(c_sq[g % 2].h, base_cq[g % 2] + 64 * (g // 2))
                            for dqc in range(4):
                                s.wait_ge(a_q.h, base_q + 4 * g + dqc + 1)
                                inst = s.activation(aQe[g % 2][:, dqc * 512:(dqc + 1) * 512], pQL[dqc][:],
                                                    AF.Relu, bias=cBq[:, dqc:dqc + 1])
                                a_qe.bump(inst, 1)
                            s.wait_ge(a_qe.h, base_qe + 4 * (g + 1))
                            for ai in range(4):
                                s.dma_start(
                                    out=d_qts[ai, :, g * 512:(g + 1) * 512],
                                    in_=aQe[g % 2][:, ai * 512:(ai + 1) * 512]
                                ).then_inc(c_sq[g % 2].h, 16); c_sq[g % 2].n += 16
                        else:
                            s.wait_ge(a_ev.h, base_ev + EVG * (g + 1))
                            for j in range(4):
                                s.dma_start(
                                    out=d_xnts[:, 2 * j:2 * j + 2, g * 512:(g + 1) * 512],
                                    in_=aXnTs[g % 2][:, (2 * j) * 512:(2 * j + 2) * 512]
                                        .rearrange("p (a t) -> p a t", a=2)
                                ).then_inc(c_sq[g % 2].h, 16); c_sq[g % 2].n += 16
                        a_bar.bump(s.memzero(cDmy[:]), 1)

                @blk.tensor
                def _(t):
                    for g in range(ngrp):
                        if g == 0:
                            t.wait_ge(k_ev.h, 64)
                            t.wait_ge(a_bar.h, base_bar)
                            t.wait_ge(a_q.h, base_q)
                        if g >= 1:
                            t.wait_ge(a_bar.h, base_bar + BAR * g)
                        for i in range(4):
                            t.wait_ge(a_xn.h, base_xn + 4 * g + i + 1)
                            for bank in range(2):
                                prev = 2 * i + bank - 2
                                if prev >= 0:
                                    t.wait_ge(a_ev.h, base_ev + EVG * g + EVB * (prev + 1))
                                p = pTr[bank]
                                for dcq in range(4):
                                    dc = bank * 4 + dcq
                                    inst = t.transpose(p[:, dcq * 128:(dcq + 1) * 128],
                                                       aXn[:, i * 1024 + dc * 128: i * 1024 + (dc + 1) * 128],
                                                       cIdF[:])
                                a_tr.bump(inst, 1)
                        t.wait_ge(a_ev.h, base_ev + EVG * (g + 1))
                        if not own:
                            for dqc in range(4):
                                for dc in range(8):
                                    inst = t.matmul(pQL[dqc][:],
                                                    cWq[:, (dc * 4 + dqc) * 128:(dc * 4 + dqc + 1) * 128],
                                                    aXnTs[0][:, dc * 512:(dc + 1) * 512],
                                                    start=(dc == 0), stop=(dc == 7))
                                a_q.bump(inst, 1)
                        else:
                            for i in range(4):
                                for dc in range(8):
                                    t.matmul(pQL[i][:, 0:8],
                                             aXnT32[:, dc * 512 + i * 128: dc * 512 + (i + 1) * 128],
                                             cWrT[:, dc * 8:(dc + 1) * 8],
                                             start=(dc == 0), stop=False)
                                inst = t.matmul(pQL[i][:, 0:8], cOn1f[:], cBr[:],
                                                start=False, stop=True)
                                a_lg.bump(inst, 1)

        phaseA("1", d_x, NGRP, own=False)
        phaseA("2", d_xo, NOG, own=True)
        st_w.close()
        A_SP = c_sp.n
        A_SQ = (c_sq[0].n, c_sq[1].n)
        assert A_SP == 0 and A_SQ == (64 * (NGRP + NOG) // 2,) * 2
        assert a_lgs.n == 32 and a_bar.n == 2 * (NGRP + NOG)
        A_BAR = a_bar.n


        with ExitStack() as str_:
            rMx = str_.enter_context(nc.sbuf_tensor("rMx", [128, 32 * 8], F32))
            rNm = str_.enter_context(nc.sbuf_tensor("rNm", [128, 32], F32))
            rE2 = str_.enter_context(nc.sbuf_tensor("rE2", [128, 32], F32))
            rRc = str_.enter_context(nc.sbuf_tensor("rRc", [128, 32], F32))
            rEx = str_.enter_context(nc.sbuf_tensor("rEx", [128, 32 * 8], F32))
            rMk = str_.enter_context(nc.sbuf_tensor("rMk", [128, 32 * 8], F32))
            rW = str_.enter_context(nc.sbuf_tensor("rW", [128, 32 * 8], F32))
            pRt = str_.enter_context(nc.psum_tensor("pRt", [128, 512], F32))

            @blk.scalar
            def _(s):
                for i2 in range(32):
                    s.wait_ge(r_w.h, 32 + i2 + 1)
                    s_rex.bump(s.activation(rE2[:, i2:i2 + 1], rMx[:, i2 * 8 + 1:i2 * 8 + 2],
                                            AF.Exp, bias=rNm[:, i2:i2 + 1]), 1)
                    s_rex.bump(s.activation(rEx[:, i2 * 8:(i2 + 1) * 8],
                                            stLG[:, i2 * 8:(i2 + 1) * 8],
                                            AF.Exp, bias=rNm[:, i2:i2 + 1]), 1)

            @blk.vector
            def _(v):
                v.wait_ge(a_lgs.h, 32)
                v.wait_ge(c_sq[0].h, A_SQ[0])
                v.wait_ge(c_sq[1].h, A_SQ[1])
                for i2 in range(32):
                    r_w.bump(v.max(out=rMx[:, i2 * 8:(i2 + 1) * 8],
                                   in_=stLG[:, i2 * 8:(i2 + 1) * 8]), 1)
                v.wait_ge(r_w.h, 32)
                for i2 in range(32):
                    r_w.bump(v.tensor_scalar(rNm[:, i2:i2 + 1], rMx[:, i2 * 8:i2 * 8 + 1],
                                             -1.0, None, ALU.mult), 1)
                for i2 in range(32):
                    v.wait_ge(s_rex.h, 2 * i2 + 1)
                    vf.bump(v.tensor_scalar(rE2[:, i2:i2 + 1], rE2[:, i2:i2 + 1],
                                            1.0, 1.0, ALU.mult, ALU.add), 1)
                v.wait_ge(vf.h, vf.n)
                for i2 in range(32):
                    vf.bump(v.reciprocal(rRc[:, i2:i2 + 1], rE2[:, i2:i2 + 1]), 1)
                for i2 in range(32):
                    vf.bump(v.tensor_scalar(rMk[:, i2 * 8:(i2 + 1) * 8],
                                            stLG[:, i2 * 8:(i2 + 1) * 8],
                                            rMx[:, i2 * 8 + 1:i2 * 8 + 2], None, ALU.is_ge), 1)
                v.wait_ge(s_rex.h, 64)
                v.wait_ge(vf.h, vf.n)
                for i2 in range(32):
                    vf.bump(v.tensor_scalar(rEx[:, i2 * 8:(i2 + 1) * 8], rEx[:, i2 * 8:(i2 + 1) * 8],
                                            rRc[:, i2:i2 + 1], None, ALU.mult), 1)
                v.wait_ge(vf.h, vf.n)
                for i2 in range(32):
                    r_w.bump(v.tensor_tensor(rW[:, i2 * 8:(i2 + 1) * 8], rEx[:, i2 * 8:(i2 + 1) * 8],
                                             rMk[:, i2 * 8:(i2 + 1) * 8], ALU.mult), 1)
                for tt in range(NTT):
                    v.wait_ge(r_t.h, 4 * (tt + 1))
                    r_wt.bump(v.tensor_copy(cWt[:, tt * 512:(tt + 1) * 512], pRt[0:8, :]), 1)

            @blk.tensor
            def _(t):
                for tt in range(NTT):
                    if tt >= 1:
                        t.wait_ge(r_wt.h, tt)
                    for i in range(4):
                        i2 = tt * 4 + i
                        t.wait_ge(r_w.h, 64 + i2 + 1)
                        inst = t.transpose(pRt[0:8, i * 128:(i + 1) * 128],
                                           rW[:, i2 * 8:(i2 + 1) * 8], cIdF[:])
                        r_t.bump(inst, 1)
        assert r_w.n == 96 and r_t.n == 32 and r_wt.n == 8 and s_rex.n == 64

        chains = []
        for tt in range(NTT):
            chains.append(("cxn", tt))
            if tt >= 1:
                for i in range(4):
                    for half in range(2):
                        chains.append(("tr", (tt - 1) * 4 + i, half, 4))
            for e in range(E):
                ge = tt * 8 + e
                chains.append(("bc", ge, 0))
                for fc in range(32):
                    chains.append(("hc", ge, fc, 8))
                for dc in range(8):
                    chains.append(("oc", ge, dc, 32))
        for i in range(4):
            for half in range(2):
                chains.append(("tr", 28 + i, half, 4))
        chains_tt = []
        cur_tt = 0
        for ch in chains:
            if ch[0] == "cxn":
                cur_tt = ch[1]
            chains_tt.append((cur_tt, ch))
        PREFIX = 0
        cmm_tt = [0] * NTT
        for tt, ch in chains_tt:
            if ch[0] in ("tr", "hc", "oc"):
                cmm_tt[tt] += ch[-1]
        w_tt = [1.0] * 8
        rem = 4096 - PREFIX
        denom = sum(w_tt[t] * cmm_tt[t] for t in range(NTT))
        rate_tt = [rem * w_tt[t] / denom for t in range(NTT)]
        EV = []
        pend = []
        bu = 0

        def push(ev):
            EV.append(ev)
            for p in pend:
                p[0] -= 1
            ready = [p for p in pend if p[0] <= 0]
            pend[:] = [p for p in pend if p[0] > 0]
            for p in ready:
                EV.append(p[1])

        def emit_unit(k, u):
            if u == 0 and k + 2 < 128:
                push(("bq", k + 2))
            push(("bu", k, u))
            if u == 31:
                push(("bev", k))
                pend.append([2, ("bnc", k)])
                if k % 16 == 15:
                    h = k // 16
                    pend.extend([[3, ("d8", h)], [5, ("pdb", h, 0)], [6, ("rcp", h, 0)],
                                 [7, ("scr", h)], [8, ("nrm", h)], [9, ("pdb", h, 1)],
                                 [10, ("rcp", h, 1)], [13, ("sadd", h)], [23, ("sst", h)]])

        push(("bq", 0))
        push(("bq", 1))
        while bu < PREFIX:
            emit_unit(bu // 32, bu % 32)
            bu += 1
        pend.sort(key=lambda p: p[0])
        for p in pend:
            EV.append(p[1])
        pend.clear()
        acc = 0.0
        for tt, ch in chains_tt:
            push(ch)
            if ch[0] in ("tr", "hc", "oc"):
                acc += ch[-1] * rate_tt[tt]
            while acc >= 1.0 and bu < 4096:
                acc -= 1.0
                emit_unit(bu // 32, bu % 32)
                bu += 1
        while bu < 4096:
            emit_unit(bu // 32, bu % 32)
            bu += 1
        pend.sort(key=lambda p: p[0])
        for p in pend:
            EV.append(p[1])

        with ExitStack() as stm:
            bQt = stm.enter_context(nc.sbuf_tensor("bQt", [128, 4 * 512], BF16))
            bEx = stm.enter_context(nc.sbuf_tensor("bEx", [128, 4 * 512], BF16))
            bXr = stm.enter_context(nc.sbuf_tensor("bXr", [128, 4 * 1024], BF16))
            bScr = stm.enter_context(nc.sbuf_tensor("bScr", [128, 4 * 1024], BF16))
            bDiv = stm.enter_context(nc.sbuf_tensor("bDiv", [128, 1024], BF16))
            bAssy = stm.enter_context(nc.sbuf_tensor("bAssy", [65, 2 * 512], BF16))
            bD8 = stm.enter_context(nc.sbuf_tensor("bD8", [8, 1024], BF16))
            cXnT = stm.enter_context(nc.sbuf_tensor("cXnT", [128, 8 * 512], BF16))
            cW1 = [stm.enter_context(nc.sbuf_tensor(f"cW1_{i}", [128, 4 * 8 * 128], BF16)) for i in range(2)]
            cW2 = [stm.enter_context(nc.sbuf_tensor(f"cW2_{i}", [128, 32 * 128], BF16)) for i in range(2)]
            cHW = stm.enter_context(nc.sbuf_tensor("cHW", [128, 32 * 512], BF16))
            cHR = stm.enter_context(nc.sbuf_tensor("cHR", [128, 2 * 512], BF16))
            cWbc = stm.enter_context(nc.sbuf_tensor("cWbc", [128, 2 * 512], BF16))
            cAcc = stm.enter_context(nc.sbuf_tensor("cAcc", [128, 8 * 512], F32))
            sT2 = stm.enter_context(nc.sbuf_tensor("sT2", [128, 1024], F32))
            pH = [stm.enter_context(nc.psum_tensor(f"pH{i}", [128, 512], F32)) for i in range(2)]
            pO = stm.enter_context(nc.psum_tensor("pO", [128, 512], F32))
            pT2m = stm.enter_context(nc.psum_tensor("pT2m", [128, 512], F32))
            pSc = [stm.enter_context(nc.psum_tensor(f"pSc{i}", [128, 512], F32)) for i in range(2)]
            pAv = [stm.enter_context(nc.psum_tensor(f"pAv{i}", [128, 512], F32)) for i in range(2)]

            @blk.gpsimd
            def _(g):
                def cast_pair(e):
                    if e >= 2:
                        g.wait_ge(a_bar.h, min(4 * e, A_BAR))
                    g.wait_ge(c_w1.h, c_w1.n)
                    g.dma_start(out=d_w1b[e], in_=d_w1t[e]).then_inc(c_w1.h, 16)
                    c_w1.n += 16
                    g.wait_ge(c_w2.h, c_w2.n)
                    g.dma_start(out=d_w2b[e], in_=d_w2t[e]).then_inc(c_w2.h, 16)
                    c_w2.n += 16

                def issue_xr(h):
                    if h >= 1:
                        g.wait_ge(b_sg.h, 4 * h)
                    g.wait_ge(c_xr.h, 16 * h)
                    c_xr.bump(g.dma_start(
                        out=bXr[:].rearrange("p (i c) -> p i c", i=4),
                        in_=d_xo[h * 512:(h + 1) * 512, :].rearrange("(i p) c -> p i c", p=128)), 16)

                for e in range(E):
                    cast_pair(e)
                g.wait_ge(a_bar.h, A_BAR)
                g.wait_ge(c_sq[0].h, A_SQ[0])
                g.wait_ge(c_sq[1].h, A_SQ[1])
                g.wait_ge(r_wt.h, 8)
                for h in range(8):
                    issue_xr(h)

            @blk.sync
            def _(sync):
                sync.wait_ge(a_bar.h, A_BAR)
                sync.wait_ge(c_sq[0].h, A_SQ[0])
                sync.wait_ge(c_sq[1].h, A_SQ[1])
                sync.wait_ge(r_wt.h, 8)
                next_w1 = [0]
                next_w2 = [0]

                def issue_w1():
                    ug = next_w1[0]
                    e_mod = (ug // 8) % 8
                    sync.wait_ge(c_w1.h, 16 * (e_mod + 1))
                    if ug >= 2:
                        sync.wait_ge(f_h.h, 4 * ug - 4)
                    c_w1l[ug % 2].bump(sync.dma_start(
                        out=cW1[ug % 2][:], in_=d_w1b[(ug // 8) % 8, ug % 8]), 16)
                    next_w1[0] += 1

                def issue_w2():
                    vg = next_w2[0]
                    e_mod = (vg // 8) % 8
                    sync.wait_ge(c_w2.h, 16 * (e_mod + 1))
                    if vg >= 1:
                        sync.wait_ge(f_o.h, vg - 1)
                    c_w2l[vg % 2].bump(sync.dma_start(
                        out=cW2[vg % 2][:],
                        in_=d_w2b[(vg // 8) % 8, vg % 8].rearrange("p a q -> p (a q)")), 16)
                    next_w2[0] += 1

                for ev in EV:
                    kind = ev[0]
                    if kind == "cxn":
                        tt = ev[1]
                        if tt >= 1:
                            sync.wait_ge(f_h.h, 256 * tt)
                        c_cxn[0].bump(sync.dma_start(
                            out=cXnT[:].rearrange("p (a t) -> p a t", a=8),
                            in_=d_xnts[:, :, tt * 512:(tt + 1) * 512]), 16)
                    elif kind == "hc":
                        ge, fc = ev[1], ev[2]
                        if fc % 4 == 0:
                            ug_cur = ge * 8 + fc // 4
                            while next_w1[0] <= min(ug_cur + 2, 511):
                                issue_w1()
                    elif kind == "oc":
                        ge, dc = ev[1], ev[2]
                        vg_cur = ge * 8 + dc
                        while next_w2[0] <= min(vg_cur + 2, 511):
                            issue_w2()
                    elif kind == "tr":
                        gi, half = ev[1], ev[2]
                        if half == 1:
                            sync.wait_ge(f_t2c.h, 2 * gi + 2)
                            c_t2s.bump(sync.dma_start(
                                out=d_t2[gi * 128:(gi + 1) * 128, :], in_=sT2[:]), 16)
                    elif kind == "bq":
                        k = ev[1]
                        hp2, a2, qt2 = k // 32, (k // 16) % 2, k % 16
                        if k >= 4:
                            sync.wait_ge(b_sc.h, 32 * (k - 3))
                        c_qT[k % 4].bump(sync.dma_start(
                            out=bQt[64 * a2:64 * a2 + 64, (k % 4) * 512:(k % 4 + 1) * 512],
                            in_=d_qts[hp2, 64 * a2:64 * a2 + 64, qt2 * 512:(qt2 + 1) * 512]), 16)

            @blk.tensor
            def _(t):
                for ev in EV:
                    kind = ev[0]
                    if kind == "bc":
                        ge = ev[1]
                        tt, e = ge // 8, ge % 8
                        t.wait_ge(r_wt.h, 8)
                        if ge >= 1:
                            t.wait_ge(f_wb.h, ge)
                        if e == 0 and tt >= 1:
                            t.wait_ge(f_t2c.h, 8 * tt)
                        t_bc.bump(t.matmul(pT2m[:], cOneHot[:, e * 128:(e + 1) * 128],
                                           cWt[:, tt * 512:(tt + 1) * 512],
                                           start=True, stop=True), 1)
                    elif kind == "hc":
                        ge, fc = ev[1], ev[2]
                        tt, e = ge // 8, ge % 8
                        gf = ge * 32 + fc
                        ug = ge * 8 + fc // 4
                        if e == 0 and fc == 0:
                            t.wait_ge(c_cxn[0].h, 16 * (tt + 1))
                        if fc % 4 == 0:
                            t.wait_ge(c_w1l[ug % 2].h, 16 * (ug // 2 + 1))
                        if gf >= 2:
                            t.wait_ge(f_hr.h, gf - 1)
                        for dc in range(8):
                            inst = t.matmul(pH[fc % 2][:],
                                            cW1[ug % 2][:, ((fc % 4) * 8 + dc) * 128:
                                                          ((fc % 4) * 8 + dc + 1) * 128],
                                            cXnT[:, dc * 512:(dc + 1) * 512],
                                            start=(dc == 0), stop=(dc == 7))
                        f_h.bump(inst, 1)
                    elif kind == "oc":
                        ge, dc = ev[1], ev[2]
                        tt, e = ge // 8, ge % 8
                        vg = ge * 8 + dc
                        t.wait_ge(c_w2l[vg % 2].h, 16 * (vg // 2 + 1))
                        if dc == 0:
                            t.wait_ge(f_hw.h, 32 * (ge + 1))
                        if vg >= 1:
                            t.wait_ge(f_oe.h, vg)
                        first = True
                        if e == 0:
                            t.matmul(pO[:], cB2[:, dc * 128:(dc + 1) * 128],
                                     cWt[:, tt * 512:(tt + 1) * 512],
                                     start=True, stop=False)
                            first = False
                        for fc in range(32):
                            inst = t.matmul(pO[:],
                                            cW2[vg % 2][:, fc * 128:(fc + 1) * 128],
                                            cHW[:, fc * 512:(fc + 1) * 512],
                                            start=(first and fc == 0), stop=(fc == 31))
                        f_o.bump(inst, 1)
                    elif kind == "tr":
                        gi, half = ev[1], ev[2]
                        tt, i = gi // 4, gi % 4
                        tridx = gi * 2 + half
                        if i == 0 and half == 0:
                            t.wait_ge(f_oe.h, 64 * (tt + 1))
                            t.wait_ge(f_wb.h, 8 * (tt + 1))
                        t.wait_ge(f_t2c.h, tridx)
                        for dcq in range(4):
                            dc = half * 4 + dcq
                            inst = t.transpose(pT2m[:, dcq * 128:(dcq + 1) * 128],
                                               cAcc[:, dc * 512 + i * 128: dc * 512 + (i + 1) * 128],
                                               cIdF[:])
                        f_t2.bump(inst, 1)
                    elif kind == "bu":
                        k, u = ev[1], ev[2]
                        h = k // 16
                        hp2, a2, qt2 = h // 2, h % 2, k % 16
                        gu = k * 32 + u
                        if gu == 0:
                            t.wait_ge(r_wt.h, 8)
                            t.wait_ge(c_vo.h, 1)
                        if gu >= 2:
                            t.wait_ge(b_ex.h, gu - 1)
                        if u == 0:
                            t.wait_ge(c_qT[k % 4].h, 16 * (k // 4 + 1))
                            if qt2 >= 1:
                                t.wait_ge(b_ae.h, k)
                            elif h >= 1:
                                t.wait_ge(b_ae.h, 16 * (h - 1) if h >= 2 else 0)
                                if h >= 2:
                                    t.wait_ge(b_rc.h, 2 * (h - 1))
                        b_sc.bump(t.matmul(
                            pSc[u % 2][:],
                            KT[64 * a2:64 * a2 + 64,
                               hp2 * TKV + u * 128: hp2 * TKV + (u + 1) * 128],
                            bQt[64 * a2:64 * a2 + 64, (k % 4) * 512:(k % 4 + 1) * 512],
                            start=True, stop=True), 1)

                        def emit_av(um, stop):
                            gum = k * 32 + um
                            inst = t.matmul(
                                pAv[a2][0:65, :],
                                vAll[:, (h * 32 + um) * 65:(h * 32 + um + 1) * 65],
                                bEx[:, (gum % 4) * 512:(gum % 4 + 1) * 512],
                                start=(um == 0), stop=stop)
                            b_av.bump(inst, 1)
                        if u >= 2:
                            emit_av(u - 2, False)
                        if u == 31:
                            for um in (30, 31):
                                t.wait_ge(b_ex.h, k * 32 + um + 1)
                                emit_av(um, um == 31)
                    elif kind == "pdb":
                        h, half = ev[1], ev[2]
                        if half == 0:
                            t.wait_ge(c_d8.h, 16 * (h + 1))
                            t.wait_ge(b_ae.h, 16 * (h + 1))
                        else:
                            t.wait_ge(b_rc.h, 2 * h + 1)
                        b_dn.bump(t.matmul(pAv[h % 2][:], cRep8[:],
                                           bD8[:, half * 512:(half + 1) * 512],
                                           start=True, stop=True), 1)

            @blk.scalar
            def _(s):
                for ev in EV:
                    kind = ev[0]
                    if kind == "hc":
                        ge, fc = ev[1], ev[2]
                        e = ge % 8
                        gf = ge * 32 + fc
                        s.wait_ge(f_h.h, gf + 1)
                        if gf >= 2:
                            s.wait_ge(f_hw.h, gf - 1)
                        f_hr.bump(s.activation(cHR[:, (fc % 2) * 512:(fc % 2 + 1) * 512],
                                               pH[fc % 2][:], AF.Relu,
                                               bias=cB1[:, e * 32 + fc:e * 32 + fc + 1]), 1)
                    elif kind == "bu":
                        k, u = ev[1], ev[2]
                        gu = k * 32 + u
                        s.wait_ge(b_sc.h, gu + 1)
                        if gu >= 4:
                            s.wait_ge(b_av.h, gu - 3)
                        b_ex.bump(s.activation(bEx[:, (gu % 4) * 512:(gu % 4 + 1) * 512],
                                               pSc[u % 2][:], AF.Exp, scale=0.125), 1)
                    elif kind == "d8":
                        h = ev[1]
                        s.wait_ge(c_bnD[0].h, 128 * (h + 1))
                        s.wait_ge(c_bnD[1].h, 128 * (h + 1))
                        if h >= 1:
                            s.wait_ge(b_dn.h, 2 * h)
                        c_d8.bump(s.dma_start(
                            out=bD8[:], in_=d_bncr2[h % 2].rearrange("(j c) -> j c", j=8)), 16)
                    elif kind == "scr":
                        h = ev[1]
                        s.wait_ge(c_bnA[0].h, 128 * (h + 1))
                        s.wait_ge(c_bnA[1].h, 128 * (h + 1))
                        if h >= 1:
                            s.wait_ge(b_sg.h, 4 * h)
                            s.wait_ge(c_sp.h, A_SP + 16 * h)
                        c_scr.bump(s.dma_start(
                            out=bScr[:].rearrange("p (i c) -> p i c", i=4),
                            in_=d_bnc[64 * (h % 2):64 * (h % 2) + 64, :]
                                .rearrange("d (j c) -> (d j) c", j=8)
                                .rearrange("(i p) c -> p i c", p=128)), 16)
                    elif kind == "nrm":
                        h = ev[1]
                        s.wait_ge(c_xr.h, 16 * (h + 1))
                        for i in range(4):
                            ti = h * 4 + i
                            b_xn.bump(s.activation(bXr[:, i * 1024:(i + 1) * 1024],
                                                   bXr[:, i * 1024:(i + 1) * 1024],
                                                   AF.Identity, bias=stBL[:, ti:ti + 1],
                                                   scale=stRS[:, ti:ti + 1]), 1)
                    elif kind == "bnc":
                        k = ev[1]
                        h, qt2 = k // 16, k % 16
                        s.wait_ge(b_ae.h, k + 1)
                        if h >= 2:
                            s.wait_ge(c_d8.h, 16 * (h - 1))
                            s.wait_ge(c_scr.h, 16 * (h - 1))
                        c_bnA[k % 2].bump(s.dma_start(
                            out=d_bnc[64 * (h % 2):64 * (h % 2) + 64, qt2 * 512:(qt2 + 1) * 512],
                            in_=bAssy[0:64, (k % 2) * 512:(k % 2 + 1) * 512]), 16)
                        c_bnD[k % 2].bump(s.dma_start(
                            out=d_bncr2[h % 2:h % 2 + 1, qt2 * 512:(qt2 + 1) * 512],
                            in_=bAssy[64:65, (k % 2) * 512:(k % 2 + 1) * 512]), 16)
                    elif kind == "sst":
                        h = ev[1]
                        s.wait_ge(b_sg.h, 4 * (h + 1))
                        s.wait_ge(c_sp.h, A_SP + 16 * h)
                        s.dma_start(out=d_ss[h * 512:(h + 1) * 512, :].rearrange("(i p) c -> p i c", p=128),
                                    in_=bScr[:].rearrange("p (i c) -> p i c", i=4)).then_inc(c_sp.h, 16)
                        c_sp.n += 16
                s.wait_ge(b_ex.h, 4096)
                s.wait_ge(f_hr.h, 2048)
                s.wait_ge(b_xn.h, 32)
                s.wait_ge(c_sp.h, A_SP + 16 * 8)
                s.wait_ge(c_bnA[0].h, 16 * 64)
                s.wait_ge(c_bnA[1].h, 16 * 64)
                s.wait_ge(c_bnD[0].h, 16 * 64)
                s.wait_ge(c_bnD[1].h, 16 * 64)
                m_sd.bump(s.memzero(cDmy[:]), 1)

            @blk.vector
            def _(v):
                for ev in EV:
                    kind = ev[0]
                    if kind == "bc":
                        ge = ev[1]
                        v.wait_ge(t_bc.h, ge + 1)
                        if ge >= 2:
                            v.wait_ge(f_hw.h, 32 * (ge - 1))
                        f_wb.bump(v.tensor_copy(cWbc[:, (ge % 2) * 512:(ge % 2 + 1) * 512],
                                                pT2m[:]), 1)
                    elif kind == "hc":
                        ge, fc = ev[1], ev[2]
                        gf = ge * 32 + fc
                        v.wait_ge(f_hr.h, gf + 1)
                        if fc == 0:
                            v.wait_ge(f_wb.h, ge + 1)
                            if ge >= 1:
                                v.wait_ge(f_o.h, ge * 8)
                        f_hw.bump(v.tensor_tensor(cHW[:, fc * 512:(fc + 1) * 512],
                                                  cHR[:, (fc % 2) * 512:(fc % 2 + 1) * 512],
                                                  cWbc[:, (ge % 2) * 512:(ge % 2 + 1) * 512],
                                                  ALU.mult), 1)
                    elif kind == "oc":
                        ge, dc = ev[1], ev[2]
                        tt, e = ge // 8, ge % 8
                        vg = ge * 8 + dc
                        v.wait_ge(f_o.h, vg + 1)
                        if e == 0:
                            if dc == 0 and tt >= 1:
                                v.wait_ge(f_t2.h, 8 * tt)
                            inst = v.tensor_copy(cAcc[:, dc * 512:(dc + 1) * 512], pO[:])
                        else:
                            inst = v.tensor_tensor(cAcc[:, dc * 512:(dc + 1) * 512],
                                                   pO[:], cAcc[:, dc * 512:(dc + 1) * 512],
                                                   ALU.add)
                        f_oe.bump(inst, 1)
                    elif kind == "tr":
                        gi, half = ev[1], ev[2]
                        tridx = gi * 2 + half
                        v.wait_ge(f_t2.h, tridx + 1)
                        if gi >= 1 and half == 0:
                            v.wait_ge(c_t2s.h, 16 * gi)
                        f_t2c.bump(v.tensor_copy(sT2[:, half * 512:(half + 1) * 512],
                                                 pT2m[:]), 1)
                    elif kind == "bev":
                        k = ev[1]
                        a2 = (k // 16) % 2
                        v.wait_ge(b_av.h, 32 * (k + 1))
                        if k >= 2:
                            v.wait_ge(c_bnA[k % 2].h, 16 * (k // 2))
                            v.wait_ge(c_bnD[k % 2].h, 16 * (k // 2))
                        b_ae.bump(v.tensor_copy(bAssy[0:65, (k % 2) * 512:(k % 2 + 1) * 512],
                                                pAv[a2][0:65, :]), 1)
                    elif kind == "rcp":
                        h, half = ev[1], ev[2]
                        v.wait_ge(b_dn.h, 2 * h + half + 1)
                        with nc.allow_low_precision(reason="1/den in bf16 is within tolerance"):
                            b_rc.bump(v.reciprocal(bDiv[:, half * 512:(half + 1) * 512],
                                                   pAv[h % 2][:]), 1)
                    elif kind == "sadd":
                        h = ev[1]
                        v.wait_ge(b_rc.h, 2 * (h + 1))
                        v.wait_ge(c_scr.h, 16 * (h + 1))
                        for i in range(4):
                            vf.bump(v.tensor_tensor(bScr[:, i * 1024:(i + 1) * 1024],
                                                    bScr[:, i * 1024:(i + 1) * 1024],
                                                    bDiv[:], ALU.mult), 1)
                        v.wait_ge(vf.h, vf.n)
                        for i in range(4):
                            v.wait_ge(b_xn.h, 4 * h + i + 1)
                            b_sg.bump(v.tensor_tensor(bScr[:, i * 1024:(i + 1) * 1024],
                                                      bScr[:, i * 1024:(i + 1) * 1024],
                                                      bXr[:, i * 1024:(i + 1) * 1024],
                                                      ALU.add), 1)
                v.wait_ge(f_t2c.h, 64)
                v.wait_ge(b_sg.h, 32)
                v.wait_ge(f_oe.h, 512)
                v.wait_ge(b_ae.h, 128)
                v.wait_ge(vf.h, vf.n)
                m_vd.bump(v.engine_nop(), 1)

        assert b_sc.n == 4096 and b_ex.n == 4096 and b_av.n == 4096
        assert b_ae.n == 128 and c_bnA[0].n == 1024 and c_bnD[1].n == 1024
        assert b_dn.n == 16 and b_rc.n == 16 and b_xn.n == 32 and b_sg.n == 32
        assert f_h.n == 2048 and f_hr.n == 2048 and f_hw.n == 2048
        assert f_o.n == 512 and f_oe.n == 512
        assert t_bc.n == 64 and f_wb.n == 64 and f_t2.n == 64 and f_t2c.n == 64
        assert c_t2s.n == 32 * 16 and c_sp.n == A_SP + 128
        st_w2 = c_w1.n

        st_ab.close()

        with ExitStack() as stf:
            cGamF = stf.enter_context(nc.sbuf_tensor("cGamF", [128, D], F32))
            cBetF = stf.enter_context(nc.sbuf_tensor("cBetF", [128, D], F32))
            cS = stf.enter_context(nc.sbuf_tensor("cS", [128, 4 * 1024], BF16))
            cX = stf.enter_context(nc.sbuf_tensor("cX", [128, 4 * 1024], F32))
            cXn2 = stf.enter_context(nc.sbuf_tensor("cXn2", [128, 4 * 1024], F32))
            cT2 = stf.enter_context(nc.sbuf_tensor("cT2", [128, 4 * 1024], F32))
            cTh = stf.enter_context(nc.sbuf_tensor("cTh", [128, 4 * 1024], F32))
            cOut = stf.enter_context(nc.sbuf_tensor("cOut", [128, 4 * 1024], F32))
            nSm = stf.enter_context(nc.sbuf_tensor("nSm", [128, 4], F32))
            nNm = stf.enter_context(nc.sbuf_tensor("nNm", [128, 4], F32))
            nV2 = stf.enter_context(nc.sbuf_tensor("nV2", [128, 4], F32))
            nVar = stf.enter_context(nc.sbuf_tensor("nVar", [128, 4], F32))
            nRS = stf.enter_context(nc.sbuf_tensor("nRS", [128, 4], F32))
            nBL = stf.enter_context(nc.sbuf_tensor("nBL", [128, 4], F32))
            c_gb = sem("c_gb")

            @blk.sync
            def _(sync):
                sync.wait_ge(m_vd.h, 1)
                sync.wait_ge(m_sd.h, 1)
                sync.wait_ge(f_t2.h, 64)
                c_gb.bump(sync.dma_start(out=cGamF[:], in_=d_gam), 16)
                c_gb.bump(sync.dma_start(out=cBetF[:], in_=d_bet), 16)
                for gs in range(32):
                    sl = gs % 4
                    sync.wait_ge(c_t2s.h, 16 * (gs + 1))
                    if gs >= 4:
                        sync.wait_ge(f_fin.h, gs - 3)
                    sync.wait_ge(c_t2l[gs % 2].h, 16 * (gs // 2))
                    c_t2l[gs % 2].bump(sync.dma_start(
                        out=cT2[:, sl * 1024:(sl + 1) * 1024],
                        in_=d_t2[gs * 128:(gs + 1) * 128, :]), 16)
                sync.wait_ge(c_out[0].h, 16 * 16)
                sync.wait_ge(c_out[1].h, 16 * 16)

            @blk.gpsimd
            def _(g):
                g.wait_ge(m_vd.h, 1)
                g.wait_ge(m_sd.h, 1)
                g.wait_ge(f_t2.h, 64)
                for gs in range(32):
                    sl = gs % 4
                    tt = gs // 4
                    if gs >= 4:
                        g.wait_ge(f_fin.h, gs - 3)
                    g.wait_ge(c_sp.h, A_SP + 16 * (tt + 1))
                    g.wait_ge(c_cs[gs % 2].h, 16 * (gs // 2))
                    c_cs[gs % 2].bump(g.dma_start(
                        out=cS[:, sl * 1024:(sl + 1) * 1024],
                        in_=d_ss[gs * 128:(gs + 1) * 128, :]), 16)
                    g.wait_ge(c_cxx[gs % 2].h, 16 * (gs // 2))
                    c_cxx[gs % 2].bump(g.dma_start(
                        out=cX[:, sl * 1024:(sl + 1) * 1024],
                        in_=d_xo[gs * 128:(gs + 1) * 128, :]), 16)

            @blk.scalar
            def _(s):
                ln_cnt = {}

                def S(stage, gs):
                    sl = gs % 4
                    lo, hi = sl * 1024, (sl + 1) * 1024
                    if stage == 0:
                        s.wait_ge(c_cxx[gs % 2].h, 16 * (gs // 2 + 1))
                        if gs >= 4:
                            s.wait_ge(f_t2a.h, gs - 3)
                        s_xn2.bump(s.activation(cXn2[:, lo:hi], cX[:, lo:hi],
                                                AF.Identity, bias=stBL[:, gs:gs + 1],
                                                scale=stRS[:, gs:gs + 1]), 1)
                    elif stage == 1:
                        s.wait_ge(f_t2a.h, gs + 1)
                        s.wait_ge(n_nm.h, gs + 1)
                        if gs >= 4:
                            s.wait_ge(f_fin.h, gs - 3)
                        s_sq.bump(s.activation(cTh[:, lo:hi], cT2[:, lo:hi],
                                               AF.Square, bias=nNm[:, sl:sl + 1]), 1)
                    elif stage == 2:
                        s.wait_ge(n_var.h, gs + 1)
                        ln_cnt[gs] = sf.bump(s.activation(nRS[:, sl:sl + 1],
                                                          nVar[:, sl:sl + 1], AF.Ln), 1)
                    elif stage == 3:
                        s.wait_ge(sf.h, ln_cnt[gs])
                        s_rs.bump(s.activation(nRS[:, sl:sl + 1], nRS[:, sl:sl + 1],
                                               AF.Exp, scale=-0.5), 1)
                    elif stage == 4:
                        s.wait_ge(f_bl.h, gs + 1)
                        s_A.bump(s.activation(cT2[:, lo:hi], cT2[:, lo:hi],
                                              AF.Identity, bias=nBL[:, sl:sl + 1],
                                              scale=nRS[:, sl:sl + 1]), 1)
                    elif stage == 5:
                        s.wait_ge(f_y.h, gs + 1)
                        s_tanh.bump(s.activation(cTh[:, lo:hi], cT2[:, lo:hi],
                                                 AF.Tanh, scale=0.5), 1)
                    elif stage == 6:
                        s.wait_ge(f_fin.h, gs + 1)
                        s.wait_ge(c_out[gs % 2].h, 16 * (gs // 2))
                        c_out[gs % 2].bump(s.dma_start(
                            out=d_y[gs * 128:(gs + 1) * 128, :],
                            in_=cOut[:, lo:hi]), 16)

                for p in range(16):
                    for stage in range(7):
                        for j in range(2):
                            S(stage, 2 * p + j)

            @blk.vector
            def _(v):
                vfc = {}

                def V(stage, gs):
                    sl = gs % 4
                    lo, hi = sl * 1024, (sl + 1) * 1024
                    if stage == 0:
                        v.wait_ge(c_t2l[gs % 2].h, 16 * (gs // 2 + 1))
                        v.wait_ge(s_xn2.h, gs + 1)
                        v.wait_ge(c_cs[gs % 2].h, 16 * (gs // 2 + 1))
                        v.wait_ge(c_cxx[gs % 2].h, 16 * (gs // 2 + 1))
                        if gs >= 4:
                            v.wait_ge(s_tanh.h, gs - 3)
                        f_t2a.bump(v.tensor_tensor(cT2[:, lo:hi], cT2[:, lo:hi],
                                                   cXn2[:, lo:hi], ALU.add), 1)
                    elif stage == 1:
                        v.wait_ge(f_t2a.h, gs + 1)
                        vfc[(gs, 0)] = vf.bump(v.reduce_sum(nSm[:, sl:sl + 1],
                                                            cT2[:, lo:hi], axis=AX.X), 1)
                    elif stage == 2:
                        v.wait_ge(vf.h, vfc[(gs, 0)])
                        n_nm.bump(v.tensor_scalar(nNm[:, sl:sl + 1], nSm[:, sl:sl + 1],
                                                  -1.0 / 1024, None, ALU.mult), 1)
                    elif stage == 3:
                        v.wait_ge(s_sq.h, gs + 1)
                        vfc[(gs, 1)] = vf.bump(v.reduce_sum(nV2[:, sl:sl + 1],
                                                            cTh[:, lo:hi], axis=AX.X), 1)
                    elif stage == 4:
                        v.wait_ge(vf.h, vfc[(gs, 1)])
                        n_var.bump(v.tensor_scalar(nVar[:, sl:sl + 1], nV2[:, sl:sl + 1],
                                                   1.0 / 1024, 1e-5, ALU.mult, ALU.add), 1)
                    elif stage == 5:
                        v.wait_ge(s_rs.h, gs + 1)
                        v.wait_ge(n_nm.h, gs + 1)
                        f_bl.bump(v.tensor_tensor(nBL[:, sl:sl + 1], nNm[:, sl:sl + 1],
                                                  nRS[:, sl:sl + 1], ALU.mult), 1)
                    elif stage == 6:
                        v.wait_ge(s_A.h, gs + 1)
                        vfc[(gs, 2)] = vf.bump(v.tensor_tensor(cT2[:, lo:hi], cT2[:, lo:hi],
                                                               cGamF[:], ALU.mult), 1)
                    elif stage == 7:
                        v.wait_ge(vf.h, vfc[(gs, 2)])
                        f_y.bump(v.tensor_tensor(cT2[:, lo:hi], cT2[:, lo:hi],
                                                 cBetF[:], ALU.add), 1)
                    elif stage == 8:
                        v.wait_ge(s_tanh.h, gs + 1)
                        vfc[(gs, 3)] = vf.bump(v.tensor_scalar(cTh[:, lo:hi], cTh[:, lo:hi],
                                                               0.5, 0.5, ALU.mult, ALU.add), 1)
                    elif stage == 9:
                        v.wait_ge(vf.h, vfc[(gs, 3)])
                        v.wait_ge(f_y.h, gs + 1)
                        vfc[(gs, 4)] = vf.bump(v.tensor_tensor(cTh[:, lo:hi], cT2[:, lo:hi],
                                                               cTh[:, lo:hi], ALU.mult), 1)
                    elif stage == 10:
                        v.wait_ge(vf.h, vfc[(gs, 4)])
                        vfc[(gs, 5)] = vf.bump(v.tensor_tensor(cTh[:, lo:hi], cS[:, lo:hi],
                                                               cTh[:, lo:hi], ALU.mult), 1)
                    elif stage == 11:
                        v.wait_ge(vf.h, vfc[(gs, 5)])
                        if gs >= 4:
                            v.wait_ge(c_out[gs % 2].h, 16 * (gs // 2 - 1))
                        f_fin.bump(v.tensor_tensor(cOut[:, lo:hi], cX[:, lo:hi],
                                                   cTh[:, lo:hi], ALU.add), 1)

                v.wait_ge(c_gb.h, 32)
                for p in range(16):
                    for stage in range(12):
                        for j in range(2):
                            V(stage, 2 * p + j)

        assert f_fin.n == 32
        assert c_out[0].n == 16 * 16 and c_out[1].n == 16 * 16

    return nc


def _prep_maps(inputs):
    x = np.ascontiguousarray(np.asarray(inputs["L0_attn_queries"]), dtype=np.float32)
    keys = np.ascontiguousarray(np.asarray(inputs["L0_attn_keys"])[0], dtype=np.float32)
    values = np.ascontiguousarray(np.asarray(inputs["L0_attn_values"])[0], dtype=np.float32)
    Wq = np.asarray(inputs["L0_attn_Wq_w"], dtype=np.float32)
    bq = np.asarray(inputs["L0_attn_Wq_b"], dtype=np.float32)
    Wk = np.asarray(inputs["L0_attn_Wk_w"], dtype=np.float32)
    bk = np.asarray(inputs["L0_attn_Wk_b"], dtype=np.float32)
    Wv = np.asarray(inputs["L0_attn_Wv_w"], dtype=np.float32)
    bv = np.asarray(inputs["L0_attn_Wv_b"], dtype=np.float32)
    Wr = np.asarray(inputs["L0_ffn_Wr"], dtype=np.float32)
    br = np.asarray(inputs["L0_ffn_br"], dtype=np.float32)
    W1 = np.asarray(inputs["L0_ffn_W1"], dtype=np.float32)
    b1 = np.asarray(inputs["L0_ffn_b1"], dtype=np.float32)
    W2 = np.asarray(inputs["L0_ffn_W2"], dtype=np.float32)
    b2 = np.asarray(inputs["L0_ffn_b2"], dtype=np.float32)
    gam = np.asarray(inputs["L0_ffn_gamma"], dtype=np.float32)
    bet = np.asarray(inputs["L0_ffn_beta"], dtype=np.float32)

    keysT = np.ascontiguousarray(keys.T)
    valuesT = np.ascontiguousarray(values.T)
    w1t = np.ascontiguousarray(
        W1.reshape(E, 8, 4, 128, 8, 128).transpose(0, 1, 5, 2, 4, 3).reshape(E, 8, 128, 4096))
    w2t = np.ascontiguousarray(W2.reshape(E, 8, 128, 32, 128).transpose(0, 1, 4, 3, 2))
    wrT = np.ascontiguousarray(Wr.T.reshape(8, 128, 8).transpose(1, 0, 2))
    b1t = np.ascontiguousarray(b1.reshape(E, 32, 128).transpose(2, 0, 1))
    brf = np.ascontiguousarray(br.reshape(1, 8))
    gamb = np.ascontiguousarray(np.broadcast_to(gam, (128, D)))
    betb = np.ascontiguousarray(np.broadcast_to(bet, (128, D)))
    idf = np.eye(128, dtype=np.float32)
    ones1f = np.ones((1, 128), dtype=np.float32)
    ones1b = np.ones((1, 128), dtype=BF)
    onehot = np.zeros((8, 8 * 128), dtype=BF)
    for e in range(8):
        onehot[e, e * 128:(e + 1) * 128] = 1.0
    rep8 = np.zeros((8, 128), dtype=BF)
    for m in range(128):
        rep8[m % 8, m] = 1.0

    maps = []
    for c in range(8):
        b, half = c // 2, c % 2
        cols = slice(half * 512, (half + 1) * 512)
        m = dict(
            x=x[b],
            xo=np.ascontiguousarray(x[b, half * 4096:(half + 1) * 4096]),
            keysT=keysT, valuesT=valuesT,
            wqT=np.ascontiguousarray(Wq.T[:, cols].reshape(8, 128, 4, 128).transpose(1, 0, 2, 3)),
            wkT=np.ascontiguousarray(Wk.T[:, cols].reshape(8, 128, 4, 128).transpose(1, 0, 2, 3)),
            wvT=np.ascontiguousarray(Wv.T[:, cols].reshape(8, 128, 4, 128).transpose(1, 0, 2, 3)),
            bq=np.ascontiguousarray(bq[cols].reshape(4, 128).T),
            bk=np.ascontiguousarray(bk[cols].reshape(4, 128).T),
            bvr=np.ascontiguousarray(bv[cols].reshape(1, 512)),
            wrT=wrT, br=brf, w1t=w1t, w2t=w2t, b1=b1t, b2=np.ascontiguousarray(b2),
            gam=gamb, bet=betb, idf=idf, ones1f=ones1f, ones1b=ones1b,
            onehot=onehot, rep8=rep8,
        )
        maps.append(m)
    return maps


_DT_EXPECT = dict(x="float32", xo="float32", keysT="float32", valuesT="float32",
                  wqT="float32", wkT="float32", wvT="float32", bq="float32",
                  bk="float32", bvr="float32", wrT="float32", br="float32",
                  w1t="float32", w2t="float32", b1="float32", b2="float32",
                  gam="float32", bet="float32", idf="float32", ones1f="float32",
                  ones1b="bfloat16", onehot="bfloat16", rep8="bfloat16")


def kernel(**inputs):
    maps = _prep_maps(inputs)
    for m in maps:
        for k, v in m.items():
            assert str(v.dtype) == _DT_EXPECT[k], (k, v.dtype)
    nc = build_program()
    res = run_bass_kernel_spmd(nc, maps, core_ids=list(range(8)))
    out = np.empty((4, S, D), dtype=np.float32)
    for c in range(8):
        b, half = c // 2, c % 2
        out[b, half * 4096:(half + 1) * 4096] = np.asarray(res.results[c]["y"])
    if _DBG:
        kernel._last = res
    return out


# version 21
# speedup vs baseline: 1.1173x; 1.0034x over previous
import numpy as np
import ml_dtypes
from contextlib import ExitStack

import concourse.bass as bass
import concourse.mybir as mybir
from concourse.bass_utils import run_bass_kernel_spmd

F32 = mybir.dt.float32
BF16 = mybir.dt.bfloat16
AF = mybir.ActivationFunctionType
ALU = mybir.AluOpType
AX = mybir.AxisListType
BF = ml_dtypes.bfloat16

_DBG = False

S = 8192
D = 1024
TKV = 4096
NGRP = 16
NOG = 8
E = 8
NTT = 8
NQT = 16
NGR = 16


class Ctr:
    def __init__(self, stack, nc, name):
        self.h = stack.enter_context(nc.semaphore(name))
        self.n = 0
    def bump(self, inst, k=1):
        inst.then_inc(self.h, k)
        self.n += k
        return self.n


def build_program():
    nc = bass.Bass("TRN2", target_bir_lowering=False)
    DT = nc.dram_tensor
    sk = "ExternalOutput" if _DBG else "Internal"

    d_x = DT("x", [S, D], F32, kind="ExternalInput").ap()
    d_xo = DT("xo", [4096, D], F32, kind="ExternalInput").ap()
    d_keysT = DT("keysT", [D, TKV], F32, kind="ExternalInput").ap()
    d_valuesT = DT("valuesT", [D, TKV], F32, kind="ExternalInput").ap()
    d_wqT = DT("wqT", [128, 8, 4, 128], F32, kind="ExternalInput").ap()
    d_wkT = DT("wkT", [128, 8, 4, 128], F32, kind="ExternalInput").ap()
    d_wvT = DT("wvT", [128, 8, 4, 128], F32, kind="ExternalInput").ap()
    d_bq = DT("bq", [128, 4], F32, kind="ExternalInput").ap()
    d_bk = DT("bk", [128, 4], F32, kind="ExternalInput").ap()
    d_bvr = DT("bvr", [1, 512], F32, kind="ExternalInput").ap()
    d_wrT = DT("wrT", [128, 8, 8], F32, kind="ExternalInput").ap()
    d_br = DT("br", [1, 8], F32, kind="ExternalInput").ap()
    d_w1t = DT("w1t", [E, 8, 128, 4096], F32, kind="ExternalInput").ap()
    d_w2t = DT("w2t", [E, 8, 128, 32, 128], F32, kind="ExternalInput").ap()
    d_b1 = DT("b1", [128, E, 32], F32, kind="ExternalInput").ap()
    d_b2 = DT("b2", [8, D], F32, kind="ExternalInput").ap()
    d_gam = DT("gam", [128, D], F32, kind="ExternalInput").ap()
    d_bet = DT("bet", [128, D], F32, kind="ExternalInput").ap()
    d_idf = DT("idf", [128, 128], F32, kind="ExternalInput").ap()
    d_ones1f = DT("ones1f", [1, 128], F32, kind="ExternalInput").ap()
    d_ones1b = DT("ones1b", [1, 128], BF16, kind="ExternalInput").ap()
    d_onehot = DT("onehot", [8, 8 * 128], BF16, kind="ExternalInput").ap()
    d_rep8 = DT("rep8", [8, 128], BF16, kind="ExternalInput").ap()

    d_y = DT("y", [4096, D], F32, kind="ExternalOutput").ap()
    d_t2 = DT("t2", [4096, D], F32, kind="Internal").ap()
    d_qts = DT("qts", [4, 128, S], BF16, kind=sk).ap()
    d_xnts = DT("xnts", [128, 8, 4096], BF16, kind=sk).ap()
    d_ss = DT("ss", [4096, D], BF16, kind=sk).ap()
    d_w1b = DT("w1b", [E, 8, 128, 4096], BF16, kind="Internal").ap()
    d_w2b = DT("w2b", [E, 8, 128, 32, 128], BF16, kind="Internal").ap()
    d_bnc = DT("bnc", [128, S], BF16, kind="Internal").ap()
    d_bncr2 = DT("bncr2", [2, S], BF16, kind="Internal").ap()

    with ExitStack() as st0:
        def sb(name, shape, dt):
            return st0.enter_context(nc.sbuf_tensor(name, shape, dt))
        def sem(name):
            return Ctr(st0, nc, name)

        cIdF = sb("cIdF", [128, 128], F32)
        cOn1f = sb("cOn1f", [1, 128], F32)
        cOn1b = sb("cOn1b", [1, 128], BF16)
        cOneHot = sb("cOneHot", [8, 8 * 128], BF16)
        cRep8 = sb("cRep8", [8, 128], BF16)
        cWrT = sb("cWrT", [128, 8 * 8], F32)
        cBr = sb("cBr", [1, 8], F32)
        cBq = sb("cBq", [128, 4], F32)
        cBk = sb("cBk", [128, 4], F32)
        cBvR = sb("cBvR", [1, 512], BF16)
        cB1 = sb("cB1", [128, E * 32], F32)
        cB2 = sb("cB2", [8, D], BF16)
        stRS = sb("stRS", [128, 32], F32)
        stBL = sb("stBL", [128, 32], F32)
        stLG = sb("stLG", [128, 32 * 8], F32)
        cDmy = sb("cDmy", [1, 4], F32)
        st_ab = ExitStack()
        KT = st_ab.enter_context(nc.sbuf_tensor("KT", [128, 4 * TKV], BF16))
        vAll = st_ab.enter_context(nc.sbuf_tensor("vAll", [128, 8 * 32 * 65], BF16))
        cWt = st_ab.enter_context(nc.sbuf_tensor("cWt", [8, 8 * 512], BF16))
        st_w = ExitStack()
        cWq = st_w.enter_context(nc.sbuf_tensor("cWq", [128, 8 * 4 * 128], BF16))
        cWk = st_w.enter_context(nc.sbuf_tensor("cWk", [128, 8 * 4 * 128], BF16))
        cWv = st_w.enter_context(nc.sbuf_tensor("cWv", [128, 8 * 4 * 128], BF16))

        c_const = sem("c_const")
        c_constg = sem("c_constg")
        c_kv = sem("c_kv")
        c_x = [sem("c_x0"), sem("c_x1")]
        c_sp = sem("c_sp")
        c_sq = [sem("c_sq0"), sem("c_sq1")]
        c_qT = [sem("c_qT0"), sem("c_qT1"), sem("c_qT2"), sem("c_qT3")]
        c_xr = sem("c_xr"); c_d8 = sem("c_d8"); c_scr = sem("c_scr")
        c_w1 = sem("c_w1")
        c_w2 = sem("c_w2")
        c_w1l = [sem("c_w1l0"), sem("c_w1l1"), sem("c_w1l2")]
        c_w2l = [sem("c_w2l0"), sem("c_w2l1"), sem("c_w2l2")]
        c_cxn = [sem("c_cxn0"), sem("c_cxn1")]
        c_cs = [sem("c_cs0"), sem("c_cs1")]
        c_cxx = [sem("c_cxx0"), sem("c_cxx1")]
        c_out = [sem("c_out0"), sem("c_out1")]
        k_mm = sem("k_mm"); k_ev = sem("k_ev")
        a_mu = sem("a_mu"); a_sq = sem("a_sq"); a_var = sem("a_var")
        a_rs = sem("a_rs"); a_bl = sem("a_bl"); a_xn = sem("a_xn")
        a_tr = sem("a_tr"); a_ev = sem("a_ev"); a_q = sem("a_q")
        a_qe = sem("a_qe"); a_lg = sem("a_lg"); a_lgs = sem("a_lgs")
        a_bar = sem("a_bar")
        c_bn = sem("c_bn")
        b_sc = sem("b_sc"); b_ex = sem("b_ex"); b_av = sem("b_av")
        b_ae = sem("b_ae"); b_ve = sem("b_ve"); b_dn = sem("b_dn")
        b_rc = sem("b_rc"); b_xn = sem("b_xn"); b_sg = sem("b_sg")
        f_h = sem("f_h"); f_hr = sem("f_hr"); f_hw = sem("f_hw")
        f_o = sem("f_o"); f_oe = sem("f_oe")
        t_wb = sem("t_wb"); f_wb = sem("f_wb")
        r_w = sem("r_w"); r_t = sem("r_t"); r_wt = sem("r_wt")
        s_rex = sem("s_rex"); s_xn2 = sem("s_xn2")
        n_nm = sem("n_nm"); n_var = sem("n_var")
        s_sq = sem("s_sq"); s_rs = sem("s_rs"); s_A = sem("s_A"); s_tanh = sem("s_tanh")
        vf = sem("vf"); sf = sem("sf")
        f_t2 = sem("f_t2"); f_t2a = sem("f_t2a"); f_bl = sem("f_bl")
        f_vf = sem("f_vf")
        f_y = sem("f_y"); f_fin = sem("f_fin")
        c_vo = sem("c_vo"); t_bc = sem("t_bc"); f_t2c = sem("f_t2c")
        c_t2s = sem("c_t2s"); c_t2l = [sem("c_t2l0"), sem("c_t2l1")]
        c_bnA = [sem("c_bnA0"), sem("c_bnA1")]
        c_bnD = [sem("c_bnD0"), sem("c_bnD1")]
        m_vd = sem("m_vd"); m_sd = sem("m_sd")

        blk = st0.enter_context(nc.Block())

        @blk.sync
        def _(sync):
            for dst, src in [(cIdF, d_idf), (cOn1f, d_ones1f), (cOn1b, d_ones1b),
                             (cOneHot, d_onehot), (cRep8, d_rep8), (cBr, d_br),
                             (cBq, d_bq), (cBk, d_bk)]:
                c_const.bump(sync.dma_start(out=dst[:], in_=src), 16)
            c_const.bump(sync.dma_start(
                out=cWrT[:].rearrange("p (a b) -> p a b", a=8), in_=d_wrT), 16)
            c_const.bump(sync.dma_start(
                out=cB1[:].rearrange("p (a b) -> p a b", a=E), in_=d_b1), 16)

        @blk.gpsimd
        def _(g):
            for dst, src in [(cWq, d_wqT), (cWk, d_wkT), (cWv, d_wvT)]:
                for half in range(2):
                    c_constg.bump(g.dma_start(
                        out=dst[:, half * 2048:(half + 1) * 2048]
                            .rearrange("p (a b c) -> p a b c", a=4, b=4),
                        in_=src[:, half * 4:(half + 1) * 4]), 16)
            c_constg.bump(g.dma_start(out=cBvR[:], in_=d_bvr), 16)
            c_constg.bump(g.dma_start(out=cB2[:], in_=d_b2), 16)
        N_CONST = 10

        @blk.vector
        def _(v):
            c_vo.bump(v.memset(
                vAll[:].rearrange("p (h t e) -> p h t e", h=8, t=32)[:, :, :, 64:65], 1.0), 1)

        with ExitStack() as stkv:
            kvK = [stkv.enter_context(nc.sbuf_tensor(f"kvK{i}", [128, 8 * 512], BF16)) for i in range(2)]
            kvV = [stkv.enter_context(nc.sbuf_tensor(f"kvV{i}", [128, 8 * 512], BF16)) for i in range(2)]
            pKV = [stkv.enter_context(nc.psum_tensor(f"pKV{i}", [128, 512], F32)) for i in range(4)]

            @blk.gpsimd
            def _(g):
                for tg in range(8):
                    if tg >= 2:
                        g.wait_ge(k_mm.h, 8 * (tg - 1))
                    g.wait_ge(c_kv.h, c_kv.n)
                    for half in range(2):
                        g.dma_start(
                            out=kvK[tg % 2][:, half * 2048:(half + 1) * 2048]
                                .rearrange("p (a b) -> p a b", a=4),
                            in_=d_keysT[half * 512:(half + 1) * 512, tg * 512:(tg + 1) * 512]
                                .rearrange("(a p) t -> p a t", p=128)
                        ).then_inc(c_kv.h, 16); c_kv.n += 16
                    g.wait_ge(c_kv.h, c_kv.n)
                    for half in range(2):
                        g.dma_start(
                            out=kvV[tg % 2][:, half * 2048:(half + 1) * 2048]
                                .rearrange("p (a b) -> p a b", a=4),
                            in_=d_valuesT[half * 512:(half + 1) * 512, tg * 512:(tg + 1) * 512]
                                .rearrange("(a p) t -> p a t", p=128)
                        ).then_inc(c_kv.h, 16); c_kv.n += 16

            @blk.tensor
            def _(t):
                t.wait_ge(c_const.h, 16 * N_CONST)
                t.wait_ge(c_constg.h, 16 * 8)
                for tg in range(8):
                    t.wait_ge(c_kv.h, 64 * (tg + 1))
                    kb, vb = kvK[tg % 2], kvV[tg % 2]
                    for dkc in range(4):
                        if dkc >= 2:
                            t.wait_ge(k_ev.h, 8 * tg + dkc - 1)
                        elif tg >= 1:
                            t.wait_ge(k_ev.h, 8 * tg + dkc - 5)
                        for dc in range(8):
                            inst = t.matmul(pKV[dkc % 2][:],
                                            cWk[:, (dc * 4 + dkc) * 128:(dc * 4 + dkc + 1) * 128],
                                            kb[:, dc * 512:(dc + 1) * 512],
                                            start=(dc == 0), stop=(dc == 7))
                        k_mm.bump(inst, 1)
                    for tcl in range(4):
                        if tcl >= 2:
                            t.wait_ge(k_ev.h, 8 * tg + tcl + 3)
                        elif tg >= 1:
                            t.wait_ge(k_ev.h, 8 * tg + tcl - 1)
                        for dc in range(8):
                            t.matmul(pKV[2 + tcl % 2][:],
                                     vb[:, dc * 512 + tcl * 128: dc * 512 + (tcl + 1) * 128],
                                     cWv[:, dc * 512:(dc + 1) * 512],
                                     start=(dc == 0), stop=False)
                        inst = t.matmul(pKV[2 + tcl % 2][:], cOn1b[:], cBvR[:],
                                        start=False, stop=True)
                        k_mm.bump(inst, 1)

            @blk.scalar
            def _(s):
                for tg in range(8):
                    for dkc in range(4):
                        s.wait_ge(k_mm.h, 8 * tg + dkc + 1)
                        inst = s.activation(KT[:, dkc * TKV + tg * 512: dkc * TKV + (tg + 1) * 512],
                                            pKV[dkc % 2][:], AF.Relu, bias=cBk[:, dkc:dkc + 1])
                        k_ev.bump(inst, 1)
                    for tcl in range(4):
                        s.wait_ge(k_mm.h, 8 * tg + 4 + tcl + 1)
                        tc = tg * 4 + tcl
                        inst = s.activation(
                            vAll[:].rearrange("p (h m) -> p h m", h=8)[:, :, tc * 65:tc * 65 + 64],
                            pKV[2 + tcl % 2][:].rearrange("p (h d) -> p h d", h=8),
                            AF.Relu)
                        k_ev.bump(inst, 1)

        assert k_mm.n == 64 and k_ev.n == 64


        def phaseA(tag, src_ap, ngrp, own):
            base_bar = a_bar.n
            base_mu, base_sq, base_var = a_mu.n, a_sq.n, a_var.n
            base_rs, base_bl, base_xn = a_rs.n, a_bl.n, a_xn.n
            base_tr, base_ev, base_q = a_tr.n, a_ev.n, a_q.n
            base_qe, base_lg = a_qe.n, a_lg.n
            base_lgs = a_lgs.n
            base_cx = [c_x[0].n, c_x[1].n]
            base_sp = c_sp.n
            base_cq = [c_sq[0].n, c_sq[1].n]
            BAR = 2
            tprog = (a_lg if own else a_q)
            base_tp = tprog.n
            EVB = 8 if own else 4
            EVG = 8 * EVB

            with ExitStack() as sta:
                aX = [sta.enter_context(nc.sbuf_tensor(f"aX{tag}{i}", [128, 4 * 1024], F32)) for i in range(2)]
                aSq = sta.enter_context(nc.sbuf_tensor(f"aSq{tag}", [128, 4 * 1024], F32))
                aXn = sta.enter_context(nc.sbuf_tensor(f"aXn{tag}", [128, 4 * 1024], F32))
                aSm = sta.enter_context(nc.sbuf_tensor(f"aSm{tag}", [128, 4], F32))
                aNM = sta.enter_context(nc.sbuf_tensor(f"aNM{tag}", [128, 4], F32))
                aVar = sta.enter_context(nc.sbuf_tensor(f"aVar{tag}", [128, 4], F32))
                aRS = sta.enter_context(nc.sbuf_tensor(f"aRS{tag}", [128, 4], F32))
                aBL = sta.enter_context(nc.sbuf_tensor(f"aBL{tag}", [128, 4], F32))
                aXnTs = [sta.enter_context(nc.sbuf_tensor(f"aXnT{tag}{i}", [128, 8 * 512], BF16))
                         for i in range(2 if own else 1)]
                aXnT32 = sta.enter_context(nc.sbuf_tensor(f"aXnT32{tag}", [128, 8 * 512], F32)) if own else None
                aQe = [sta.enter_context(nc.sbuf_tensor(f"aQe{tag}{i}", [128, 4 * 512], BF16))
                       for i in range(2)] if not own else None
                pTr = [sta.enter_context(nc.psum_tensor(f"pTr{tag}{i}", [128, 512], F32)) for i in range(2)]
                pQL = [sta.enter_context(nc.psum_tensor(f"pQL{tag}{i}", [128, 512], F32)) for i in range(4)]

                @blk.sync
                def _(sync):
                    for g in range(ngrp):
                        if g == 0:
                            sync.wait_ge(k_ev.h, 64)
                            sync.wait_ge(a_bar.h, base_bar)
                            sync.wait_ge(a_q.h, base_q)
                            sync.wait_ge(c_sq[0].h, base_cq[0])
                            sync.wait_ge(c_sq[1].h, base_cq[1])
                        if g >= 2:
                            sync.wait_ge(a_bar.h, base_bar + BAR * (g - 1))
                            sync.wait_ge(tprog.h, base_tp + 4 * (g - 1))
                        for i in range(4):
                            c_x[g % 2].bump(sync.dma_start(
                                out=aX[g % 2][:, i * 1024:(i + 1) * 1024],
                                in_=src_ap[g * 512 + i * 128: g * 512 + (i + 1) * 128, :]
                            ), 16)

                @blk.vector
                def _(v):
                    for g in range(ngrp):
                        if g >= 1:
                            v.wait_ge(a_bar.h, base_bar + BAR * g)
                            v.wait_ge(tprog.h, base_tp + 4 * g)
                        xb = aX[g % 2]
                        v.wait_ge(c_x[g % 2].h, base_cx[g % 2] + 64 * (g // 2 + 1))
                        for i in range(4):
                            vf.bump(v.reduce_sum(aSm[:, i:i + 1], xb[:, i * 1024:(i + 1) * 1024], axis=AX.X), 1)
                        v.wait_ge(vf.h, vf.n)
                        for i in range(4):
                            inst = v.tensor_scalar(aNM[:, i:i + 1], aSm[:, i:i + 1],
                                                   -1.0 / 1024, None, ALU.mult)
                            a_mu.bump(inst, 1)
                        for i in range(4):
                            v.wait_ge(a_sq.h, base_sq + 4 * g + i + 1)
                            vf.bump(v.reduce_sum(aVar[:, i:i + 1], aSq[:, i * 1024:(i + 1) * 1024], axis=AX.X), 1)
                        v.wait_ge(vf.h, vf.n)
                        for i in range(4):
                            inst = v.tensor_scalar(aVar[:, i:i + 1], aVar[:, i:i + 1],
                                                   1.0 / 1024, 1e-5, ALU.mult, ALU.add)
                            a_var.bump(inst, 1)
                        v.wait_ge(a_mu.h, base_mu + 4 * g + 4)
                        for i in range(4):
                            v.wait_ge(a_rs.h, base_rs + 4 * g + i + 1)
                            inst = v.tensor_tensor(aBL[:, i:i + 1], aNM[:, i:i + 1],
                                                   aRS[:, i:i + 1], ALU.mult)
                            a_bl.bump(inst, 1)
                        if own:
                            v.wait_ge(a_bl.h, base_bl + 4 * g + 4)
                            vf.bump(v.tensor_copy(stRS[:, 4 * g:4 * g + 4], aRS[:]), 1)
                            vf.bump(v.tensor_copy(stBL[:, 4 * g:4 * g + 4], aBL[:]), 1)
                        if own and g >= 2:
                            v.wait_ge(c_sq[g % 2].h, base_cq[g % 2] + 64 * (g // 2))
                        xnt = aXnTs[g % 2] if own else aXnTs[0]
                        for i in range(4):
                            for bank in range(2):
                                v.wait_ge(a_tr.h, base_tr + 8 * g + 2 * i + bank + 1)
                                p = pTr[bank]
                                for dcq in range(4):
                                    dc = bank * 4 + dcq
                                    a_ev.bump(v.tensor_copy(
                                        xnt[:, dc * 512 + i * 128: dc * 512 + (i + 1) * 128],
                                        p[:, dcq * 128:(dcq + 1) * 128]), 1)
                                    if own:
                                        a_ev.bump(v.tensor_copy(
                                            aXnT32[:, dc * 512 + i * 128: dc * 512 + (i + 1) * 128],
                                            p[:, dcq * 128:(dcq + 1) * 128]), 1)
                        if own:
                            for i in range(4):
                                v.wait_ge(a_lg.h, base_lg + 4 * g + i + 1)
                                inst = v.tensor_copy(stLG[:, (4 * g + i) * 8:(4 * g + i + 1) * 8],
                                                     pQL[i][:, 0:8])
                                a_lgs.bump(inst, 1)
                        v.wait_ge(a_ev.h, base_ev + EVG * (g + 1))
                        if own:
                            v.wait_ge(a_lgs.h, base_lgs + 4 * (g + 1))
                        v.wait_ge(vf.h, vf.n)
                        inst = v.engine_nop()
                        a_bar.bump(inst, 1)

                @blk.scalar
                def _(s):
                    for g in range(ngrp):
                        if g >= 1:
                            s.wait_ge(a_bar.h, base_bar + BAR * g)
                            s.wait_ge(tprog.h, base_tp + 4 * g)
                        xb = aX[g % 2]
                        for i in range(4):
                            s.wait_ge(a_mu.h, base_mu + 4 * g + i + 1)
                            inst = s.activation(aSq[:, i * 1024:(i + 1) * 1024],
                                                xb[:, i * 1024:(i + 1) * 1024],
                                                AF.Square, bias=aNM[:, i:i + 1])
                            a_sq.bump(inst, 1)
                        for i in range(4):
                            s.wait_ge(a_var.h, base_var + 4 * g + i + 1)
                            sf.bump(s.activation(aRS[:, i:i + 1], aVar[:, i:i + 1], AF.Ln), 1)
                        s.wait_ge(sf.h, sf.n)
                        for i in range(4):
                            inst = s.activation(aRS[:, i:i + 1], aRS[:, i:i + 1], AF.Exp, scale=-0.5)
                            a_rs.bump(inst, 1)
                        for i in range(4):
                            s.wait_ge(a_bl.h, base_bl + 4 * g + i + 1)
                            inst = s.activation(aXn[:, i * 1024:(i + 1) * 1024],
                                                xb[:, i * 1024:(i + 1) * 1024],
                                                AF.Identity, bias=aBL[:, i:i + 1], scale=aRS[:, i:i + 1])
                            a_xn.bump(inst, 1)
                        if not own:
                            if g >= 2:
                                s.wait_ge(c_sq[g % 2].h, base_cq[g % 2] + 64 * (g // 2))
                            for dqc in range(4):
                                s.wait_ge(a_q.h, base_q + 4 * g + dqc + 1)
                                inst = s.activation(aQe[g % 2][:, dqc * 512:(dqc + 1) * 512], pQL[dqc][:],
                                                    AF.Relu, bias=cBq[:, dqc:dqc + 1])
                                a_qe.bump(inst, 1)
                            s.wait_ge(a_qe.h, base_qe + 4 * (g + 1))
                            for ai in range(4):
                                s.dma_start(
                                    out=d_qts[ai, :, g * 512:(g + 1) * 512],
                                    in_=aQe[g % 2][:, ai * 512:(ai + 1) * 512]
                                ).then_inc(c_sq[g % 2].h, 16); c_sq[g % 2].n += 16
                        else:
                            s.wait_ge(a_ev.h, base_ev + EVG * (g + 1))
                            for j in range(4):
                                s.dma_start(
                                    out=d_xnts[:, 2 * j:2 * j + 2, g * 512:(g + 1) * 512],
                                    in_=aXnTs[g % 2][:, (2 * j) * 512:(2 * j + 2) * 512]
                                        .rearrange("p (a t) -> p a t", a=2)
                                ).then_inc(c_sq[g % 2].h, 16); c_sq[g % 2].n += 16
                        a_bar.bump(s.memzero(cDmy[:]), 1)

                @blk.tensor
                def _(t):
                    for g in range(ngrp):
                        if g == 0:
                            t.wait_ge(k_ev.h, 64)
                            t.wait_ge(a_bar.h, base_bar)
                            t.wait_ge(a_q.h, base_q)
                        if g >= 1:
                            t.wait_ge(a_bar.h, base_bar + BAR * g)
                        for i in range(4):
                            t.wait_ge(a_xn.h, base_xn + 4 * g + i + 1)
                            for bank in range(2):
                                prev = 2 * i + bank - 2
                                if prev >= 0:
                                    t.wait_ge(a_ev.h, base_ev + EVG * g + EVB * (prev + 1))
                                p = pTr[bank]
                                for dcq in range(4):
                                    dc = bank * 4 + dcq
                                    inst = t.transpose(p[:, dcq * 128:(dcq + 1) * 128],
                                                       aXn[:, i * 1024 + dc * 128: i * 1024 + (dc + 1) * 128],
                                                       cIdF[:])
                                a_tr.bump(inst, 1)
                        t.wait_ge(a_ev.h, base_ev + EVG * (g + 1))
                        if not own:
                            for dqc in range(4):
                                for dc in range(8):
                                    inst = t.matmul(pQL[dqc][:],
                                                    cWq[:, (dc * 4 + dqc) * 128:(dc * 4 + dqc + 1) * 128],
                                                    aXnTs[0][:, dc * 512:(dc + 1) * 512],
                                                    start=(dc == 0), stop=(dc == 7))
                                a_q.bump(inst, 1)
                        else:
                            for i in range(4):
                                for dc in range(8):
                                    t.matmul(pQL[i][:, 0:8],
                                             aXnT32[:, dc * 512 + i * 128: dc * 512 + (i + 1) * 128],
                                             cWrT[:, dc * 8:(dc + 1) * 8],
                                             start=(dc == 0), stop=False)
                                inst = t.matmul(pQL[i][:, 0:8], cOn1f[:], cBr[:],
                                                start=False, stop=True)
                                a_lg.bump(inst, 1)

        phaseA("1", d_x, NGRP, own=False)
        phaseA("2", d_xo, NOG, own=True)
        st_w.close()
        A_SP = c_sp.n
        A_SQ = (c_sq[0].n, c_sq[1].n)
        assert A_SP == 0 and A_SQ == (64 * (NGRP + NOG) // 2,) * 2
        assert a_lgs.n == 32 and a_bar.n == 2 * (NGRP + NOG)
        A_BAR = a_bar.n


        with ExitStack() as str_:
            rMx = str_.enter_context(nc.sbuf_tensor("rMx", [128, 32 * 8], F32))
            rNm = str_.enter_context(nc.sbuf_tensor("rNm", [128, 32], F32))
            rE2 = str_.enter_context(nc.sbuf_tensor("rE2", [128, 32], F32))
            rRc = str_.enter_context(nc.sbuf_tensor("rRc", [128, 32], F32))
            rEx = str_.enter_context(nc.sbuf_tensor("rEx", [128, 32 * 8], F32))
            rMk = str_.enter_context(nc.sbuf_tensor("rMk", [128, 32 * 8], F32))
            rW = str_.enter_context(nc.sbuf_tensor("rW", [128, 32 * 8], F32))
            pRt = str_.enter_context(nc.psum_tensor("pRt", [128, 512], F32))

            @blk.scalar
            def _(s):
                for i2 in range(32):
                    s.wait_ge(r_w.h, 32 + i2 + 1)
                    s_rex.bump(s.activation(rE2[:, i2:i2 + 1], rMx[:, i2 * 8 + 1:i2 * 8 + 2],
                                            AF.Exp, bias=rNm[:, i2:i2 + 1]), 1)
                    s_rex.bump(s.activation(rEx[:, i2 * 8:(i2 + 1) * 8],
                                            stLG[:, i2 * 8:(i2 + 1) * 8],
                                            AF.Exp, bias=rNm[:, i2:i2 + 1]), 1)

            @blk.vector
            def _(v):
                v.wait_ge(a_lgs.h, 32)
                v.wait_ge(c_sq[0].h, A_SQ[0])
                v.wait_ge(c_sq[1].h, A_SQ[1])
                for i2 in range(32):
                    r_w.bump(v.max(out=rMx[:, i2 * 8:(i2 + 1) * 8],
                                   in_=stLG[:, i2 * 8:(i2 + 1) * 8]), 1)
                v.wait_ge(r_w.h, 32)
                for i2 in range(32):
                    r_w.bump(v.tensor_scalar(rNm[:, i2:i2 + 1], rMx[:, i2 * 8:i2 * 8 + 1],
                                             -1.0, None, ALU.mult), 1)
                for i2 in range(32):
                    v.wait_ge(s_rex.h, 2 * i2 + 1)
                    vf.bump(v.tensor_scalar(rE2[:, i2:i2 + 1], rE2[:, i2:i2 + 1],
                                            1.0, 1.0, ALU.mult, ALU.add), 1)
                v.wait_ge(vf.h, vf.n)
                for i2 in range(32):
                    vf.bump(v.reciprocal(rRc[:, i2:i2 + 1], rE2[:, i2:i2 + 1]), 1)
                for i2 in range(32):
                    vf.bump(v.tensor_scalar(rMk[:, i2 * 8:(i2 + 1) * 8],
                                            stLG[:, i2 * 8:(i2 + 1) * 8],
                                            rMx[:, i2 * 8 + 1:i2 * 8 + 2], None, ALU.is_ge), 1)
                v.wait_ge(s_rex.h, 64)
                v.wait_ge(vf.h, vf.n)
                for i2 in range(32):
                    vf.bump(v.tensor_scalar(rEx[:, i2 * 8:(i2 + 1) * 8], rEx[:, i2 * 8:(i2 + 1) * 8],
                                            rRc[:, i2:i2 + 1], None, ALU.mult), 1)
                v.wait_ge(vf.h, vf.n)
                for i2 in range(32):
                    r_w.bump(v.tensor_tensor(rW[:, i2 * 8:(i2 + 1) * 8], rEx[:, i2 * 8:(i2 + 1) * 8],
                                             rMk[:, i2 * 8:(i2 + 1) * 8], ALU.mult), 1)
                for tt in range(NTT):
                    v.wait_ge(r_t.h, 4 * (tt + 1))
                    r_wt.bump(v.tensor_copy(cWt[:, tt * 512:(tt + 1) * 512], pRt[0:8, :]), 1)

            @blk.tensor
            def _(t):
                for tt in range(NTT):
                    if tt >= 1:
                        t.wait_ge(r_wt.h, tt)
                    for i in range(4):
                        i2 = tt * 4 + i
                        t.wait_ge(r_w.h, 64 + i2 + 1)
                        inst = t.transpose(pRt[0:8, i * 128:(i + 1) * 128],
                                           rW[:, i2 * 8:(i2 + 1) * 8], cIdF[:])
                        r_t.bump(inst, 1)
        assert r_w.n == 96 and r_t.n == 32 and r_wt.n == 8 and s_rex.n == 64

        chains = []
        for tt in range(NTT):
            chains.append(("cxn", tt))
            if tt >= 1:
                for i in range(4):
                    for half in range(2):
                        chains.append(("tr", (tt - 1) * 4 + i, half, 4))
            for e in range(E):
                ge = tt * 8 + e
                chains.append(("bc", ge, 0))
                for fc in range(32):
                    chains.append(("hc", ge, fc, 8))
                for dc in range(8):
                    chains.append(("oc", ge, dc, 32))
        for i in range(4):
            for half in range(2):
                chains.append(("tr", 28 + i, half, 4))
        chains_tt = []
        cur_tt = 0
        for ch in chains:
            if ch[0] == "cxn":
                cur_tt = ch[1]
            chains_tt.append((cur_tt, ch))
        PREFIX = 0
        cmm_tt = [0] * NTT
        for tt, ch in chains_tt:
            if ch[0] in ("tr", "hc", "oc"):
                cmm_tt[tt] += ch[-1]
        w_tt = [1.0] * 8
        rem = 4096 - PREFIX
        denom = sum(w_tt[t] * cmm_tt[t] for t in range(NTT))
        rate_tt = [rem * w_tt[t] / denom for t in range(NTT)]
        EV = []
        pend = []
        bu = 0

        def push(ev):
            EV.append(ev)
            for p in pend:
                p[0] -= 1
            ready = [p for p in pend if p[0] <= 0]
            pend[:] = [p for p in pend if p[0] > 0]
            for p in ready:
                EV.append(p[1])

        def emit_unit(k, u):
            if u == 0 and k + 2 < 128:
                push(("bq", k + 2))
            push(("bu", k, u))
            if u == 31:
                push(("bev", k))
                pend.append([2, ("bnc", k)])
                if k % 16 == 15:
                    h = k // 16
                    pend.extend([[3, ("d8", h)], [5, ("pdb", h, 0)], [6, ("rcp", h, 0)],
                                 [7, ("scr", h)], [8, ("nrm", h)], [9, ("pdb", h, 1)],
                                 [10, ("rcp", h, 1)], [13, ("sadd", h)], [23, ("sst", h)]])

        push(("bq", 0))
        push(("bq", 1))
        while bu < PREFIX:
            emit_unit(bu // 32, bu % 32)
            bu += 1
        pend.sort(key=lambda p: p[0])
        for p in pend:
            EV.append(p[1])
        pend.clear()
        acc = 0.0
        for tt, ch in chains_tt:
            push(ch)
            if ch[0] in ("tr", "hc", "oc"):
                acc += ch[-1] * rate_tt[tt]
            while acc >= 1.0 and bu < 4096:
                acc -= 1.0
                emit_unit(bu // 32, bu % 32)
                bu += 1
        while bu < 4096:
            emit_unit(bu // 32, bu % 32)
            bu += 1
        pend.sort(key=lambda p: p[0])
        for p in pend:
            EV.append(p[1])

        with ExitStack() as stm:
            bQt = stm.enter_context(nc.sbuf_tensor("bQt", [128, 4 * 512], BF16))
            bEx = stm.enter_context(nc.sbuf_tensor("bEx", [128, 4 * 512], BF16))
            bXr = stm.enter_context(nc.sbuf_tensor("bXr", [128, 4 * 1024], BF16))
            bScr = stm.enter_context(nc.sbuf_tensor("bScr", [128, 4 * 1024], BF16))
            bDiv = stm.enter_context(nc.sbuf_tensor("bDiv", [128, 1024], BF16))
            bAssy = stm.enter_context(nc.sbuf_tensor("bAssy", [65, 2 * 512], BF16))
            bD8 = stm.enter_context(nc.sbuf_tensor("bD8", [8, 1024], BF16))
            cXnT = stm.enter_context(nc.sbuf_tensor("cXnT", [128, 8 * 512], BF16))
            cW1 = [stm.enter_context(nc.sbuf_tensor(f"cW1_{i}", [128, 4 * 8 * 128], BF16)) for i in range(2)]
            cW2 = [stm.enter_context(nc.sbuf_tensor(f"cW2_{i}", [128, 32 * 128], BF16)) for i in range(2)]
            cHW = stm.enter_context(nc.sbuf_tensor("cHW", [128, 32 * 512], BF16))
            cHR = stm.enter_context(nc.sbuf_tensor("cHR", [128, 2 * 512], BF16))
            cWbc = stm.enter_context(nc.sbuf_tensor("cWbc", [128, 2 * 512], BF16))
            cAcc = stm.enter_context(nc.sbuf_tensor("cAcc", [128, 8 * 512], F32))
            sT2 = stm.enter_context(nc.sbuf_tensor("sT2", [128, 1024], F32))
            pH = [stm.enter_context(nc.psum_tensor(f"pH{i}", [128, 512], F32)) for i in range(2)]
            pO = stm.enter_context(nc.psum_tensor("pO", [128, 512], F32))
            pT2m = stm.enter_context(nc.psum_tensor("pT2m", [128, 512], F32))
            pSc = [stm.enter_context(nc.psum_tensor(f"pSc{i}", [128, 512], F32)) for i in range(2)]
            pAv = [stm.enter_context(nc.psum_tensor(f"pAv{i}", [128, 512], F32)) for i in range(2)]

            @blk.gpsimd
            def _(g):
                def cast_pair(e):
                    g.wait_ge(a_bar.h, min(26 + 2 * e, A_BAR))
                    g.wait_ge(c_w1.h, c_w1.n)
                    g.dma_start(out=d_w1b[e], in_=d_w1t[e]).then_inc(c_w1.h, 16)
                    c_w1.n += 16
                    g.wait_ge(c_w2.h, c_w2.n)
                    g.dma_start(out=d_w2b[e], in_=d_w2t[e]).then_inc(c_w2.h, 16)
                    c_w2.n += 16

                def issue_xr(h):
                    if h >= 1:
                        g.wait_ge(b_sg.h, 4 * h)
                    g.wait_ge(c_xr.h, 16 * h)
                    c_xr.bump(g.dma_start(
                        out=bXr[:].rearrange("p (i c) -> p i c", i=4),
                        in_=d_xo[h * 512:(h + 1) * 512, :].rearrange("(i p) c -> p i c", p=128)), 16)

                for e in range(E):
                    cast_pair(e)
                g.wait_ge(a_bar.h, A_BAR)
                g.wait_ge(c_sq[0].h, A_SQ[0])
                g.wait_ge(c_sq[1].h, A_SQ[1])
                g.wait_ge(r_wt.h, 8)
                for h in range(8):
                    issue_xr(h)

            @blk.sync
            def _(sync):
                sync.wait_ge(a_bar.h, A_BAR)
                sync.wait_ge(c_sq[0].h, A_SQ[0])
                sync.wait_ge(c_sq[1].h, A_SQ[1])
                sync.wait_ge(r_wt.h, 8)
                next_w1 = [0]
                next_w2 = [0]

                def issue_w1():
                    ug = next_w1[0]
                    e_mod = (ug // 8) % 8
                    sync.wait_ge(c_w1.h, 16 * (e_mod + 1))
                    if ug >= 2:
                        sync.wait_ge(f_h.h, 4 * ug - 4)
                    c_w1l[ug % 2].bump(sync.dma_start(
                        out=cW1[ug % 2][:], in_=d_w1b[(ug // 8) % 8, ug % 8]), 16)
                    next_w1[0] += 1

                def issue_w2():
                    vg = next_w2[0]
                    e_mod = (vg // 8) % 8
                    sync.wait_ge(c_w2.h, 16 * (e_mod + 1))
                    if vg >= 1:
                        sync.wait_ge(f_o.h, vg - 1)
                    c_w2l[vg % 2].bump(sync.dma_start(
                        out=cW2[vg % 2][:],
                        in_=d_w2b[(vg // 8) % 8, vg % 8].rearrange("p a q -> p (a q)")), 16)
                    next_w2[0] += 1

                for ev in EV:
                    kind = ev[0]
                    if kind == "cxn":
                        tt = ev[1]
                        if tt >= 1:
                            sync.wait_ge(f_h.h, 256 * tt)
                        c_cxn[0].bump(sync.dma_start(
                            out=cXnT[:].rearrange("p (a t) -> p a t", a=8),
                            in_=d_xnts[:, :, tt * 512:(tt + 1) * 512]), 16)
                    elif kind == "hc":
                        ge, fc = ev[1], ev[2]
                        if fc % 4 == 0:
                            ug_cur = ge * 8 + fc // 4
                            while next_w1[0] <= min(ug_cur + 2, 511):
                                issue_w1()
                    elif kind == "oc":
                        ge, dc = ev[1], ev[2]
                        vg_cur = ge * 8 + dc
                        while next_w2[0] <= min(vg_cur + 2, 511):
                            issue_w2()
                    elif kind == "tr":
                        gi, half = ev[1], ev[2]
                        if half == 1:
                            sync.wait_ge(f_t2c.h, 2 * gi + 2)
                            c_t2s.bump(sync.dma_start(
                                out=d_t2[gi * 128:(gi + 1) * 128, :], in_=sT2[:]), 16)
                    elif kind == "bq":
                        k = ev[1]
                        hp2, a2, qt2 = k // 32, (k // 16) % 2, k % 16
                        if k >= 4:
                            sync.wait_ge(b_sc.h, 32 * (k - 3))
                        c_qT[k % 4].bump(sync.dma_start(
                            out=bQt[64 * a2:64 * a2 + 64, (k % 4) * 512:(k % 4 + 1) * 512],
                            in_=d_qts[hp2, 64 * a2:64 * a2 + 64, qt2 * 512:(qt2 + 1) * 512]), 16)

            @blk.tensor
            def _(t):
                for ev in EV:
                    kind = ev[0]
                    if kind == "bc":
                        ge = ev[1]
                        tt, e = ge // 8, ge % 8
                        t.wait_ge(r_wt.h, 8)
                        if ge >= 1:
                            t.wait_ge(f_wb.h, ge)
                        if e == 0 and tt >= 1:
                            t.wait_ge(f_t2c.h, 8 * tt)
                        t_bc.bump(t.matmul(pT2m[:], cOneHot[:, e * 128:(e + 1) * 128],
                                           cWt[:, tt * 512:(tt + 1) * 512],
                                           start=True, stop=True), 1)
                    elif kind == "hc":
                        ge, fc = ev[1], ev[2]
                        tt, e = ge // 8, ge % 8
                        gf = ge * 32 + fc
                        ug = ge * 8 + fc // 4
                        if e == 0 and fc == 0:
                            t.wait_ge(c_cxn[0].h, 16 * (tt + 1))
                        if fc % 4 == 0:
                            t.wait_ge(c_w1l[ug % 2].h, 16 * (ug // 2 + 1))
                        if gf >= 2:
                            t.wait_ge(f_hr.h, gf - 1)
                        for dc in range(8):
                            inst = t.matmul(pH[fc % 2][:],
                                            cW1[ug % 2][:, ((fc % 4) * 8 + dc) * 128:
                                                          ((fc % 4) * 8 + dc + 1) * 128],
                                            cXnT[:, dc * 512:(dc + 1) * 512],
                                            start=(dc == 0), stop=(dc == 7))
                        f_h.bump(inst, 1)
                    elif kind == "oc":
                        ge, dc = ev[1], ev[2]
                        tt, e = ge // 8, ge % 8
                        vg = ge * 8 + dc
                        t.wait_ge(c_w2l[vg % 2].h, 16 * (vg // 2 + 1))
                        if dc == 0:
                            t.wait_ge(f_hw.h, 32 * (ge + 1))
                        if vg >= 1:
                            t.wait_ge(f_oe.h, vg)
                        first = True
                        if e == 0:
                            t.matmul(pO[:], cB2[:, dc * 128:(dc + 1) * 128],
                                     cWt[:, tt * 512:(tt + 1) * 512],
                                     start=True, stop=False)
                            first = False
                        for fc in range(32):
                            inst = t.matmul(pO[:],
                                            cW2[vg % 2][:, fc * 128:(fc + 1) * 128],
                                            cHW[:, fc * 512:(fc + 1) * 512],
                                            start=(first and fc == 0), stop=(fc == 31))
                        f_o.bump(inst, 1)
                    elif kind == "tr":
                        gi, half = ev[1], ev[2]
                        tt, i = gi // 4, gi % 4
                        tridx = gi * 2 + half
                        if i == 0 and half == 0:
                            t.wait_ge(f_oe.h, 64 * (tt + 1))
                            t.wait_ge(f_wb.h, 8 * (tt + 1))
                        t.wait_ge(f_t2c.h, tridx)
                        for dcq in range(4):
                            dc = half * 4 + dcq
                            inst = t.transpose(pT2m[:, dcq * 128:(dcq + 1) * 128],
                                               cAcc[:, dc * 512 + i * 128: dc * 512 + (i + 1) * 128],
                                               cIdF[:])
                        f_t2.bump(inst, 1)
                    elif kind == "bu":
                        k, u = ev[1], ev[2]
                        h = k // 16
                        hp2, a2, qt2 = h // 2, h % 2, k % 16
                        gu = k * 32 + u
                        if gu == 0:
                            t.wait_ge(r_wt.h, 8)
                            t.wait_ge(c_vo.h, 1)
                        if gu >= 2:
                            t.wait_ge(b_ex.h, gu - 1)
                        if u == 0:
                            t.wait_ge(c_qT[k % 4].h, 16 * (k // 4 + 1))
                            if qt2 >= 1:
                                t.wait_ge(b_ae.h, k)
                            elif h >= 1:
                                t.wait_ge(b_ae.h, 16 * (h - 1) if h >= 2 else 0)
                                if h >= 2:
                                    t.wait_ge(b_rc.h, 2 * (h - 1))
                        b_sc.bump(t.matmul(
                            pSc[u % 2][:],
                            KT[64 * a2:64 * a2 + 64,
                               hp2 * TKV + u * 128: hp2 * TKV + (u + 1) * 128],
                            bQt[64 * a2:64 * a2 + 64, (k % 4) * 512:(k % 4 + 1) * 512],
                            start=True, stop=True), 1)

                        def emit_av(um, stop):
                            gum = k * 32 + um
                            inst = t.matmul(
                                pAv[a2][0:65, :],
                                vAll[:, (h * 32 + um) * 65:(h * 32 + um + 1) * 65],
                                bEx[:, (gum % 4) * 512:(gum % 4 + 1) * 512],
                                start=(um == 0), stop=stop)
                            b_av.bump(inst, 1)
                        if u >= 2:
                            emit_av(u - 2, False)
                        if u == 31:
                            for um in (30, 31):
                                t.wait_ge(b_ex.h, k * 32 + um + 1)
                                emit_av(um, um == 31)
                    elif kind == "pdb":
                        h, half = ev[1], ev[2]
                        if half == 0:
                            t.wait_ge(c_d8.h, 16 * (h + 1))
                            t.wait_ge(b_ae.h, 16 * (h + 1))
                        else:
                            t.wait_ge(b_rc.h, 2 * h + 1)
                        b_dn.bump(t.matmul(pAv[h % 2][:], cRep8[:],
                                           bD8[:, half * 512:(half + 1) * 512],
                                           start=True, stop=True), 1)

            @blk.scalar
            def _(s):
                for ev in EV:
                    kind = ev[0]
                    if kind == "hc":
                        ge, fc = ev[1], ev[2]
                        e = ge % 8
                        gf = ge * 32 + fc
                        s.wait_ge(f_h.h, gf + 1)
                        if gf >= 2:
                            s.wait_ge(f_hw.h, gf - 1)
                        f_hr.bump(s.activation(cHR[:, (fc % 2) * 512:(fc % 2 + 1) * 512],
                                               pH[fc % 2][:], AF.Relu,
                                               bias=cB1[:, e * 32 + fc:e * 32 + fc + 1]), 1)
                    elif kind == "bu":
                        k, u = ev[1], ev[2]
                        gu = k * 32 + u
                        s.wait_ge(b_sc.h, gu + 1)
                        if gu >= 4:
                            s.wait_ge(b_av.h, gu - 3)
                        b_ex.bump(s.activation(bEx[:, (gu % 4) * 512:(gu % 4 + 1) * 512],
                                               pSc[u % 2][:], AF.Exp, scale=0.125), 1)
                    elif kind == "d8":
                        h = ev[1]
                        s.wait_ge(c_bnD[0].h, 128 * (h + 1))
                        s.wait_ge(c_bnD[1].h, 128 * (h + 1))
                        if h >= 1:
                            s.wait_ge(b_dn.h, 2 * h)
                        c_d8.bump(s.dma_start(
                            out=bD8[:], in_=d_bncr2[h % 2].rearrange("(j c) -> j c", j=8)), 16)
                    elif kind == "scr":
                        h = ev[1]
                        s.wait_ge(c_bnA[0].h, 128 * (h + 1))
                        s.wait_ge(c_bnA[1].h, 128 * (h + 1))
                        if h >= 1:
                            s.wait_ge(b_sg.h, 4 * h)
                            s.wait_ge(c_sp.h, A_SP + 16 * h)
                        c_scr.bump(s.dma_start(
                            out=bScr[:].rearrange("p (i c) -> p i c", i=4),
                            in_=d_bnc[64 * (h % 2):64 * (h % 2) + 64, :]
                                .rearrange("d (j c) -> (d j) c", j=8)
                                .rearrange("(i p) c -> p i c", p=128)), 16)
                    elif kind == "nrm":
                        h = ev[1]
                        s.wait_ge(c_xr.h, 16 * (h + 1))
                        for i in range(4):
                            ti = h * 4 + i
                            b_xn.bump(s.activation(bXr[:, i * 1024:(i + 1) * 1024],
                                                   bXr[:, i * 1024:(i + 1) * 1024],
                                                   AF.Identity, bias=stBL[:, ti:ti + 1],
                                                   scale=stRS[:, ti:ti + 1]), 1)
                    elif kind == "bnc":
                        k = ev[1]
                        h, qt2 = k // 16, k % 16
                        s.wait_ge(b_ae.h, k + 1)
                        if h >= 2:
                            s.wait_ge(c_d8.h, 16 * (h - 1))
                            s.wait_ge(c_scr.h, 16 * (h - 1))
                        c_bnA[k % 2].bump(s.dma_start(
                            out=d_bnc[64 * (h % 2):64 * (h % 2) + 64, qt2 * 512:(qt2 + 1) * 512],
                            in_=bAssy[0:64, (k % 2) * 512:(k % 2 + 1) * 512]), 16)
                        c_bnD[k % 2].bump(s.dma_start(
                            out=d_bncr2[h % 2:h % 2 + 1, qt2 * 512:(qt2 + 1) * 512],
                            in_=bAssy[64:65, (k % 2) * 512:(k % 2 + 1) * 512]), 16)
                    elif kind == "sst":
                        h = ev[1]
                        s.wait_ge(b_sg.h, 4 * (h + 1))
                        s.wait_ge(c_sp.h, A_SP + 16 * h)
                        s.dma_start(out=d_ss[h * 512:(h + 1) * 512, :].rearrange("(i p) c -> p i c", p=128),
                                    in_=bScr[:].rearrange("p (i c) -> p i c", i=4)).then_inc(c_sp.h, 16)
                        c_sp.n += 16
                s.wait_ge(b_ex.h, 4096)
                s.wait_ge(f_hr.h, 2048)
                s.wait_ge(b_xn.h, 32)
                s.wait_ge(c_sp.h, A_SP + 16 * 8)
                s.wait_ge(c_bnA[0].h, 16 * 64)
                s.wait_ge(c_bnA[1].h, 16 * 64)
                s.wait_ge(c_bnD[0].h, 16 * 64)
                s.wait_ge(c_bnD[1].h, 16 * 64)
                m_sd.bump(s.memzero(cDmy[:]), 1)

            @blk.vector
            def _(v):
                for ev in EV:
                    kind = ev[0]
                    if kind == "bc":
                        ge = ev[1]
                        v.wait_ge(t_bc.h, ge + 1)
                        if ge >= 2:
                            v.wait_ge(f_hw.h, 32 * (ge - 1))
                        f_wb.bump(v.tensor_copy(cWbc[:, (ge % 2) * 512:(ge % 2 + 1) * 512],
                                                pT2m[:]), 1)
                    elif kind == "hc":
                        ge, fc = ev[1], ev[2]
                        gf = ge * 32 + fc
                        v.wait_ge(f_hr.h, gf + 1)
                        if fc == 0:
                            v.wait_ge(f_wb.h, ge + 1)
                            if ge >= 1:
                                v.wait_ge(f_o.h, ge * 8)
                        f_hw.bump(v.tensor_tensor(cHW[:, fc * 512:(fc + 1) * 512],
                                                  cHR[:, (fc % 2) * 512:(fc % 2 + 1) * 512],
                                                  cWbc[:, (ge % 2) * 512:(ge % 2 + 1) * 512],
                                                  ALU.mult), 1)
                    elif kind == "oc":
                        ge, dc = ev[1], ev[2]
                        tt, e = ge // 8, ge % 8
                        vg = ge * 8 + dc
                        v.wait_ge(f_o.h, vg + 1)
                        if e == 0:
                            if dc == 0 and tt >= 1:
                                v.wait_ge(f_t2.h, 8 * tt)
                            inst = v.tensor_copy(cAcc[:, dc * 512:(dc + 1) * 512], pO[:])
                        else:
                            inst = v.tensor_tensor(cAcc[:, dc * 512:(dc + 1) * 512],
                                                   pO[:], cAcc[:, dc * 512:(dc + 1) * 512],
                                                   ALU.add)
                        f_oe.bump(inst, 1)
                    elif kind == "tr":
                        gi, half = ev[1], ev[2]
                        tridx = gi * 2 + half
                        v.wait_ge(f_t2.h, tridx + 1)
                        if gi >= 1 and half == 0:
                            v.wait_ge(c_t2s.h, 16 * gi)
                        f_t2c.bump(v.tensor_copy(sT2[:, half * 512:(half + 1) * 512],
                                                 pT2m[:]), 1)
                    elif kind == "bev":
                        k = ev[1]
                        a2 = (k // 16) % 2
                        v.wait_ge(b_av.h, 32 * (k + 1))
                        if k >= 2:
                            v.wait_ge(c_bnA[k % 2].h, 16 * (k // 2))
                            v.wait_ge(c_bnD[k % 2].h, 16 * (k // 2))
                        b_ae.bump(v.tensor_copy(bAssy[0:65, (k % 2) * 512:(k % 2 + 1) * 512],
                                                pAv[a2][0:65, :]), 1)
                    elif kind == "rcp":
                        h, half = ev[1], ev[2]
                        v.wait_ge(b_dn.h, 2 * h + half + 1)
                        with nc.allow_low_precision(reason="1/den in bf16 is within tolerance"):
                            b_rc.bump(v.reciprocal(bDiv[:, half * 512:(half + 1) * 512],
                                                   pAv[h % 2][:]), 1)
                    elif kind == "sadd":
                        h = ev[1]
                        v.wait_ge(b_rc.h, 2 * (h + 1))
                        v.wait_ge(c_scr.h, 16 * (h + 1))
                        for i in range(4):
                            vf.bump(v.tensor_tensor(bScr[:, i * 1024:(i + 1) * 1024],
                                                    bScr[:, i * 1024:(i + 1) * 1024],
                                                    bDiv[:], ALU.mult), 1)
                        v.wait_ge(vf.h, vf.n)
                        for i in range(4):
                            v.wait_ge(b_xn.h, 4 * h + i + 1)
                            b_sg.bump(v.tensor_tensor(bScr[:, i * 1024:(i + 1) * 1024],
                                                      bScr[:, i * 1024:(i + 1) * 1024],
                                                      bXr[:, i * 1024:(i + 1) * 1024],
                                                      ALU.add), 1)
                v.wait_ge(f_t2c.h, 64)
                v.wait_ge(b_sg.h, 32)
                v.wait_ge(f_oe.h, 512)
                v.wait_ge(b_ae.h, 128)
                v.wait_ge(vf.h, vf.n)
                m_vd.bump(v.engine_nop(), 1)

        assert b_sc.n == 4096 and b_ex.n == 4096 and b_av.n == 4096
        assert b_ae.n == 128 and c_bnA[0].n == 1024 and c_bnD[1].n == 1024
        assert b_dn.n == 16 and b_rc.n == 16 and b_xn.n == 32 and b_sg.n == 32
        assert f_h.n == 2048 and f_hr.n == 2048 and f_hw.n == 2048
        assert f_o.n == 512 and f_oe.n == 512
        assert t_bc.n == 64 and f_wb.n == 64 and f_t2.n == 64 and f_t2c.n == 64
        assert c_t2s.n == 32 * 16 and c_sp.n == A_SP + 128
        st_w2 = c_w1.n

        st_ab.close()

        with ExitStack() as stf:
            cGamF = stf.enter_context(nc.sbuf_tensor("cGamF", [128, D], F32))
            cBetF = stf.enter_context(nc.sbuf_tensor("cBetF", [128, D], F32))
            cS = stf.enter_context(nc.sbuf_tensor("cS", [128, 4 * 1024], BF16))
            cX = stf.enter_context(nc.sbuf_tensor("cX", [128, 4 * 1024], F32))
            cXn2 = stf.enter_context(nc.sbuf_tensor("cXn2", [128, 4 * 1024], F32))
            cT2 = stf.enter_context(nc.sbuf_tensor("cT2", [128, 4 * 1024], F32))
            cTh = stf.enter_context(nc.sbuf_tensor("cTh", [128, 4 * 1024], F32))
            cOut = stf.enter_context(nc.sbuf_tensor("cOut", [128, 4 * 1024], F32))
            nSm = stf.enter_context(nc.sbuf_tensor("nSm", [128, 4], F32))
            nNm = stf.enter_context(nc.sbuf_tensor("nNm", [128, 4], F32))
            nV2 = stf.enter_context(nc.sbuf_tensor("nV2", [128, 4], F32))
            nVar = stf.enter_context(nc.sbuf_tensor("nVar", [128, 4], F32))
            nRS = stf.enter_context(nc.sbuf_tensor("nRS", [128, 4], F32))
            nBL = stf.enter_context(nc.sbuf_tensor("nBL", [128, 4], F32))
            c_gb = sem("c_gb")

            @blk.sync
            def _(sync):
                sync.wait_ge(m_vd.h, 1)
                sync.wait_ge(m_sd.h, 1)
                sync.wait_ge(f_t2.h, 64)
                c_gb.bump(sync.dma_start(out=cGamF[:], in_=d_gam), 16)
                c_gb.bump(sync.dma_start(out=cBetF[:], in_=d_bet), 16)
                for gs in range(32):
                    sl = gs % 4
                    sync.wait_ge(c_t2s.h, 16 * (gs + 1))
                    if gs >= 4:
                        sync.wait_ge(f_fin.h, gs - 3)
                    sync.wait_ge(c_t2l[gs % 2].h, 16 * (gs // 2))
                    c_t2l[gs % 2].bump(sync.dma_start(
                        out=cT2[:, sl * 1024:(sl + 1) * 1024],
                        in_=d_t2[gs * 128:(gs + 1) * 128, :]), 16)
                sync.wait_ge(c_out[0].h, 16 * 16)
                sync.wait_ge(c_out[1].h, 16 * 16)

            @blk.gpsimd
            def _(g):
                g.wait_ge(m_vd.h, 1)
                g.wait_ge(m_sd.h, 1)
                g.wait_ge(f_t2.h, 64)
                for gs in range(32):
                    sl = gs % 4
                    tt = gs // 4
                    if gs >= 4:
                        g.wait_ge(f_fin.h, gs - 3)
                    g.wait_ge(c_sp.h, A_SP + 16 * (tt + 1))
                    g.wait_ge(c_cs[gs % 2].h, 16 * (gs // 2))
                    c_cs[gs % 2].bump(g.dma_start(
                        out=cS[:, sl * 1024:(sl + 1) * 1024],
                        in_=d_ss[gs * 128:(gs + 1) * 128, :]), 16)
                    g.wait_ge(c_cxx[gs % 2].h, 16 * (gs // 2))
                    c_cxx[gs % 2].bump(g.dma_start(
                        out=cX[:, sl * 1024:(sl + 1) * 1024],
                        in_=d_xo[gs * 128:(gs + 1) * 128, :]), 16)

            @blk.scalar
            def _(s):
                ln_cnt = {}

                def S(stage, gs):
                    sl = gs % 4
                    lo, hi = sl * 1024, (sl + 1) * 1024
                    if stage == 0:
                        s.wait_ge(c_cxx[gs % 2].h, 16 * (gs // 2 + 1))
                        if gs >= 4:
                            s.wait_ge(f_t2a.h, gs - 3)
                        s_xn2.bump(s.activation(cXn2[:, lo:hi], cX[:, lo:hi],
                                                AF.Identity, bias=stBL[:, gs:gs + 1],
                                                scale=stRS[:, gs:gs + 1]), 1)
                    elif stage == 1:
                        s.wait_ge(f_t2a.h, gs + 1)
                        s.wait_ge(n_nm.h, gs + 1)
                        if gs >= 4:
                            s.wait_ge(f_fin.h, gs - 3)
                        s_sq.bump(s.activation(cTh[:, lo:hi], cT2[:, lo:hi],
                                               AF.Square, bias=nNm[:, sl:sl + 1]), 1)
                    elif stage == 2:
                        s.wait_ge(n_var.h, gs + 1)
                        ln_cnt[gs] = sf.bump(s.activation(nRS[:, sl:sl + 1],
                                                          nVar[:, sl:sl + 1], AF.Ln), 1)
                    elif stage == 3:
                        s.wait_ge(sf.h, ln_cnt[gs])
                        s_rs.bump(s.activation(nRS[:, sl:sl + 1], nRS[:, sl:sl + 1],
                                               AF.Exp, scale=-0.5), 1)
                    elif stage == 4:
                        s.wait_ge(f_bl.h, gs + 1)
                        s_A.bump(s.activation(cT2[:, lo:hi], cT2[:, lo:hi],
                                              AF.Identity, bias=nBL[:, sl:sl + 1],
                                              scale=nRS[:, sl:sl + 1]), 1)
                    elif stage == 5:
                        s.wait_ge(f_y.h, gs + 1)
                        s_tanh.bump(s.activation(cTh[:, lo:hi], cT2[:, lo:hi],
                                                 AF.Tanh, scale=0.5), 1)
                    elif stage == 6:
                        s.wait_ge(f_fin.h, gs + 1)
                        s.wait_ge(c_out[gs % 2].h, 16 * (gs // 2))
                        c_out[gs % 2].bump(s.dma_start(
                            out=d_y[gs * 128:(gs + 1) * 128, :],
                            in_=cOut[:, lo:hi]), 16)

                for p in range(16):
                    for stage in range(7):
                        for j in range(2):
                            S(stage, 2 * p + j)

            @blk.vector
            def _(v):
                vfc = {}

                def V(stage, gs):
                    sl = gs % 4
                    lo, hi = sl * 1024, (sl + 1) * 1024
                    if stage == 0:
                        v.wait_ge(c_t2l[gs % 2].h, 16 * (gs // 2 + 1))
                        v.wait_ge(s_xn2.h, gs + 1)
                        v.wait_ge(c_cs[gs % 2].h, 16 * (gs // 2 + 1))
                        v.wait_ge(c_cxx[gs % 2].h, 16 * (gs // 2 + 1))
                        if gs >= 4:
                            v.wait_ge(s_tanh.h, gs - 3)
                        f_t2a.bump(v.tensor_tensor(cT2[:, lo:hi], cT2[:, lo:hi],
                                                   cXn2[:, lo:hi], ALU.add), 1)
                    elif stage == 1:
                        v.wait_ge(f_t2a.h, gs + 1)
                        vfc[(gs, 0)] = vf.bump(v.reduce_sum(nSm[:, sl:sl + 1],
                                                            cT2[:, lo:hi], axis=AX.X), 1)
                    elif stage == 2:
                        v.wait_ge(vf.h, vfc[(gs, 0)])
                        n_nm.bump(v.tensor_scalar(nNm[:, sl:sl + 1], nSm[:, sl:sl + 1],
                                                  -1.0 / 1024, None, ALU.mult), 1)
                    elif stage == 3:
                        v.wait_ge(s_sq.h, gs + 1)
                        vfc[(gs, 1)] = vf.bump(v.reduce_sum(nV2[:, sl:sl + 1],
                                                            cTh[:, lo:hi], axis=AX.X), 1)
                    elif stage == 4:
                        v.wait_ge(vf.h, vfc[(gs, 1)])
                        n_var.bump(v.tensor_scalar(nVar[:, sl:sl + 1], nV2[:, sl:sl + 1],
                                                   1.0 / 1024, 1e-5, ALU.mult, ALU.add), 1)
                    elif stage == 5:
                        v.wait_ge(s_rs.h, gs + 1)
                        v.wait_ge(n_nm.h, gs + 1)
                        f_bl.bump(v.tensor_tensor(nBL[:, sl:sl + 1], nNm[:, sl:sl + 1],
                                                  nRS[:, sl:sl + 1], ALU.mult), 1)
                    elif stage == 6:
                        v.wait_ge(s_A.h, gs + 1)
                        vfc[(gs, 2)] = vf.bump(v.tensor_tensor(cT2[:, lo:hi], cT2[:, lo:hi],
                                                               cGamF[:], ALU.mult), 1)
                    elif stage == 7:
                        v.wait_ge(vf.h, vfc[(gs, 2)])
                        f_y.bump(v.tensor_tensor(cT2[:, lo:hi], cT2[:, lo:hi],
                                                 cBetF[:], ALU.add), 1)
                    elif stage == 8:
                        v.wait_ge(s_tanh.h, gs + 1)
                        vfc[(gs, 3)] = vf.bump(v.tensor_scalar(cTh[:, lo:hi], cTh[:, lo:hi],
                                                               0.5, 0.5, ALU.mult, ALU.add), 1)
                    elif stage == 9:
                        v.wait_ge(vf.h, vfc[(gs, 3)])
                        v.wait_ge(f_y.h, gs + 1)
                        vfc[(gs, 4)] = vf.bump(v.tensor_tensor(cTh[:, lo:hi], cT2[:, lo:hi],
                                                               cTh[:, lo:hi], ALU.mult), 1)
                    elif stage == 10:
                        v.wait_ge(vf.h, vfc[(gs, 4)])
                        vfc[(gs, 5)] = vf.bump(v.tensor_tensor(cTh[:, lo:hi], cS[:, lo:hi],
                                                               cTh[:, lo:hi], ALU.mult), 1)
                    elif stage == 11:
                        v.wait_ge(vf.h, vfc[(gs, 5)])
                        if gs >= 4:
                            v.wait_ge(c_out[gs % 2].h, 16 * (gs // 2 - 1))
                        f_fin.bump(v.tensor_tensor(cOut[:, lo:hi], cX[:, lo:hi],
                                                   cTh[:, lo:hi], ALU.add), 1)

                v.wait_ge(c_gb.h, 32)
                for p in range(16):
                    for stage in range(12):
                        for j in range(2):
                            V(stage, 2 * p + j)

        assert f_fin.n == 32
        assert c_out[0].n == 16 * 16 and c_out[1].n == 16 * 16

    return nc


def _prep_maps(inputs):
    x = np.ascontiguousarray(np.asarray(inputs["L0_attn_queries"]), dtype=np.float32)
    keys = np.ascontiguousarray(np.asarray(inputs["L0_attn_keys"])[0], dtype=np.float32)
    values = np.ascontiguousarray(np.asarray(inputs["L0_attn_values"])[0], dtype=np.float32)
    Wq = np.asarray(inputs["L0_attn_Wq_w"], dtype=np.float32)
    bq = np.asarray(inputs["L0_attn_Wq_b"], dtype=np.float32)
    Wk = np.asarray(inputs["L0_attn_Wk_w"], dtype=np.float32)
    bk = np.asarray(inputs["L0_attn_Wk_b"], dtype=np.float32)
    Wv = np.asarray(inputs["L0_attn_Wv_w"], dtype=np.float32)
    bv = np.asarray(inputs["L0_attn_Wv_b"], dtype=np.float32)
    Wr = np.asarray(inputs["L0_ffn_Wr"], dtype=np.float32)
    br = np.asarray(inputs["L0_ffn_br"], dtype=np.float32)
    W1 = np.asarray(inputs["L0_ffn_W1"], dtype=np.float32)
    b1 = np.asarray(inputs["L0_ffn_b1"], dtype=np.float32)
    W2 = np.asarray(inputs["L0_ffn_W2"], dtype=np.float32)
    b2 = np.asarray(inputs["L0_ffn_b2"], dtype=np.float32)
    gam = np.asarray(inputs["L0_ffn_gamma"], dtype=np.float32)
    bet = np.asarray(inputs["L0_ffn_beta"], dtype=np.float32)

    keysT = np.ascontiguousarray(keys.T)
    valuesT = np.ascontiguousarray(values.T)
    w1t = np.ascontiguousarray(
        W1.reshape(E, 8, 4, 128, 8, 128).transpose(0, 1, 5, 2, 4, 3).reshape(E, 8, 128, 4096))
    w2t = np.ascontiguousarray(W2.reshape(E, 8, 128, 32, 128).transpose(0, 1, 4, 3, 2))
    wrT = np.ascontiguousarray(Wr.T.reshape(8, 128, 8).transpose(1, 0, 2))
    b1t = np.ascontiguousarray(b1.reshape(E, 32, 128).transpose(2, 0, 1))
    brf = np.ascontiguousarray(br.reshape(1, 8))
    gamb = np.ascontiguousarray(np.broadcast_to(gam, (128, D)))
    betb = np.ascontiguousarray(np.broadcast_to(bet, (128, D)))
    idf = np.eye(128, dtype=np.float32)
    ones1f = np.ones((1, 128), dtype=np.float32)
    ones1b = np.ones((1, 128), dtype=BF)
    onehot = np.zeros((8, 8 * 128), dtype=BF)
    for e in range(8):
        onehot[e, e * 128:(e + 1) * 128] = 1.0
    rep8 = np.zeros((8, 128), dtype=BF)
    for m in range(128):
        rep8[m % 8, m] = 1.0

    maps = []
    for c in range(8):
        b, half = c // 2, c % 2
        cols = slice(half * 512, (half + 1) * 512)
        m = dict(
            x=x[b],
            xo=np.ascontiguousarray(x[b, half * 4096:(half + 1) * 4096]),
            keysT=keysT, valuesT=valuesT,
            wqT=np.ascontiguousarray(Wq.T[:, cols].reshape(8, 128, 4, 128).transpose(1, 0, 2, 3)),
            wkT=np.ascontiguousarray(Wk.T[:, cols].reshape(8, 128, 4, 128).transpose(1, 0, 2, 3)),
            wvT=np.ascontiguousarray(Wv.T[:, cols].reshape(8, 128, 4, 128).transpose(1, 0, 2, 3)),
            bq=np.ascontiguousarray(bq[cols].reshape(4, 128).T),
            bk=np.ascontiguousarray(bk[cols].reshape(4, 128).T),
            bvr=np.ascontiguousarray(bv[cols].reshape(1, 512)),
            wrT=wrT, br=brf, w1t=w1t, w2t=w2t, b1=b1t, b2=np.ascontiguousarray(b2),
            gam=gamb, bet=betb, idf=idf, ones1f=ones1f, ones1b=ones1b,
            onehot=onehot, rep8=rep8,
        )
        maps.append(m)
    return maps


_DT_EXPECT = dict(x="float32", xo="float32", keysT="float32", valuesT="float32",
                  wqT="float32", wkT="float32", wvT="float32", bq="float32",
                  bk="float32", bvr="float32", wrT="float32", br="float32",
                  w1t="float32", w2t="float32", b1="float32", b2="float32",
                  gam="float32", bet="float32", idf="float32", ones1f="float32",
                  ones1b="bfloat16", onehot="bfloat16", rep8="bfloat16")


def kernel(**inputs):
    maps = _prep_maps(inputs)
    for m in maps:
        for k, v in m.items():
            assert str(v.dtype) == _DT_EXPECT[k], (k, v.dtype)
    nc = build_program()
    res = run_bass_kernel_spmd(nc, maps, core_ids=list(range(8)))
    out = np.empty((4, S, D), dtype=np.float32)
    for c in range(8):
        b, half = c // 2, c % 2
        out[b, half * 4096:(half + 1) * 4096] = np.asarray(res.results[c]["y"])
    if _DBG:
        kernel._last = res
    return out
